# version 08
# baseline (speedup 1.0000x reference)
import numpy as np
import ml_dtypes
import concourse.bass as bass
import concourse.mybir as mybir
from concourse.bass_utils import run_bass_kernel_spmd

F32 = mybir.dt.float32
BF16 = mybir.dt.bfloat16
U8 = mybir.dt.uint8
AF = mybir.ActivationFunctionType
ALU = mybir.AluOpType
AX = mybir.AxisListType.X
BFNP = ml_dtypes.bfloat16

D = 1024
NDB = 8
T = 2048
HALO = 256
TH = T + 2 * HALO
NTILES = TH // 128
H = 8
E = 8
F = 3264
FPAD = 3328
NFB = FPAD // 128
DFF = 4096
SCALE = float(1.0 / np.sqrt(128.0))
NEG = -1.0e30
EPS = 1.0e-5

P_ID32 = 0
P_ID16 = 512
P_ONES32 = 768
P_ONESC32 = 1280
P_BQ = 1312
P_BK = 1344
P_BO = 1376
P_B2 = 1408
P_GAMMA = 1440
P_BETA = 1472
P_B1 = 1504
P_GWT = 1632
P_GBB = 1888
P_UPB = 1920
P_GPB = 2752
P_DNB = 3584
P_COMB = 3840
P_SMALL = 4352
P_SEL = 4416
P_XNT16 = 8512
P_BUFR = 49472
S = 115008
A_XT0 = S + 0
A_XT1 = S + 4096
A_XN0 = S + 8192
A_XN1 = S + 12288
A_XN16_0 = S + 16384
A_XN16_1 = S + 18432
A_G8 = S + 20480
A_P8 = A_G8 + 32
A_PRB = A_G8 + 64
A_TOP8 = A_G8 + 96
A_MSK = A_G8 + 128
A_CW = A_G8 + 160
A_ST6 = A_G8 + 192
FI_OT0 = S + 21504
FI_OT1 = S + 25600
AT_V16 = S + 0
AT_BVBC = S + 24576
AT_MB = S + 28672
AT_ME0 = S + 31744
AT_ME7 = S + 34816
AT_OTG = S + 37888
AT_KT = S + 54272
AT_Q16 = S + 60416
AT_WSL0 = S + 64512
AT_WSL1 = S + 68608
AT_PN0 = S + 72704
AT_PN1 = S + 75776
AT_PEXPA = S + 78848
AT_PEXPB = S + 80384
AT_PTA = S + 81920
AT_PTB = S + 83456
L_SUM = S + 0
L_SQ = S + 8192
L_TA = S + 20480
L_TMP = S + 70656
FF_W1SL = S + 0
FF_W2SL = S + 16384
FF_BT16 = S + 24576
FF_H1 = S + 57344
M_COMBT = S + 0
M_CB = S + 4096
M_TU = S + 6144
M_TD = S + 10240
M_UPW = S + 14336
M_GPW = S + 22528
M_DWT = S + 30720
M_H16 = S + 44032
ARENA = S + 97280

DMAQ = {"xin": 4, "wdm": 4, "out": 3, "cst": 1, "cvl": 2, "cvs": 2}
SEMS = tuple(f"{q}{j}" for q, n in DMAQ.items() for j in range(n)) + ("pe", "dve", "act", "gps")


class Wk:
    def __init__(self, which, engs, sems):
        self.which = which
        self.engs = engs
        self.sems = sems
        self.c = {k: 0 for k in sems}
        self.hw = {}
        self.qidx = {q: 0 for q in DMAQ}

    def dma(self, q, instr):
        i = self.qidx[q]
        self.qidx[q] = i + 1
        ns = DMAQ[q]
        sname = f"{q}{i % ns}"
        v = 16 * (i // ns + 1)
        self.c[sname] = v
        if instr is not None:
            instr.then_inc(self.sems[sname], 16)
        return (sname, v)

    def wait_tok(self, engname, tok):
        self.wait(engname, tok[0], tok[1])

    def wait_all(self, engname, q):
        ns = DMAQ[q]
        tot = self.qidx[q]
        for j in range(ns):
            uses = (tot - j + ns - 1) // ns
            if uses > 0:
                self.wait(engname, f"{q}{j}", 16 * uses)

    def eng(self, name):
        return self.engs[name] if self.which == name else None

    SELFW = {"dve": "vector", "act": "scalar", "gps": "gpsimd"}

    def inc(self, sem, instr, k=1):
        self.c[sem] += k
        if instr is not None:
            instr.then_inc(self.sems[sem], k)
        eng = self.SELFW.get(sem)
        if eng is not None and self.which == eng:
            v = self.c[sem]
            if self.hw.get(sem, -1) < v:
                self.hw[sem] = v
                self.engs[eng].wait_ge(self.sems[sem], v)
        return self.c[sem]

    def wait(self, engname, sem, val=None):
        v = self.c[sem] if val is None else val
        if v <= 0:
            return
        if self.which == engname:
            key = sem
            if self.hw.get(key, -1) >= v:
                return
            self.hw[key] = v
            self.engs[engname].wait_ge(self.sems[sem], v)


class Slot:
    def __init__(self, ap=None):
        self.ap = ap
        self.ready = None
        self.freed = None

    def pre_write(self, w, eng):
        if self.freed is not None:
            w.wait(eng, *self.freed)

    def set_ready(self, w, sem, instr, k=1):
        self.ready = (sem, w.inc(sem, instr, k))

    def pre_read(self, w, eng):
        if self.ready is not None:
            w.wait(eng, *self.ready)

    def set_freed(self, w, sem, instr, k=1):
        self.freed = (sem, w.inc(sem, instr, k))


def build_program(debug=()):
    nc = bass.Bass("TRN2", target_bir_lowering=False)

    def din(name, shape, dt=F32):
        return nc.dram_tensor(name, list(shape), dt, kind="ExternalInput").ap()

    xc = din("xc", [TH, D])
    wq = din("wq", [D, D]); wk = din("wk", [D, D])
    wv = din("wv", [D, D]); wo = din("wo", [D, D])
    w1 = din("w1", [D, DFF]); w2 = din("w2", [DFF, D])
    upt = din("upt", [E, D, FPAD]); gpt = din("gpt", [E, D, FPAD])
    dwt = din("dwt", [E, FPAD, D])
    bq_i = din("bq", [128, 8]); bk_i = din("bk", [128, 8])
    bo_i = din("bo", [128, 8]); b2_i = din("b2", [128, 8])
    gam_i = din("gamma", [128, 8]); bet_i = din("beta", [128, 8])
    b1_i = din("b1", [128, 32])
    gwt_i = din("gwt", [128, 64]); gbb_i = din("gbb", [128, 8])
    upb_i = din("upb", [128, E * NFB]); gpb_i = din("gpb", [128, E * NFB])
    dnb_i = din("dnb", [128, E * 8])
    bvbc_i = din("bvbc", [128, D])
    sel_i = din("sel", [8, 1024])
    mb_i = din("maskb", [128, 1536], BF16)
    me0_i = din("maske0", [128, 1536], BF16)
    me7_i = din("maske7", [128, 1536], BF16)

    w1b = nc.dram_tensor("w1b", [D * DFF], BF16, kind="Internal").ap()
    w2b = nc.dram_tensor("w2b", [DFF * D], BF16, kind="Internal").ap()
    y = nc.dram_tensor("y", [T, D], F32, kind="ExternalOutput").ap()
    dbg = {}
    for name in debug:
        if name == "xnt":
            dbg[name] = nc.dram_tensor("d_xnt", [128, NDB * TH], BF16, kind="ExternalOutput").ap()
        elif name in ("xnTf", "aT", "bT", "ffn", "moe"):
            dbg[name] = nc.dram_tensor("d_" + name, [128, NDB * T], F32, kind="ExternalOutput").ap()
        elif name == "comb":
            dbg[name] = nc.dram_tensor("d_comb", [128, 128], F32, kind="ExternalOutput").ap()
        elif name == "gate8":
            dbg[name] = nc.dram_tensor("d_gate8", [128, 48], F32, kind="ExternalOutput").ap()
        elif name in ("bt16", "bt16b"):
            dbg[name] = nc.dram_tensor("d_" + name, [128, NDB * T], BF16, kind="ExternalOutput").ap()
        elif name == "h1f":
            dbg[name] = nc.dram_tensor("d_h1f", [128, 32 * 512], BF16, kind="ExternalOutput").ap()
        elif name == "h16":
            dbg[name] = nc.dram_tensor("d_h16", [128, NFB * 1024], BF16, kind="ExternalOutput").ap()

    arena = nc.alloc_sbuf_tensor("arena", [128, ARENA], U8)
    arena_addr = nc.lookup_mloc(arena).addr

    def place(name, off, shape, dt):
        h = nc.virtual_sbuf_tensor(name, list(shape), dt)
        nc._tensor(h.name, list(shape), dt, type="SB")
        mloc = nc.lookup_mloc(h)
        mloc.allocated = True
        mloc.addr = arena_addr + off
        return h.ap()

    id32 = place("id32", P_ID32, [128, 128], F32)
    id16 = place("id16", P_ID16, [128, 128], BF16)
    ones32 = place("ones32", P_ONES32, [1, 128], F32)
    onesc32 = place("onesc32", P_ONESC32, [128, 1], F32)
    bq = place("bq_s", P_BQ, [128, 8], F32)
    bk = place("bk_s", P_BK, [128, 8], F32)
    bo = place("bo_s", P_BO, [128, 8], F32)
    b2 = place("b2_s", P_B2, [128, 8], F32)
    gamma = place("gamma_s", P_GAMMA, [128, 8], F32)
    beta = place("beta_s", P_BETA, [128, 8], F32)
    b1 = place("b1_s", P_B1, [128, 32], F32)
    gwt = place("gwt_s", P_GWT, [128, 64], F32)
    gbb = place("gbb_s", P_GBB, [128, 8], F32)
    upb = place("upb_s", P_UPB, [128, E * NFB], F32)
    gpb = place("gpb_s", P_GPB, [128, E * NFB], F32)
    dnb = place("dnb_s", P_DNB, [128, E * 8], F32)
    comb = place("comb_s", P_COMB, [128, 128], F32)
    small = place("small_s", P_SMALL, [128, 16], F32)
    sel = place("sel_s", P_SEL, [8, 1024], F32)
    xnt16 = place("xnt16", P_XNT16, [128, NDB * TH], BF16)
    bufr = place("bufr", P_BUFR, [128, NDB * T], F32)
    xt = [place("xt0", A_XT0, [128, D], F32), place("xt1", A_XT1, [128, D], F32)]
    xn32 = [place("xn0", A_XN0, [128, D], F32), place("xn1", A_XN1, [128, D], F32)]
    xn16 = [place("xn16_0", A_XN16_0, [128, D], BF16), place("xn16_1", A_XN16_1, [128, D], BF16)]
    g8 = place("g8_s", A_G8, [128, 8], F32)
    g8all = place("g8all", FI_OT0, [128, 128], F32)
    gsm = place("gsm", FI_OT0 + 512, [128, 64], F32)
    p8 = place("p8_s", A_P8, [128, 8], F32)
    prb = place("prb_s", A_PRB, [128, 8], F32)
    top8 = place("top8_s", A_TOP8, [128, 8], F32)
    mskg = place("mskg_s", A_MSK, [128, 8], F32)
    cw8 = place("cw8_s", A_CW, [128, 8], F32)
    st6 = place("st6_s", A_ST6, [128, 12], F32)
    fot = [place("fot0", FI_OT0, [128, D], F32), place("fot1", FI_OT1, [128, D], F32)]
    v16 = place("v16", AT_V16, [128, 12 * 1024], BF16)
    bvbc = place("bvbc_s", AT_BVBC, [128, D], F32)
    maskb = place("maskb_s", AT_MB, [128, 1536], BF16)
    maske0 = place("maske0_s", AT_ME0, [128, 1536], BF16)
    maske7 = place("maske7_s", AT_ME7, [128, 1536], BF16)
    otg = place("otg", AT_OTG, [128, H * 1024], BF16)
    kt16 = place("kt16", AT_KT, [128, 2 * 1536], BF16)
    q16 = place("q16", AT_Q16, [128, 2 * 1024], BF16)
    wsl = [place("wsl0", AT_WSL0, [128, 2048], BF16), place("wsl1", AT_WSL1, [128, 2048], BF16)]
    pn = [place("pn0", AT_PN0, [128, 768], F32), place("pn1", AT_PN1, [128, 768], F32)]
    pexp = [place("pexpA", AT_PEXPA, [128, 768], BF16), place("pexpB", AT_PEXPB, [128, 768], BF16)]
    pt16 = [place("ptA", AT_PTA, [128, 768], BF16), place("ptB", AT_PTB, [128, 768], BF16)]
    lsum = place("lsum", L_SUM, [1, 2048], F32)
    lsq = place("lsq", L_SQ, [1, 2048], F32)
    ltmp = place("ltmp", L_TMP, [1, 2048], F32)
    lta = [place("lta0", L_TA, [128, 512], F32), place("lta1", L_TA + 2048, [128, 512], F32)]
    w1sl = [place("w1sl0", FF_W1SL, [128, 4096], BF16), place("w1sl1", FF_W1SL + 8192, [128, 4096], BF16)]
    w2sl = [place("w2sl0", FF_W2SL, [128, 2048], BF16), place("w2sl1", FF_W2SL + 4096, [128, 2048], BF16)]
    bt16 = place("bt16", FF_BT16, [128, NDB * T], BF16)
    h1 = place("h1", FF_H1, [128, 32 * 512], BF16)
    combt = place("combt", M_COMBT, [8, 1024], F32)
    cbuf = place("cbuf", M_CB, [128, 1024], BF16)
    tu = [place("tu0", M_TU, [128, 512], F32), place("tu1", M_TU + 2048, [128, 512], F32)]
    td = [place("td0", M_TD, [128, 512], F32), place("td1", M_TD + 2048, [128, 512], F32)]
    upw = [place("upw0", M_UPW, [128, 2048], BF16), place("upw1", M_UPW + 4096, [128, 2048], BF16)]
    cvb = [place("cvb0", S + 84992, [128, 2048], BF16), place("cvb1", S + 89088, [128, 2048], BF16)]
    gpw = [place("gpw0", M_GPW, [128, 2048], BF16), place("gpw1", M_GPW + 4096, [128, 2048], BF16)]
    dwts = [place("dwt0", M_DWT, [128, 3328], BF16), place("dwt1", M_DWT + 6656, [128, 3328], BF16)]
    h16 = place("h16", M_H16, [128, NFB * 1024], BF16)

    pf = [nc.alloc_psum_tensor(f"pf{i}", [128, 512], F32).ap() for i in range(7)]
    pb = nc.alloc_psum_tensor("pb", [128, 1024], BF16).ap()

    from contextlib import ExitStack
    with ExitStack() as ctx:
        semh = {s: ctx.enter_context(nc.semaphore(s)) for s in SEMS}
        block = ctx.enter_context(nc.Block())

        def flow(w: Wk):
            te = w.eng("tensor"); ve = w.eng("vector"); se = w.eng("scalar")
            gp = w.eng("gpsimd"); sy = w.eng("sync")
            PF = [Slot(p) for p in pf]
            PB = Slot(pb)
            xcol = lambda db, t0: db * TH + t0
            rcol = lambda db, t0: db * T + t0

            for dst, src in ((bq, bq_i), (bk, bk_i), (bo, bo_i), (b2, b2_i),
                             (gamma, gam_i), (beta, bet_i), (b1, b1_i),
                             (gwt, gwt_i), (gbb, gbb_i), (upb, upb_i),
                             (gpb, gpb_i), (dnb, dnb_i), (bvbc, bvbc_i),
                             (sel, sel_i), (maskb, mb_i), (maske0, me0_i),
                             (maske7, me7_i)):
                w.dma("cst", sy and sy.dma_start(out=dst, in_=src))
            w.inc("gps", gp and gp.iota(xn32[0][:, 0:128], pattern=[[1, 128]], base=0,
                                        channel_multiplier=-1,
                                        allow_small_or_imprecise_dtypes=True), 1)
            w.wait("vector", "gps")
            w.inc("dve", ve and ve.memset(small[:, 15:16], EPS))
            w.inc("dve", ve and ve.tensor_scalar(out=id32, in0=xn32[0][:, 0:128],
                                                 scalar1=0.0, scalar2=None,
                                                 op0=ALU.is_equal))
            w.inc("dve", ve and ve.tensor_copy(out=id16, in_=id32))
            w.inc("dve", ve and ve.memset(ones32, 1.0))
            w.inc("dve", ve and ve.memset(onesc32, 1.0))
            w.inc("dve", ve and ve.tensor_copy(out=small[:, 14:15], in_=small[:, 15:16]))
            ident_dve = w.c["dve"]

            xslot = [Slot(), Slot()]
            xnslot = [Slot(), Slot()]
            xn16slot = [Slot(), Slot()]
            w.wait("scalar", "dve", ident_dve)
            cv_jobs = []
            for j in range(16):
                cv_jobs.append((w1.rearrange("a b -> (a b)"), w1b, j))
            for j in range(16):
                cv_jobs.append((w2.rearrange("a b -> (a b)"), w2b, j))
            cv_state = {"i": 0, "tok": [None, None], "sttok": [None, None]}

            def emit_convert(n):
                CHUNK = 128 * 2048
                while cv_jobs and n > 0:
                    srcf, dstb, j = cv_jobs.pop(0)
                    i = cv_state["i"]; cv_state["i"] += 1
                    sl = i % 2
                    if cv_state["sttok"][sl] is not None:
                        w.wait_tok("gpsimd", cv_state["sttok"][sl])
                    cv_state["tok"][sl] = w.dma("cvl", gp and gp.dma_start(
                        out=cvb[sl],
                        in_=srcf[j * CHUNK:(j + 1) * CHUNK].rearrange("(p n) -> p n", p=128)))
                    w.wait_tok("sync", cv_state["tok"][sl])
                    cv_state["sttok"][sl] = w.dma("cvs", sy and sy.dma_start(
                        out=dstb[j * CHUNK:(j + 1) * CHUNK].rearrange("(p n) -> p n", p=128),
                        in_=cvb[sl]))
                    n -= 1
            w.wait_all("vector", "cst")
            w.wait_all("scalar", "cst")
            w.wait_all("tensor", "cst")
            w.wait_all("sync", "cst")
            for i in range(NTILES):
                s = i % 2
                xslot[s].pre_write(w, "sync")
                tok_x = w.dma("xin", sy and sy.dma_start(out=xt[s], in_=xc[i * 128:(i + 1) * 128, :]))
                emit_convert(1)
                w.wait_tok("vector", tok_x)
                xnslot[s].pre_write(w, "vector")
                w.inc("dve", ve and ve.reduce_sum(out=small[:, 0:1], in_=xt[s], axis=AX))
                w.inc("dve", ve and ve.tensor_mul(xn32[s], xt[s], xt[s]))
                w.inc("dve", ve and ve.reduce_sum(out=small[:, 1:2], in_=xn32[s], axis=AX))
                w.inc("dve", ve and ve.tensor_scalar_mul(small[:, 2:3], small[:, 0:1], 1.0 / D))
                w.inc("dve", ve and ve.tensor_scalar_mul(small[:, 3:4], small[:, 1:2], 1.0 / D))
                w.inc("dve", ve and ve.tensor_mul(small[:, 4:5], small[:, 2:3], small[:, 2:3]))
                d_var = w.inc("dve", ve and ve.tensor_sub(small[:, 3:4], small[:, 3:4], small[:, 4:5]))
                w.wait("scalar", "dve", d_var)
                w.inc("act", se and se.activation(out=small[:, 5:6], in_=small[:, 3:4], func=AF.Ln, bias=small[:, 15:16]))
                a_rstd = w.inc("act", se and se.activation(out=small[:, 6:7], in_=small[:, 5:6],
                                                           func=AF.Exp, scale=-0.5))
                w.wait("vector", "act", a_rstd)
                d_xn = w.inc("dve", ve and ve.tensor_scalar(out=xn32[s], in0=xt[s],
                                                            scalar1=small[:, 2:3],
                                                            scalar2=small[:, 6:7],
                                                            op0=ALU.subtract, op1=ALU.mult))
                xslot[s].freed = ("dve", d_xn)
                w.wait("scalar", "dve", d_xn)
                xn16slot[s].pre_write(w, "scalar")
                a_xn16 = w.inc("act", se and se.copy(out=xn16[s], in_=xn32[s]))
                is_center = 2 <= i < 18
                tb = i - 2
                w.wait("tensor", "act", a_xn16)
                w.wait("tensor", "dve", d_xn)
                if is_center:
                    for half in range(2):
                        PF[half].pre_write(w, "tensor")
                        for k in range(4):
                            db = half * 4 + k
                            ins = te and te.transpose(pf[half][:, k * 128:(k + 1) * 128],
                                                      xn32[s][:, db * 128:(db + 1) * 128], id32)
                            if k == 3:
                                PF[half].set_ready(w, "pe", ins)
                    xnslot[s].freed = ("pe", w.c["pe"])
                    for half in range(2):
                        PF[half].pre_read(w, "scalar")
                        dst = bufr.rearrange("p (b t) -> p b t", b=NDB)[:, half * 4:(half + 1) * 4,
                                                                       tb * 128:(tb + 1) * 128]
                        src = pf[half].rearrange("p (k t) -> p k t", k=4)
                        ins = se and se.copy(out=dst, in_=src)
                        PF[half].set_freed(w, "act", ins)
                PB.pre_write(w, "tensor")
                for db in range(NDB):
                    ins = te and te.transpose(pb[:, db * 128:(db + 1) * 128],
                                              xn16[s][:, db * 128:(db + 1) * 128], id16)
                    if db == 7:
                        PB.set_ready(w, "pe", ins)
                xn16slot[s].freed = ("pe", w.c["pe"])
                if not is_center:
                    xnslot[s].freed = ("pe", w.c["pe"])
                PB.pre_read(w, "scalar")
                dstx = xnt16.rearrange("p (b t) -> p b t", b=NDB)[:, :, i * 128:(i + 1) * 128]
                ins = se and se.copy(out=dstx, in_=pb.rearrange("p (k t) -> p k t", k=NDB))
                PB.set_freed(w, "act", ins)
                if is_center:
                    PF[2].pre_write(w, "tensor")
                    w.wait("tensor", "dve")
                    w.wait("tensor", "act")
                    for db in range(NDB):
                        c0 = rcol(db, tb * 128)
                        ins = te and te.matmul(pf[2][:, 0:8], bufr[:, c0:c0 + 128],
                                               gwt[:, db * 8:(db + 1) * 8],
                                               start=(db == 0), stop=(db == 7))
                        if db == 7:
                            PF[2].set_ready(w, "pe", ins)
                    PF[2].pre_read(w, "vector")
                    d1 = w.inc("dve", ve and ve.tensor_add(g8all[:, tb * 8:(tb + 1) * 8],
                                                           pf[2][:, 0:8], gbb))
                    PF[2].freed = ("dve", d1)

            g3 = g8all.rearrange("p (a b) -> p a b", b=8)
            w.inc("dve", ve and ve.reduce_max(out=gsm[:, 0:16].rearrange("p (a b) -> p a b", b=1),
                                              in_=g3, axis=AX, negate=True))
            w.inc("dve", ve and ve.tensor_tensor(out=g3, in0=g3,
                                                 in1=gsm[:, 0:16].rearrange("p (a b) -> p a b", b=1)
                                                 .to_broadcast([128, 16, 8]),
                                                 op=ALU.add))
            d_g = w.c["dve"]
            w.wait("scalar", "dve", d_g)
            a_g = w.inc("act", se and se.activation(out=g8all, in_=g8all, func=AF.Exp))
            w.wait("vector", "act", a_g)
            w.inc("dve", ve and ve.reduce_sum(out=gsm[:, 16:32].rearrange("p (a b) -> p a b", b=1),
                                              in_=g3, axis=AX))
            w.inc("dve", ve and ve.reciprocal(out=gsm[:, 16:32], in_=gsm[:, 16:32]))
            w.inc("dve", ve and ve.tensor_mul(g3, g3,
                                              gsm[:, 16:32].rearrange("p (a b) -> p a b", b=1)
                                              .to_broadcast([128, 16, 8])))
            w.inc("dve", ve and ve.reduce_max(out=gsm[:, 32:48].rearrange("p (a b) -> p a b", b=1),
                                              in_=g3, axis=AX))
            cm3 = comb.rearrange("p (a b) -> p a b", b=8)
            m1b = gsm[:, 32:48].rearrange("p (a b) -> p a b", b=1).to_broadcast([128, 16, 8])
            w.inc("dve", ve and ve.tensor_tensor(out=cm3, in0=g3, in1=m1b, op=ALU.is_equal))
            w.inc("dve", ve and ve.scalar_tensor_tensor(out=comb, in0=comb, scalar=-2.0,
                                                        in1=g8all, op0=ALU.mult, op1=ALU.add))
            w.inc("dve", ve and ve.reduce_max(out=gsm[:, 48:64].rearrange("p (a b) -> p a b", b=1),
                                              in_=comb.rearrange("p (a b) -> p a b", b=8), axis=AX))
            m2b = gsm[:, 48:64].rearrange("p (a b) -> p a b", b=1).to_broadcast([128, 16, 8])
            w.inc("dve", ve and ve.tensor_add(gsm[:, 32:48], gsm[:, 32:48], gsm[:, 48:64]))
            w.inc("dve", ve and ve.tensor_scalar_add(gsm[:, 32:48], gsm[:, 32:48], 1.0e-20))
            w.inc("dve", ve and ve.reciprocal(out=gsm[:, 32:48], in_=gsm[:, 32:48]))
            rdb = gsm[:, 32:48].rearrange("p (a b) -> p a b", b=1).to_broadcast([128, 16, 8])
            w.inc("dve", ve and ve.tensor_tensor(out=cm3, in0=g3, in1=m2b, op=ALU.is_ge))
            w.inc("dve", ve and ve.tensor_mul(comb, comb, g8all))
            w.inc("dve", ve and ve.tensor_tensor(out=cm3, in0=cm3, in1=rdb, op=ALU.mult))

            if "xnt" in dbg:
                w.wait("sync", "dve")
                w.wait("sync", "act")
                w.dma("out", sy and sy.dma_start(out=dbg["xnt"], in_=xnt16))
            if "xnTf" in dbg:
                w.wait("sync", "dve")
                w.wait("sync", "act")
                w.dma("out", sy and sy.dma_start(out=dbg["xnTf"], in_=bufr))
                w.wait_all("vector", "out")
            if "comb" in dbg:
                w.wait("sync", "dve")
                w.dma("out", sy and sy.dma_start(out=dbg["comb"], in_=comb))
            if "gate8" in dbg:
                w.wait("sync", "dve")
                w.wait("sync", "act")
                w.dma("out", sy and sy.dma_start(out=dbg["gate8"][:, 0:8], in_=g8))
                w.dma("out", sy and sy.dma_start(out=dbg["gate8"][:, 8:16], in_=prb))
                w.dma("out", sy and sy.dma_start(out=dbg["gate8"][:, 16:24], in_=top8))
                w.dma("out", sy and sy.dma_start(out=dbg["gate8"][:, 24:32], in_=mskg))
                w.dma("out", sy and sy.dma_start(out=dbg["gate8"][:, 32:40], in_=cw8))
                w.dma("out", sy and sy.dma_start(out=dbg["gate8"][:, 40:48], in_=small[:, 8:16]))

            w.wait("tensor", "dve")
            w.wait("tensor", "act")
            wslot = [Slot(), Slot()]
            att_it = [0]
            att_prev = [None]
            pexp_free = [0, 0]

            def emit_tr_o(pv):
                osl = pv["osl"]
                h_, cl_ = pv["h"], pv["cl"]
                for qh in range(2):
                    PB.pre_write(w, "tensor")
                    w.wait("tensor", "dve", pv["d_px"][qh])
                    for kc6 in range(6):
                        ins = te and te.transpose(pb[:, kc6 * 128:(kc6 + 1) * 128],
                                                  pexp[qh][:, kc6 * 128:(kc6 + 1) * 128], id16)
                        if kc6 == 5:
                            PB.set_ready(w, "pe", ins)
                    PB.pre_read(w, "scalar")
                    ins = se and se.copy(out=pt16[qh], in_=pb[:, 0:768])
                    PB.set_freed(w, "act", ins)
                    d_pt = w.c["act"]
                    w.wait("tensor", "act", d_pt)
                    for kc6 in range(6):
                        kpb = cl_ * 2 + kc6
                        v0 = kpb * 1024 + h_ * 128
                        ins = te and te.matmul(osl.ap[:, qh * 128:qh * 128 + 128],
                                               v16[:, v0:v0 + 128],
                                               pt16[qh][:, kc6 * 128:(kc6 + 1) * 128],
                                               start=(kc6 == 0), stop=(kc6 == 5))
                        if kc6 == 5:
                            if qh == 1:
                                osl.set_ready(w, "pe", ins)
                            else:
                                w.inc("pe", ins)
                    pexp_free[qh] = w.c["pe"]
                osl.pre_read(w, "scalar")
                o0 = pv["h"] * 1024 + pv["cl"] * 256
                ins = se and se.copy(out=otg[:, o0:o0 + 256], in_=osl.ap[:, 0:256])
                osl.set_freed(w, "act", ins)
            for g in range(2):
                gbase = g * 1024
                cbase = g * 1024
                for qtr in range(4):
                    sq = qtr % 2
                    wslot[sq].pre_write(w, "gpsimd")
                    tok_v = w.dma("wdm", gp and gp.dma_start(
                        out=wsl[sq].rearrange("p (b n) -> p b n", b=NDB),
                        in_=wv.rearrange("(b p) n -> p b n", p=128)[:, :, qtr * 256:(qtr + 1) * 256]))
                    emit_convert(1)
                    w.wait_tok("tensor", tok_v)
                    for pbi in range(12):
                        sl = PF[pbi % 2]
                        sl.pre_write(w, "tensor")
                        for db in range(NDB):
                            c0 = xcol(db, gbase + pbi * 128)
                            ins = te and te.matmul(sl.ap[:, 0:256], xnt16[:, c0:c0 + 128],
                                                   wsl[sq][:, db * 256:(db + 1) * 256],
                                                   start=(db == 0), stop=(db == 7))
                            if db == 7:
                                sl.set_ready(w, "pe", ins)
                        sl.pre_read(w, "vector")
                        o0 = pbi * 1024 + qtr * 256
                        ins = ve and ve.tensor_tensor(out=v16[:, o0:o0 + 256], in0=sl.ap[:, 0:256],
                                                      in1=bvbc[:, qtr * 256:(qtr + 1) * 256],
                                                      op=ALU.add)
                        sl.set_freed(w, "dve", ins)
                    wslot[sq].freed = ("pe", w.c["pe"])
                for hp in range(4):
                    wslot[0].pre_write(w, "gpsimd")
                    tok_q = w.dma("wdm", gp and gp.dma_start(
                        out=wsl[0].rearrange("p (b n) -> p b n", b=NDB),
                        in_=wq.rearrange("(b p) n -> p b n", p=128)[:, :, hp * 256:(hp + 1) * 256]))
                    wslot[1].pre_write(w, "gpsimd")
                    tok_k = w.dma("wdm", gp and gp.dma_start(
                        out=wsl[1].rearrange("p (b n) -> p b n", b=NDB),
                        in_=wk.rearrange("(b p) n -> p b n", p=128)[:, :, hp * 256:(hp + 1) * 256]))
                    emit_convert(1)
                    w.wait_tok("tensor", tok_q)
                    for h2 in range(2):
                        for nt in range(2):
                            sl = PF[(h2 * 2 + nt) % 2]
                            sl.pre_write(w, "tensor")
                            for db in range(NDB):
                                c0 = xcol(db, HALO + cbase + nt * 512)
                                ins = te and te.matmul(
                                    sl.ap,
                                    wsl[0][:, db * 256 + h2 * 128:db * 256 + h2 * 128 + 128],
                                    xnt16[:, c0:c0 + 512],
                                    start=(db == 0), stop=(db == 7))
                                if db == 7:
                                    sl.set_ready(w, "pe", ins)
                            sl.pre_read(w, "scalar")
                            o0 = h2 * 1024 + nt * 512
                            ins = se and se.activation(out=q16[:, o0:o0 + 512], in_=sl.ap,
                                                       func=AF.Identity,
                                                       bias=bq[:, hp * 2 + h2:hp * 2 + h2 + 1])
                            sl.set_freed(w, "act", ins)
                    w.wait_tok("tensor", tok_k)
                    for h2 in range(2):
                        for nt in range(3):
                            sl = PF[(h2 * 3 + nt) % 2]
                            sl.pre_write(w, "tensor")
                            for db in range(NDB):
                                c0 = xcol(db, gbase + nt * 512)
                                ins = te and te.matmul(
                                    sl.ap,
                                    wsl[1][:, db * 256 + h2 * 128:db * 256 + h2 * 128 + 128],
                                    xnt16[:, c0:c0 + 512],
                                    start=(db == 0), stop=(db == 7))
                                if db == 7:
                                    sl.set_ready(w, "pe", ins)
                            sl.pre_read(w, "scalar")
                            o0 = h2 * 1536 + nt * 512
                            ins = se and se.activation(out=kt16[:, o0:o0 + 512], in_=sl.ap,
                                                       func=AF.Identity,
                                                       bias=bk[:, hp * 2 + h2:hp * 2 + h2 + 1])
                            sl.set_freed(w, "act", ins)
                    wslot[0].freed = ("pe", w.c["pe"])
                    wslot[1].freed = ("pe", w.c["pe"])
                    qk_act = w.c["act"]
                    for cl in range(4):
                        ch = g * 4 + cl
                        msk = maske0 if ch == 0 else (maske7 if ch == 7 else maskb)
                        for h2 in range(2):
                            h = hp * 2 + h2
                            it = att_it[0]
                            att_it[0] += 1
                            osl = PF[6] if it % 2 == 0 else PF[0]
                            cur = dict(h=h, cl=cl, osl=osl, d_px=[0, 0])
                            osl.pre_write(w, "tensor")
                            scp = [(PF[2], PF[4]), (PF[3], PF[5])]
                            for qh in range(2):
                                scA, scB = scp[qh]
                                scA.pre_write(w, "tensor")
                                scB.pre_write(w, "tensor")
                                w.wait("tensor", "act", qk_act)
                                q0 = h2 * 1024 + cl * 256 + qh * 128
                                qap = q16[:, q0:q0 + 128]
                                for kc in range(3):
                                    dst = scA.ap[:, kc * 256:kc * 256 + 256] if kc < 2 \
                                        else scB.ap[:, 0:256]
                                    k0 = h2 * 1536 + cl * 256 + kc * 256
                                    ins = te and te.matmul(dst, qap, kt16[:, k0:k0 + 256],
                                                           start=True, stop=True)
                                    if kc == 1:
                                        scA.set_ready(w, "pe", ins)
                                    elif kc == 2:
                                        scB.set_ready(w, "pe", ins)
                            if att_prev[0] is not None:
                                emit_tr_o(att_prev[0])
                            for qh in range(2):
                                scA, scB = scp[qh]
                                c_nm = 8 + 3 * qh
                                scA.pre_read(w, "vector")
                                scB.pre_read(w, "vector")
                                ia = ve and ve.scalar_tensor_tensor(
                                    out=pn[qh][:, 0:512], in0=scA.ap, scalar=SCALE,
                                    in1=msk[:, qh * 768:qh * 768 + 512],
                                    op0=ALU.mult, op1=ALU.add)
                                scA.set_freed(w, "dve", ia)
                                ib = ve and ve.scalar_tensor_tensor(
                                    out=pn[qh][:, 512:768], in0=scB.ap[:, 0:256], scalar=SCALE,
                                    in1=msk[:, qh * 768 + 512:qh * 768 + 768],
                                    op0=ALU.mult, op1=ALU.add)
                                scB.set_freed(w, "dve", ib)
                                d_nm = w.inc("dve", ve and ve.reduce_max(
                                    out=small[:, c_nm:c_nm + 1], in_=pn[qh], axis=AX, negate=True))
                                w.wait("scalar", "dve", d_nm)
                                w.inc("act", se and se.activation(
                                    out=pn[qh], in_=pn[qh], func=AF.Exp,
                                    bias=small[:, c_nm:c_nm + 1],
                                    accum_out=small[:, c_nm + 1:c_nm + 2]))
                            for qh in range(2):
                                c_nm = 8 + 3 * qh
                                w.wait("vector", "act")
                                w.inc("dve", ve and ve.reciprocal(
                                    out=small[:, c_nm + 2:c_nm + 3], in_=small[:, c_nm + 1:c_nm + 2]))
                                w.wait("vector", "pe", pexp_free[qh])
                                cur["d_px"][qh] = w.inc("dve", ve and ve.tensor_scalar_mul(
                                    pexp[qh], pn[qh], small[:, c_nm + 2:c_nm + 3]))
                            att_prev[0] = cur
                if att_prev[0] is not None:
                    emit_tr_o(att_prev[0])
                    att_prev[0] = None
                w.wait("tensor", "act")
                for dob in range(NDB):
                    sw = dob % 2
                    wslot[sw].pre_write(w, "gpsimd")
                    tok_o = w.dma("wdm", gp and gp.dma_start(
                        out=wsl[sw][:, 0:1024].rearrange("p (b n) -> p b n", b=NDB),
                        in_=wo.rearrange("(b p) n -> p b n", p=128)[:, :, dob * 128:(dob + 1) * 128]))
                    w.wait_tok("tensor", tok_o)
                    for tb2 in range(2):
                        sl = PF[2 + tb2]
                        sl.pre_write(w, "tensor")
                        for dib in range(NDB):
                            ins = te and te.matmul(
                                sl.ap,
                                wsl[sw][:, dib * 128:(dib + 1) * 128],
                                otg[:, dib * 1024 + tb2 * 512:dib * 1024 + tb2 * 512 + 512],
                                start=(dib == 0), stop=(dib == 7))
                            if dib == 7:
                                sl.set_ready(w, "pe", ins)
                        sl.pre_read(w, "vector")
                        dst = bufr[:, rcol(dob, cbase + tb2 * 512):rcol(dob, cbase + tb2 * 512) + 512]
                        ins = ve and ve.scalar_tensor_tensor(out=dst, in0=sl.ap,
                                                             scalar=bo[:, dob:dob + 1],
                                                             in1=dst, op0=ALU.add, op1=ALU.add)
                        sl.set_freed(w, "dve", ins)
                    wslot[sw].freed = ("pe", w.c["pe"])

            if "aT" in dbg:
                w.wait("sync", "dve")
                w.dma("out", sy and sy.dma_start(out=dbg["aT"], in_=bufr))
                w.wait_all("vector", "out")

            aT_dve = w.c["dve"]
            w.wait("scalar", "dve", aT_dve)
            w.wait("tensor", "dve", aT_dve)
            ltaslot = [Slot(), Slot()]
            for tb4 in range(4):
                PF[2].pre_write(w, "tensor")
                for db in range(NDB):
                    src = bufr[:, rcol(db, tb4 * 512):rcol(db, tb4 * 512) + 512]
                    i1 = te and te.matmul(pf[2][0:1, :], onesc32, src,
                                          start=(db == 0), stop=(db == 7))
                    p1 = w.inc("pe", i1)
                    if db == 7:
                        PF[2].ready = ("pe", p1)
                PF[3].pre_write(w, "tensor")
                for db in range(NDB):
                    sidx = db % 2
                    src = bufr[:, rcol(db, tb4 * 512):rcol(db, tb4 * 512) + 512]
                    ltaslot[sidx].pre_write(w, "scalar")
                    a_s = w.inc("act", se and se.activation(out=lta[sidx], in_=src, func=AF.Square))
                    w.wait("tensor", "act", a_s)
                    i2 = te and te.matmul(pf[3][0:1, :], onesc32, lta[sidx],
                                          start=(db == 0), stop=(db == 7))
                    p2 = w.inc("pe", i2)
                    ltaslot[sidx].freed = ("pe", p2)
                    if db == 7:
                        PF[3].ready = ("pe", p2)
                PF[2].pre_read(w, "vector")
                ins = ve and ve.tensor_copy(out=lsum[0:1, tb4 * 512:(tb4 + 1) * 512], in_=pf[2][0:1, :])
                PF[2].set_freed(w, "dve", ins)
                PF[3].pre_read(w, "vector")
                ins = ve and ve.tensor_copy(out=lsq[0:1, tb4 * 512:(tb4 + 1) * 512], in_=pf[3][0:1, :])
                PF[3].set_freed(w, "dve", ins)
            w.inc("dve", ve and ve.tensor_scalar_mul(lsum, lsum, 1.0 / D))
            w.inc("dve", ve and ve.tensor_mul(ltmp, lsum, lsum))
            w.inc("dve", ve and ve.tensor_scalar_mul(lsq, lsq, 1.0 / D))
            d_v = w.inc("dve", ve and ve.tensor_sub(lsq, lsq, ltmp))
            w.wait("scalar", "dve", d_v)
            w.inc("act", se and se.activation(out=ltmp, in_=lsq, func=AF.Ln, bias=small[0:1, 15:16]))
            a_r = w.inc("act", se and se.activation(out=lsq, in_=ltmp, func=AF.Exp, scale=-0.5))
            for tb4 in range(4):
                PF[0].pre_write(w, "tensor")
                w.wait("tensor", "act", a_r)
                i0 = te and te.matmul(pf[0], ones32, lsum[0:1, tb4 * 512:(tb4 + 1) * 512],
                                      start=True, stop=True)
                PF[0].set_ready(w, "pe", i0)
                PF[1].pre_write(w, "tensor")
                i1 = te and te.matmul(pf[1], ones32, lsq[0:1, tb4 * 512:(tb4 + 1) * 512],
                                      start=True, stop=True)
                PF[1].set_ready(w, "pe", i1)
                PF[0].pre_read(w, "vector")
                PF[1].pre_read(w, "vector")
                for db in range(NDB):
                    tA = lta[db % 2]
                    ltaslot[db % 2].pre_write(w, "vector")
                    src = bufr[:, rcol(db, tb4 * 512):rcol(db, tb4 * 512) + 512]
                    w.inc("dve", ve and ve.tensor_sub(tA, src, pf[0]))
                    w.inc("dve", ve and ve.tensor_mul(tA, tA, pf[1]))
                    d_b = w.inc("dve", ve and ve.tensor_scalar(out=src, in0=tA,
                                                               scalar1=gamma[:, db:db + 1],
                                                               scalar2=beta[:, db:db + 1],
                                                               op0=ALU.mult, op1=ALU.add))
                    w.wait("scalar", "dve", d_b)
                    a_b = w.inc("act", se and se.copy(
                        out=bt16[:, rcol(db, tb4 * 512):rcol(db, tb4 * 512) + 512], in_=src))
                    ltaslot[db % 2].freed = ("dve", d_b)
                PF[0].freed = ("dve", w.c["dve"])
                PF[1].freed = ("dve", w.c["dve"])
            bT_act = w.c["act"]
            if "bT" in dbg:
                w.wait("sync", "dve")
                w.dma("out", sy and sy.dma_start(out=dbg["bT"], in_=bufr))
                w.wait_all("vector", "out")
            if "bt16" in dbg:
                w.wait("sync", "act", bT_act)
                w.dma("out", sy and sy.dma_start(out=dbg["bt16"], in_=bt16))

            w.wait("sync", "pe")
            w.wait("sync", "dve")
            emit_convert(100)
            if cv_state["sttok"][0] is not None:
                w.wait_tok("sync", cv_state["sttok"][0])
            if cv_state["sttok"][1] is not None:
                w.wait_tok("sync", cv_state["sttok"][1])
            w.wait("gpsimd", "pe")
            w.wait("gpsimd", "dve")
            w1slot = [Slot(), Slot()]
            w2slot = [Slot(), Slot()]
            h1slot = Slot()
            for tt in range(4):
                h1slot.pre_write(w, "scalar")
                for do4 in range(8):
                    s = do4 % 2
                    w1slot[s].pre_write(w, "sync")
                    tok_w1 = w.dma("xin", sy and sy.dma_start(
                        out=w1sl[s].rearrange("p (b n) -> p b n", b=NDB),
                        in_=w1b.rearrange("(a b) -> a b", b=DFF)
                        .rearrange("(b p) n -> p b n", p=128)[:, :, do4 * 512:(do4 + 1) * 512]))
                    w.wait_tok("tensor", tok_w1)
                    w.wait("tensor", "act", bT_act)
                    for k in range(4):
                        do = do4 * 4 + k
                        sl = PF[do % 4]
                        sl.pre_write(w, "tensor")
                        for db in range(NDB):
                            ins = te and te.matmul(
                                sl.ap, w1sl[s][:, db * 512 + k * 128:db * 512 + k * 128 + 128],
                                bt16[:, rcol(db, tt * 512):rcol(db, tt * 512) + 512],
                                start=(db == 0), stop=(db == 7))
                            if db == 7:
                                sl.set_ready(w, "pe", ins)
                        sl.pre_read(w, "scalar")
                        ins = se and se.activation(out=h1[:, do * 512:(do + 1) * 512], in_=sl.ap,
                                                   func=AF.Gelu, bias=b1[:, do:do + 1])
                        sl.set_freed(w, "act", ins)
                    w1slot[s].freed = ("pe", w.c["pe"])
                h1_act = w.c["act"]
                for do2 in range(NDB):
                    sl = PF[4 + do2 % 2]
                    sl.pre_write(w, "tensor")
                    for half in range(2):
                        w2slot[half].pre_write(w, "gpsimd")
                        tok_w2 = w.dma("wdm", gp and gp.dma_start(
                            out=w2sl[half].rearrange("p (b n) -> p b n", b=16),
                            in_=w2b.rearrange("(a b) -> a b", b=D)
                            .rearrange("(b p) n -> p b n", p=128)[:, half * 16:(half + 1) * 16,
                                                                 do2 * 128:(do2 + 1) * 128]))
                        w.wait_tok("tensor", tok_w2)
                        w.wait("tensor", "act", h1_act)
                        for k in range(16):
                            dib = half * 16 + k
                            ins = te and te.matmul(sl.ap, w2sl[half][:, k * 128:(k + 1) * 128],
                                                   h1[:, dib * 512:(dib + 1) * 512],
                                                   start=(dib == 0), stop=(dib == 31))
                            if dib == 31:
                                sl.set_ready(w, "pe", ins)
                    for half in range(2):
                        w2slot[half].freed = ("pe", w.c["pe"])
                    sl.pre_read(w, "vector")
                    dst = bufr[:, rcol(do2, tt * 512):rcol(do2, tt * 512) + 512]
                    ins = ve and ve.scalar_tensor_tensor(out=dst, in0=sl.ap,
                                                         scalar=b2[:, do2:do2 + 1],
                                                         in1=dst, op0=ALU.add, op1=ALU.add)
                    sl.set_freed(w, "dve", ins)
                h1slot.freed = ("pe", w.c["pe"])
            if "ffn" in dbg:
                w.wait("sync", "dve")
                w.dma("out", sy and sy.dma_start(out=dbg["ffn"], in_=bufr))
                w.wait_all("vector", "out")
            if "bt16b" in dbg:
                w.wait("sync", "dve")
                w.dma("out", sy and sy.dma_start(out=dbg["bt16b"], in_=bt16))
                w.wait_all("vector", "out")
                w.wait_all("scalar", "out")
                w.wait_all("gpsimd", "out")
            if "h1f" in dbg:
                w.wait("sync", "act")
                w.dma("out", sy and sy.dma_start(out=dbg["h1f"], in_=h1))
                w.wait_all("gpsimd", "out")
                w.wait_all("scalar", "out")

            w.wait("gpsimd", "pe")
            w.wait("gpsimd", "dve")
            if cv_state["sttok"][0] is not None:
                w.wait_tok("gpsimd", cv_state["sttok"][0])
            if cv_state["sttok"][1] is not None:
                w.wait_tok("gpsimd", cv_state["sttok"][1])
            upslot = [Slot(), Slot()]
            gpslot = [Slot(), Slot()]
            dwslot = [Slot(), Slot()]
            tuslot = [Slot(), Slot()]
            tdslot = [Slot(), Slot()]
            h16slot = Slot()
            combt_slot = Slot()
            for th in range(2):
                tbase = th * 1024
                combt_slot.pre_write(w, "vector")
                for tb8 in range(8):
                    tbg = th * 8 + tb8
                    sl = PF[6]
                    sl.pre_write(w, "tensor")
                    w.wait("tensor", "dve")
                    ins = te and te.transpose(pf[6][0:8, 0:128], comb[:, tbg * 8:(tbg + 1) * 8], id32)
                    sl.set_ready(w, "pe", ins)
                    sl.pre_read(w, "vector")
                    ins = ve and ve.tensor_copy(out=combt[0:8, tb8 * 128:(tb8 + 1) * 128],
                                                in_=pf[6][0:8, 0:128])
                    sl.set_freed(w, "dve", ins)
                combt_dve = w.c["dve"]
                for e in range(E):
                    for tb2 in range(2):
                        sl = PF[6]
                        sl.pre_write(w, "tensor")
                        w.wait("tensor", "dve", combt_dve)
                        ins = te and te.matmul(pf[6], sel[:, e * 128:(e + 1) * 128],
                                               combt[0:8, tb2 * 512:(tb2 + 1) * 512],
                                               start=True, stop=True)
                        sl.set_ready(w, "pe", ins)
                        sl.pre_read(w, "vector")
                        ins = ve and ve.tensor_copy(out=cbuf[:, tb2 * 512:(tb2 + 1) * 512], in_=pf[6])
                        sl.set_freed(w, "dve", ins)
                    cb_dve = w.c["dve"]
                    h16slot.pre_write(w, "scalar")
                    for fb2 in range(NFB // 2):
                        s = fb2 % 2
                        upslot[s].pre_write(w, "gpsimd")
                        tok_u = w.dma("wdm", gp and gp.dma_start(
                            out=upw[s].rearrange("p (b n) -> p b n", b=NDB),
                            in_=upt[e].rearrange("(b p) f -> p b f", p=128)[:, :, fb2 * 256:(fb2 + 1) * 256]))
                        gpslot[s].pre_write(w, "gpsimd")
                        tok_f = w.dma("wdm", gp and gp.dma_start(
                            out=gpw[s].rearrange("p (b n) -> p b n", b=NDB),
                            in_=gpt[e].rearrange("(b p) f -> p b f", p=128)[:, :, fb2 * 256:(fb2 + 1) * 256]))
                        w.wait_tok("tensor", tok_u)
                        w.wait_tok("tensor", tok_f)
                        for k in range(2):
                            fb = fb2 * 2 + k
                            for tb2 in range(2):
                                r3 = (fb * 2 + tb2) % 3
                                slu = PF[(0, 1, 4)[r3]]
                                slg = PF[(2, 3, 5)[r3]]
                                slu.pre_write(w, "tensor")
                                slg.pre_write(w, "tensor")
                                for db in range(NDB):
                                    c0 = xcol(db, HALO + tbase + tb2 * 512)
                                    ins = te and te.matmul(
                                        slu.ap, upw[s][:, db * 256 + k * 128:db * 256 + k * 128 + 128],
                                        xnt16[:, c0:c0 + 512],
                                        start=(db == 0), stop=(db == 7))
                                    if db == 7:
                                        slu.set_ready(w, "pe", ins)
                                for db in range(NDB):
                                    c0 = xcol(db, HALO + tbase + tb2 * 512)
                                    ins = te and te.matmul(
                                        slg.ap, gpw[s][:, db * 256 + k * 128:db * 256 + k * 128 + 128],
                                        xnt16[:, c0:c0 + 512],
                                        start=(db == 0), stop=(db == 7))
                                    if db == 7:
                                        slg.set_ready(w, "pe", ins)
                                slu.pre_read(w, "scalar")
                                tuslot[tb2].pre_write(w, "scalar")
                                ins = se and se.activation(out=tu[tb2], in_=slu.ap, func=AF.Identity,
                                                           bias=upb[:, e * NFB + fb:e * NFB + fb + 1])
                                slu.set_freed(w, "act", ins)
                                a_u = w.c["act"]
                                slg.pre_read(w, "vector")
                                w.wait("vector", "act", a_u)
                                d_m = w.inc("dve", ve and ve.scalar_tensor_tensor(
                                    out=tu[tb2], in0=slg.ap,
                                    scalar=gpb[:, e * NFB + fb:e * NFB + fb + 1],
                                    in1=tu[tb2], op0=ALU.add, op1=ALU.mult))
                                slg.freed = ("dve", d_m)
                                w.wait("scalar", "dve", d_m)
                                o0 = fb * 1024 + tb2 * 512
                                a_h = w.inc("act", se and se.activation(out=h16[:, o0:o0 + 512],
                                                                        in_=tu[tb2], func=AF.Silu))
                                tuslot[tb2].freed = ("act", a_h)
                        upslot[s].freed = ("pe", w.c["pe"])
                        gpslot[s].freed = ("pe", w.c["pe"])
                    h_act = w.c["act"]
                    w.wait("vector", "dve", cb_dve)
                    for do in range(NDB):
                        s = do % 2
                        dwslot[s].pre_write(w, "gpsimd")
                        tok_d = w.dma("wdm", gp and gp.dma_start(
                            out=dwts[s].rearrange("p (b n) -> p b n", b=NFB),
                            in_=dwt[e].rearrange("(b p) n -> p b n", p=128)[:, :, do * 128:(do + 1) * 128]))
                        w.wait_tok("tensor", tok_d)
                        w.wait("tensor", "act", h_act)
                        for tb2 in range(2):
                            sl = PF[4 + tb2]
                            sl.pre_write(w, "tensor")
                            for fb in range(NFB):
                                o0 = fb * 1024 + tb2 * 512
                                ins = te and te.matmul(sl.ap, dwts[s][:, fb * 128:(fb + 1) * 128],
                                                       h16[:, o0:o0 + 512],
                                                       start=(fb == 0), stop=(fb == NFB - 1))
                                if fb == NFB - 1:
                                    sl.set_ready(w, "pe", ins)
                            sl.pre_read(w, "vector")
                            tdslot[tb2].pre_write(w, "vector")
                            ins = ve and ve.scalar_tensor_tensor(
                                out=td[tb2], in0=sl.ap,
                                scalar=dnb[:, e * 8 + do:e * 8 + do + 1],
                                in1=cbuf[:, tb2 * 512:(tb2 + 1) * 512],
                                op0=ALU.add, op1=ALU.mult)
                            sl.set_freed(w, "dve", ins)
                            dst = bufr[:, rcol(do, tbase + tb2 * 512):rcol(do, tbase + tb2 * 512) + 512]
                            d_add = w.inc("dve", ve and ve.tensor_add(dst, dst, td[tb2]))
                            tdslot[tb2].freed = ("dve", d_add)
                        dwslot[s].freed = ("pe", w.c["pe"])
                    h16slot.freed = ("pe", w.c["pe"])
                combt_slot.freed = ("pe", w.c["pe"])
            if "moe" in dbg:
                w.wait("sync", "dve")
                w.dma("out", sy and sy.dma_start(out=dbg["moe"], in_=bufr))
            if "h16" in dbg:
                w.wait("sync", "act")
                w.dma("out", sy and sy.dma_start(out=dbg["h16"], in_=h16))

            res_dve = w.c["dve"]
            w.wait("sync", "dve", res_dve)
            fslot = [Slot(), Slot()]
            oslot = [Slot(), Slot()]
            for i in range(16):
                s = i % 2
                fslot[s].pre_write(w, "sync")
                tok_xf = w.dma("xin", sy and sy.dma_start(out=xt[s], in_=xc[(2 + i) * 128:(3 + i) * 128, :]))
                for half in range(2):
                    sl = PF[half]
                    sl.pre_write(w, "tensor")
                    w.wait("tensor", "dve", res_dve)
                    for k in range(4):
                        db = half * 4 + k
                        c0 = rcol(db, i * 128)
                        ins = te and te.transpose(pf[half][:, k * 128:(k + 1) * 128],
                                                  bufr[:, c0:c0 + 128], id32)
                        if k == 3:
                            sl.set_ready(w, "pe", ins)
                oslot[s].pre_write(w, "vector")
                w.wait_tok("vector", tok_xf)
                for half in range(2):
                    sl = PF[half]
                    sl.pre_read(w, "vector")
                    ins = ve and ve.tensor_add(fot[s][:, half * 512:(half + 1) * 512],
                                               pf[half][:, 0:512],
                                               xt[s][:, half * 512:(half + 1) * 512])
                    sl.set_freed(w, "dve", ins)
                d_out = w.c["dve"]
                fslot[s].freed = ("dve", d_out)
                w.wait("sync", "dve", d_out)
                ins = sy and sy.dma_start(out=y[i * 128:(i + 1) * 128, :], in_=fot[s])
                oslot[s].freed = w.dma("out", ins)
            w.wait_all("sync", "out")

        engs = {}

        @block.tensor
        def _(eng):
            engs["tensor"] = eng
            flow(Wk("tensor", engs, semh))

        @block.vector
        def _(eng):
            engs["vector"] = eng
            flow(Wk("vector", engs, semh))

        @block.scalar
        def _(eng):
            engs["scalar"] = eng
            flow(Wk("scalar", engs, semh))

        @block.gpsimd
        def _(eng):
            engs["gpsimd"] = eng
            flow(Wk("gpsimd", engs, semh))

        @block.sync
        def _(eng):
            engs["sync"] = eng
            flow(Wk("sync", engs, semh))

    return nc


def _prep_inputs(L0_attn_x, L0_attn_Wq, L0_attn_bq, L0_attn_Wk, L0_attn_bk,
                 L0_attn_Wv, L0_attn_bv, L0_attn_Wo, L0_attn_bo,
                 L0_attn_gamma, L0_attn_beta, L0_attn_W1, L0_attn_b1,
                 L0_attn_W2, L0_attn_b2, L0_ffn_gate_w, L0_ffn_gate_b,
                 L0_ffn_up_w, L0_ffn_up_b, L0_ffn_gp_w, L0_ffn_gp_b,
                 L0_ffn_down_w, L0_ffn_down_b):
    f = lambda a: np.ascontiguousarray(np.asarray(a, dtype=np.float32))
    x = f(L0_attn_x).reshape(16384, D)
    xpad = np.zeros((16384 + 2 * HALO, D), np.float32)
    xpad[HALO:HALO + 16384] = x

    colb = lambda v, nb: np.ascontiguousarray(f(v).reshape(nb, 128).T)
    wq_, wk_, wv_, wo_ = f(L0_attn_Wq), f(L0_attn_Wk), f(L0_attn_Wv), f(L0_attn_Wo)
    w1_, w2_ = f(L0_attn_W1), f(L0_attn_W2)
    upt = np.zeros((E, D, FPAD), np.float32)
    upt[:, :, :F] = np.transpose(f(L0_ffn_up_w), (0, 2, 1))
    gpt = np.zeros((E, D, FPAD), np.float32)
    gpt[:, :, :F] = np.transpose(f(L0_ffn_gp_w), (0, 2, 1))
    dwt = np.zeros((E, FPAD, D), np.float32)
    dwt[:, :F, :] = np.transpose(f(L0_ffn_down_w), (0, 2, 1))
    upb = np.zeros((E, FPAD), np.float32); upb[:, :F] = f(L0_ffn_up_b)
    gpb = np.zeros((E, FPAD), np.float32); gpb[:, :F] = f(L0_ffn_gp_b)
    upb_re = np.ascontiguousarray(np.transpose(upb.reshape(E, NFB, 128), (2, 0, 1)).reshape(128, E * NFB))
    gpb_re = np.ascontiguousarray(np.transpose(gpb.reshape(E, NFB, 128), (2, 0, 1)).reshape(128, E * NFB))
    dnb_re = np.ascontiguousarray(np.transpose(f(L0_ffn_down_b).reshape(E, 8, 128), (2, 0, 1)).reshape(128, E * 8))
    gw = f(L0_ffn_gate_w)
    gwt_re = np.ascontiguousarray(np.transpose(gw.T.reshape(8, 128, E), (1, 0, 2)).reshape(128, 8 * E))
    gbb = np.ascontiguousarray(np.tile(f(L0_ffn_gate_b)[None, :], (128, 1)))
    bvbc = np.ascontiguousarray(np.tile(f(L0_attn_bv)[None, :], (128, 1)))
    selm = np.zeros((8, 1024), np.float32)
    for e in range(8):
        selm[e, e * 128:(e + 1) * 128] = 1.0

    qi = np.arange(256)[:, None]
    kj = np.arange(768)[None, :]
    band = (kj >= qi) & (kj <= qi + 512)
    mb = np.where(band, 0.0, NEG).astype(np.float32)
    m_int = np.concatenate([mb[0:128], mb[128:256]], axis=1)
    kill_first = np.where(np.arange(768) < 256, NEG, 0.0).astype(np.float32)
    kill_last = np.where(np.arange(768) >= 512, NEG, 0.0).astype(np.float32)

    per_core = []
    for c in range(8):
        m_e0 = m_int.copy()
        m_e7 = m_int.copy()
        if c == 0:
            for qh in range(2):
                m_e0[:, qh * 768:(qh + 1) * 768] = np.minimum(
                    m_e0[:, qh * 768:(qh + 1) * 768], kill_first[None, :])
        if c == 7:
            for qh in range(2):
                m_e7[:, qh * 768:(qh + 1) * 768] = np.minimum(
                    m_e7[:, qh * 768:(qh + 1) * 768], kill_last[None, :])
        per_core.append(dict(
            xc=np.ascontiguousarray(xpad[c * T:c * T + TH]),
            wq=wq_, wk=wk_, wv=wv_, wo=wo_, w1=w1_, w2=w2_,
            upt=upt, gpt=gpt, dwt=dwt,
            bq=colb(L0_attn_bq, 8), bk=colb(L0_attn_bk, 8),
            bo=colb(L0_attn_bo, 8), b2=colb(L0_attn_b2, 8),
            gamma=colb(L0_attn_gamma, 8), beta=colb(L0_attn_beta, 8),
            b1=colb(L0_attn_b1, 32),
            gwt=gwt_re, gbb=gbb, upb=upb_re, gpb=gpb_re, dnb=dnb_re,
            bvbc=bvbc, sel=selm,
            maskb=m_int.astype(BFNP), maske0=m_e0.astype(BFNP),
            maske7=m_e7.astype(BFNP),
        ))
    return per_core


def kernel(**inputs):
    in_maps = _prep_inputs(**inputs)
    nc = build_program()
    res = run_bass_kernel_spmd(nc, in_maps, core_ids=list(range(8)))
    out = np.concatenate([np.asarray(r["y"]) for r in res.results], axis=0)
    return out.reshape(1, 16384, D).astype(np.float32)


# version 09
# speedup vs baseline: 1.0401x; 1.0401x over previous
import numpy as np
import ml_dtypes
import concourse.bass as bass
import concourse.mybir as mybir
from concourse.bass_utils import run_bass_kernel_spmd

F32 = mybir.dt.float32
BF16 = mybir.dt.bfloat16
U8 = mybir.dt.uint8
AF = mybir.ActivationFunctionType
ALU = mybir.AluOpType
AX = mybir.AxisListType.X
BFNP = ml_dtypes.bfloat16

D = 1024
NDB = 8
T = 2048
HALO = 256
TH = T + 2 * HALO
NTILES = TH // 128
H = 8
E = 8
F = 3264
FPAD = 3328
NFB = FPAD // 128
DFF = 4096
SCALE = float(1.0 / np.sqrt(128.0))
NEG = -1.0e30
EPS = 1.0e-5

P_ID32 = 0
P_ID16 = 512
P_ONES32 = 768
P_ONESC32 = 1280
P_BQ = 1312
P_BK = 1344
P_BO = 1376
P_B2 = 1408
P_GAMMA = 1440
P_BETA = 1472
P_B1 = 1504
P_GWT = 1632
P_GBB = 1888
P_UPB = 1920
P_GPB = 2752
P_DNB = 3584
P_COMB = 3840
P_SMALL = 4352
P_SEL = 4416
P_XNT16 = 8512
P_BUFR = 49472
S = 115008
A_XT0 = S + 0
A_XT1 = S + 4096
A_XN0 = S + 8192
A_XN1 = S + 12288
A_XN16_0 = S + 16384
A_XN16_1 = S + 18432
A_G8 = S + 20480
A_P8 = A_G8 + 32
A_PRB = A_G8 + 64
A_TOP8 = A_G8 + 96
A_MSK = A_G8 + 128
A_CW = A_G8 + 160
A_ST6 = A_G8 + 192
FI_OT0 = S + 21504
FI_OT1 = S + 25600
AT_V16 = S + 0
AT_BVBC = S + 24576
AT_MB = S + 28672
AT_ME0 = S + 31744
AT_ME7 = S + 34816
AT_OTG = S + 37888
AT_KT = S + 54272
AT_Q16 = S + 60416
AT_WSL0 = S + 64512
AT_WSL1 = S + 68608
AT_PN0 = S + 72704
AT_PN1 = S + 75776
AT_PEXPA = S + 78848
AT_PEXPB = S + 80384
AT_PTA = S + 81920
AT_PTB = S + 83456
L_SUM = S + 0
L_SQ = S + 8192
L_TA = S + 20480
L_TMP = S + 70656
FF_W1SL = S + 0
FF_W2SL = S + 16384
FF_BT16 = S + 24576
FF_H1 = S + 57344
M_COMBT = S + 0
M_CB = S + 4096
M_TU = S + 6144
M_TD = S + 10240
M_UPW = S + 14336
M_GPW = S + 22528
M_DWT = S + 30720
M_H16 = S + 44032
ARENA = S + 97280

DMAQ = {"xin": 4, "wdm": 4, "out": 3, "cst": 1, "cvl": 2, "cvs": 2}
SEMS = tuple(f"{q}{j}" for q, n in DMAQ.items() for j in range(n)) + ("pe", "dve", "act", "gps")


class Wk:
    def __init__(self, which, engs, sems):
        self.which = which
        self.engs = engs
        self.sems = sems
        self.c = {k: 0 for k in sems}
        self.hw = {}
        self.qidx = {q: 0 for q in DMAQ}

    def dma(self, q, instr):
        i = self.qidx[q]
        self.qidx[q] = i + 1
        ns = DMAQ[q]
        sname = f"{q}{i % ns}"
        v = 16 * (i // ns + 1)
        self.c[sname] = v
        if instr is not None:
            instr.then_inc(self.sems[sname], 16)
        return (sname, v)

    def wait_tok(self, engname, tok):
        self.wait(engname, tok[0], tok[1])

    def wait_all(self, engname, q):
        ns = DMAQ[q]
        tot = self.qidx[q]
        for j in range(ns):
            uses = (tot - j + ns - 1) // ns
            if uses > 0:
                self.wait(engname, f"{q}{j}", 16 * uses)

    def eng(self, name):
        return self.engs[name] if self.which == name else None

    SELFW = {"dve": "vector", "act": "scalar", "gps": "gpsimd"}

    def inc(self, sem, instr, k=1):
        self.c[sem] += k
        if instr is not None:
            instr.then_inc(self.sems[sem], k)
        eng = self.SELFW.get(sem)
        if eng is not None and self.which == eng:
            v = self.c[sem]
            if self.hw.get(sem, -1) < v:
                self.hw[sem] = v
                self.engs[eng].wait_ge(self.sems[sem], v)
        return self.c[sem]

    def wait(self, engname, sem, val=None):
        v = self.c[sem] if val is None else val
        if v <= 0:
            return
        if self.which == engname:
            key = sem
            if self.hw.get(key, -1) >= v:
                return
            self.hw[key] = v
            self.engs[engname].wait_ge(self.sems[sem], v)


class Slot:
    def __init__(self, ap=None):
        self.ap = ap
        self.ready = None
        self.freed = None

    def pre_write(self, w, eng):
        if self.freed is not None:
            w.wait(eng, *self.freed)

    def set_ready(self, w, sem, instr, k=1):
        self.ready = (sem, w.inc(sem, instr, k))

    def pre_read(self, w, eng):
        if self.ready is not None:
            w.wait(eng, *self.ready)

    def set_freed(self, w, sem, instr, k=1):
        self.freed = (sem, w.inc(sem, instr, k))


def build_program(debug=()):
    nc = bass.Bass("TRN2", target_bir_lowering=False)

    def din(name, shape, dt=F32):
        return nc.dram_tensor(name, list(shape), dt, kind="ExternalInput").ap()

    xc = din("xc", [TH, D])
    wq = din("wq", [D, D]); wk = din("wk", [D, D])
    wv = din("wv", [D, D]); wo = din("wo", [D, D])
    w1 = din("w1", [D, DFF]); w2 = din("w2", [DFF, D])
    upt = din("upt", [E, D, FPAD]); gpt = din("gpt", [E, D, FPAD])
    dwt = din("dwt", [E, FPAD, D])
    bq_i = din("bq", [128, 8]); bk_i = din("bk", [128, 8])
    bo_i = din("bo", [128, 8]); b2_i = din("b2", [128, 8])
    gam_i = din("gamma", [128, 8]); bet_i = din("beta", [128, 8])
    b1_i = din("b1", [128, 32])
    gwt_i = din("gwt", [128, 64]); gbb_i = din("gbb", [128, 8])
    upb_i = din("upb", [128, E * NFB]); gpb_i = din("gpb", [128, E * NFB])
    dnb_i = din("dnb", [128, E * 8])
    bvbc_i = din("bvbc", [128, D])
    sel_i = din("sel", [8, 1024])
    mb_i = din("maskb", [128, 1536], BF16)
    me0_i = din("maske0", [128, 1536], BF16)
    me7_i = din("maske7", [128, 1536], BF16)

    w1b = nc.dram_tensor("w1b", [D * DFF], BF16, kind="Internal").ap()
    w2b = nc.dram_tensor("w2b", [DFF * D], BF16, kind="Internal").ap()
    y = nc.dram_tensor("y", [T, D], F32, kind="ExternalOutput").ap()
    dbg = {}
    for name in debug:
        if name == "xnt":
            dbg[name] = nc.dram_tensor("d_xnt", [128, NDB * TH], BF16, kind="ExternalOutput").ap()
        elif name in ("xnTf", "aT", "bT", "ffn", "moe"):
            dbg[name] = nc.dram_tensor("d_" + name, [128, NDB * T], F32, kind="ExternalOutput").ap()
        elif name == "comb":
            dbg[name] = nc.dram_tensor("d_comb", [128, 128], F32, kind="ExternalOutput").ap()
        elif name == "gate8":
            dbg[name] = nc.dram_tensor("d_gate8", [128, 48], F32, kind="ExternalOutput").ap()
        elif name in ("bt16", "bt16b"):
            dbg[name] = nc.dram_tensor("d_" + name, [128, NDB * T], BF16, kind="ExternalOutput").ap()
        elif name == "h1f":
            dbg[name] = nc.dram_tensor("d_h1f", [128, 32 * 512], BF16, kind="ExternalOutput").ap()
        elif name == "h16":
            dbg[name] = nc.dram_tensor("d_h16", [128, NFB * 1024], BF16, kind="ExternalOutput").ap()

    arena = nc.alloc_sbuf_tensor("arena", [128, ARENA], U8)
    arena_addr = nc.lookup_mloc(arena).addr

    def place(name, off, shape, dt):
        h = nc.virtual_sbuf_tensor(name, list(shape), dt)
        nc._tensor(h.name, list(shape), dt, type="SB")
        mloc = nc.lookup_mloc(h)
        mloc.allocated = True
        mloc.addr = arena_addr + off
        return h.ap()

    id32 = place("id32", P_ID32, [128, 128], F32)
    id16 = place("id16", P_ID16, [128, 128], BF16)
    ones32 = place("ones32", P_ONES32, [1, 128], F32)
    onesc32 = place("onesc32", P_ONESC32, [128, 1], F32)
    bq = place("bq_s", P_BQ, [128, 8], F32)
    bk = place("bk_s", P_BK, [128, 8], F32)
    bo = place("bo_s", P_BO, [128, 8], F32)
    b2 = place("b2_s", P_B2, [128, 8], F32)
    gamma = place("gamma_s", P_GAMMA, [128, 8], F32)
    beta = place("beta_s", P_BETA, [128, 8], F32)
    b1 = place("b1_s", P_B1, [128, 32], F32)
    gwt = place("gwt_s", P_GWT, [128, 64], F32)
    gbb = place("gbb_s", P_GBB, [128, 8], F32)
    upb = place("upb_s", P_UPB, [128, E * NFB], F32)
    gpb = place("gpb_s", P_GPB, [128, E * NFB], F32)
    dnb = place("dnb_s", P_DNB, [128, E * 8], F32)
    comb = place("comb_s", P_COMB, [128, 128], F32)
    small = place("small_s", P_SMALL, [128, 16], F32)
    sel = place("sel_s", P_SEL, [8, 1024], F32)
    xnt16 = place("xnt16", P_XNT16, [128, NDB * TH], BF16)
    bufr = place("bufr", P_BUFR, [128, NDB * T], F32)
    xt = [place("xt0", A_XT0, [128, D], F32), place("xt1", A_XT1, [128, D], F32)]
    xn32 = [place("xn0", A_XN0, [128, D], F32), place("xn1", A_XN1, [128, D], F32)]
    xn16 = [place("xn16_0", A_XN16_0, [128, D], BF16), place("xn16_1", A_XN16_1, [128, D], BF16)]
    g8 = place("g8_s", A_G8, [128, 8], F32)
    g8all = place("g8all", FI_OT0, [128, 128], F32)
    gsm = place("gsm", FI_OT0 + 512, [128, 64], F32)
    p8 = place("p8_s", A_P8, [128, 8], F32)
    prb = place("prb_s", A_PRB, [128, 8], F32)
    top8 = place("top8_s", A_TOP8, [128, 8], F32)
    mskg = place("mskg_s", A_MSK, [128, 8], F32)
    cw8 = place("cw8_s", A_CW, [128, 8], F32)
    st6 = place("st6_s", A_ST6, [128, 12], F32)
    fot = [place("fot0", FI_OT0, [128, D], F32), place("fot1", FI_OT1, [128, D], F32)]
    v16 = place("v16", AT_V16, [128, 12 * 1024], BF16)
    bvbc = place("bvbc_s", AT_BVBC, [128, D], F32)
    maskb = place("maskb_s", AT_MB, [128, 1536], BF16)
    maske0 = place("maske0_s", AT_ME0, [128, 1536], BF16)
    maske7 = place("maske7_s", AT_ME7, [128, 1536], BF16)
    otg = place("otg", AT_OTG, [128, H * 1024], BF16)
    kt16 = place("kt16", AT_KT, [128, 2 * 1536], BF16)
    q16 = place("q16", AT_Q16, [128, 2 * 1024], BF16)
    wsl = [place("wsl0", AT_WSL0, [128, 2048], BF16), place("wsl1", AT_WSL1, [128, 2048], BF16)]
    pn = [place("pn0", AT_PN0, [128, 768], F32), place("pn1", AT_PN1, [128, 768], F32)]
    pexp = [place("pexpA", AT_PEXPA, [128, 768], BF16), place("pexpB", AT_PEXPB, [128, 768], BF16)]
    pt16 = [place("ptA", AT_PTA, [128, 768], BF16), place("ptB", AT_PTB, [128, 768], BF16)]
    lsum = place("lsum", L_SUM, [1, 2048], F32)
    lsq = place("lsq", L_SQ, [1, 2048], F32)
    ltmp = place("ltmp", L_TMP, [1, 2048], F32)
    lta = [place("lta0", L_TA, [128, 512], F32), place("lta1", L_TA + 2048, [128, 512], F32)]
    w1sl = [place("w1sl0", FF_W1SL, [128, 4096], BF16), place("w1sl1", FF_W1SL + 8192, [128, 4096], BF16)]
    w2sl = [place("w2sl0", FF_W2SL, [128, 2048], BF16), place("w2sl1", FF_W2SL + 4096, [128, 2048], BF16)]
    bt16 = place("bt16", FF_BT16, [128, NDB * T], BF16)
    h1 = place("h1", FF_H1, [128, 32 * 512], BF16)
    combt = place("combt", M_COMBT, [8, 1024], F32)
    cbuf = place("cbuf", M_CB, [128, 1024], BF16)
    tu = [place("tu0", M_TU, [128, 512], F32), place("tu1", M_TU + 2048, [128, 512], F32)]
    td = [place("td0", M_TD, [128, 512], F32), place("td1", M_TD + 2048, [128, 512], F32)]
    upw = [place("upw0", M_UPW, [128, 2048], BF16), place("upw1", M_UPW + 4096, [128, 2048], BF16)]
    cvb = [place("cvb0", S + 84992, [128, 2048], BF16), place("cvb1", S + 89088, [128, 2048], BF16)]
    gpw = [place("gpw0", M_GPW, [128, 2048], BF16), place("gpw1", M_GPW + 4096, [128, 2048], BF16)]
    dwts = [place("dwt0", M_DWT, [128, 3328], BF16), place("dwt1", M_DWT + 6656, [128, 3328], BF16)]
    h16 = place("h16", M_H16, [128, NFB * 1024], BF16)

    pf = [nc.alloc_psum_tensor(f"pf{i}", [128, 512], F32).ap() for i in range(7)]
    pb = nc.alloc_psum_tensor("pb", [128, 1024], BF16).ap()

    from contextlib import ExitStack
    with ExitStack() as ctx:
        semh = {s: ctx.enter_context(nc.semaphore(s)) for s in SEMS}
        block = ctx.enter_context(nc.Block())

        def flow(w: Wk):
            te = w.eng("tensor"); ve = w.eng("vector"); se = w.eng("scalar")
            gp = w.eng("gpsimd"); sy = w.eng("sync")
            PF = [Slot(p) for p in pf]
            PB = Slot(pb)
            xcol = lambda db, t0: db * TH + t0
            rcol = lambda db, t0: db * T + t0

            for dst, src in ((bq, bq_i), (bk, bk_i), (bo, bo_i), (b2, b2_i),
                             (gamma, gam_i), (beta, bet_i), (b1, b1_i),
                             (gwt, gwt_i), (gbb, gbb_i), (upb, upb_i),
                             (gpb, gpb_i), (dnb, dnb_i), (bvbc, bvbc_i),
                             (sel, sel_i), (maskb, mb_i), (maske0, me0_i),
                             (maske7, me7_i)):
                w.dma("cst", sy and sy.dma_start(out=dst, in_=src))
            w.inc("gps", gp and gp.iota(xn32[0][:, 0:128], pattern=[[1, 128]], base=0,
                                        channel_multiplier=-1,
                                        allow_small_or_imprecise_dtypes=True), 1)
            w.wait("vector", "gps")
            w.inc("dve", ve and ve.memset(small[:, 15:16], EPS))
            w.inc("dve", ve and ve.tensor_scalar(out=id32, in0=xn32[0][:, 0:128],
                                                 scalar1=0.0, scalar2=None,
                                                 op0=ALU.is_equal))
            w.inc("dve", ve and ve.tensor_copy(out=id16, in_=id32))
            w.inc("dve", ve and ve.memset(ones32, 1.0))
            w.inc("dve", ve and ve.memset(onesc32, 1.0))
            w.inc("dve", ve and ve.tensor_copy(out=small[:, 14:15], in_=small[:, 15:16]))
            ident_dve = w.c["dve"]

            xslot = [Slot(), Slot()]
            xnslot = [Slot(), Slot()]
            xn16slot = [Slot(), Slot()]
            w.wait("scalar", "dve", ident_dve)
            cv_jobs = []
            for j in range(16):
                cv_jobs.append((w1.rearrange("a b -> (a b)"), w1b, j))
            for j in range(16):
                cv_jobs.append((w2.rearrange("a b -> (a b)"), w2b, j))
            cv_state = {"i": 0, "tok": [None, None], "sttok": [None, None]}

            def emit_convert(n):
                CHUNK = 128 * 2048
                while cv_jobs and n > 0:
                    srcf, dstb, j = cv_jobs.pop(0)
                    i = cv_state["i"]; cv_state["i"] += 1
                    sl = i % 2
                    if cv_state["sttok"][sl] is not None:
                        w.wait_tok("gpsimd", cv_state["sttok"][sl])
                    cv_state["tok"][sl] = w.dma("cvl", gp and gp.dma_start(
                        out=cvb[sl],
                        in_=srcf[j * CHUNK:(j + 1) * CHUNK].rearrange("(p n) -> p n", p=128)))
                    w.wait_tok("sync", cv_state["tok"][sl])
                    cv_state["sttok"][sl] = w.dma("cvs", sy and sy.dma_start(
                        out=dstb[j * CHUNK:(j + 1) * CHUNK].rearrange("(p n) -> p n", p=128),
                        in_=cvb[sl]))
                    n -= 1
            w.wait_all("vector", "cst")
            w.wait_all("scalar", "cst")
            w.wait_all("tensor", "cst")
            for i in range(NTILES):
                s = i % 2
                xslot[s].pre_write(w, "sync")
                tok_x = w.dma("xin", sy and sy.dma_start(out=xt[s], in_=xc[i * 128:(i + 1) * 128, :]))
                emit_convert(1)
                w.wait_tok("vector", tok_x)
                xnslot[s].pre_write(w, "vector")
                w.inc("dve", ve and ve.reduce_sum(out=small[:, 0:1], in_=xt[s], axis=AX))
                w.inc("dve", ve and ve.tensor_mul(xn32[s], xt[s], xt[s]))
                w.inc("dve", ve and ve.reduce_sum(out=small[:, 1:2], in_=xn32[s], axis=AX))
                w.inc("dve", ve and ve.tensor_scalar_mul(small[:, 2:3], small[:, 0:1], 1.0 / D))
                w.inc("dve", ve and ve.tensor_scalar_mul(small[:, 3:4], small[:, 1:2], 1.0 / D))
                w.inc("dve", ve and ve.tensor_mul(small[:, 4:5], small[:, 2:3], small[:, 2:3]))
                d_var = w.inc("dve", ve and ve.tensor_sub(small[:, 3:4], small[:, 3:4], small[:, 4:5]))
                w.wait("scalar", "dve", d_var)
                w.inc("act", se and se.activation(out=small[:, 5:6], in_=small[:, 3:4], func=AF.Ln, bias=small[:, 15:16]))
                a_rstd = w.inc("act", se and se.activation(out=small[:, 6:7], in_=small[:, 5:6],
                                                           func=AF.Exp, scale=-0.5))
                w.wait("vector", "act", a_rstd)
                d_xn = w.inc("dve", ve and ve.tensor_scalar(out=xn32[s], in0=xt[s],
                                                            scalar1=small[:, 2:3],
                                                            scalar2=small[:, 6:7],
                                                            op0=ALU.subtract, op1=ALU.mult))
                xslot[s].freed = ("dve", d_xn)
                w.wait("scalar", "dve", d_xn)
                xn16slot[s].pre_write(w, "scalar")
                a_xn16 = w.inc("act", se and se.copy(out=xn16[s], in_=xn32[s]))
                is_center = 2 <= i < 18
                tb = i - 2
                w.wait("tensor", "act", a_xn16)
                w.wait("tensor", "dve", d_xn)
                if is_center:
                    for half in range(2):
                        PF[half].pre_write(w, "tensor")
                        for k in range(4):
                            db = half * 4 + k
                            ins = te and te.transpose(pf[half][:, k * 128:(k + 1) * 128],
                                                      xn32[s][:, db * 128:(db + 1) * 128], id32)
                            if k == 3:
                                PF[half].set_ready(w, "pe", ins)
                    xnslot[s].freed = ("pe", w.c["pe"])
                    for half in range(2):
                        PF[half].pre_read(w, "scalar")
                        dst = bufr.rearrange("p (b t) -> p b t", b=NDB)[:, half * 4:(half + 1) * 4,
                                                                       tb * 128:(tb + 1) * 128]
                        src = pf[half].rearrange("p (k t) -> p k t", k=4)
                        ins = se and se.copy(out=dst, in_=src)
                        PF[half].set_freed(w, "act", ins)
                PB.pre_write(w, "tensor")
                for db in range(NDB):
                    ins = te and te.transpose(pb[:, db * 128:(db + 1) * 128],
                                              xn16[s][:, db * 128:(db + 1) * 128], id16)
                    if db == 7:
                        PB.set_ready(w, "pe", ins)
                xn16slot[s].freed = ("pe", w.c["pe"])
                if not is_center:
                    xnslot[s].freed = ("pe", w.c["pe"])
                PB.pre_read(w, "scalar")
                dstx = xnt16.rearrange("p (b t) -> p b t", b=NDB)[:, :, i * 128:(i + 1) * 128]
                ins = se and se.copy(out=dstx, in_=pb.rearrange("p (k t) -> p k t", k=NDB))
                PB.set_freed(w, "act", ins)
                if is_center:
                    PF[2].pre_write(w, "tensor")
                    w.wait("tensor", "dve")
                    w.wait("tensor", "act")
                    for db in range(NDB):
                        c0 = rcol(db, tb * 128)
                        ins = te and te.matmul(pf[2][:, 0:8], bufr[:, c0:c0 + 128],
                                               gwt[:, db * 8:(db + 1) * 8],
                                               start=(db == 0), stop=(db == 7))
                        if db == 7:
                            PF[2].set_ready(w, "pe", ins)
                    PF[2].pre_read(w, "vector")
                    d1 = w.inc("dve", ve and ve.tensor_add(g8all[:, tb * 8:(tb + 1) * 8],
                                                           pf[2][:, 0:8], gbb))
                    PF[2].freed = ("dve", d1)

            g3 = g8all.rearrange("p (a b) -> p a b", b=8)
            w.inc("dve", ve and ve.reduce_max(out=gsm[:, 0:16].rearrange("p (a b) -> p a b", b=1),
                                              in_=g3, axis=AX, negate=True))
            w.inc("dve", ve and ve.tensor_tensor(out=g3, in0=g3,
                                                 in1=gsm[:, 0:16].rearrange("p (a b) -> p a b", b=1)
                                                 .to_broadcast([128, 16, 8]),
                                                 op=ALU.add))
            d_g = w.c["dve"]
            w.wait("scalar", "dve", d_g)
            a_g = w.inc("act", se and se.activation(out=g8all, in_=g8all, func=AF.Exp))
            w.wait("vector", "act", a_g)
            w.inc("dve", ve and ve.reduce_sum(out=gsm[:, 16:32].rearrange("p (a b) -> p a b", b=1),
                                              in_=g3, axis=AX))
            w.inc("dve", ve and ve.reciprocal(out=gsm[:, 16:32], in_=gsm[:, 16:32]))
            w.inc("dve", ve and ve.tensor_mul(g3, g3,
                                              gsm[:, 16:32].rearrange("p (a b) -> p a b", b=1)
                                              .to_broadcast([128, 16, 8])))
            w.inc("dve", ve and ve.reduce_max(out=gsm[:, 32:48].rearrange("p (a b) -> p a b", b=1),
                                              in_=g3, axis=AX))
            cm3 = comb.rearrange("p (a b) -> p a b", b=8)
            m1b = gsm[:, 32:48].rearrange("p (a b) -> p a b", b=1).to_broadcast([128, 16, 8])
            w.inc("dve", ve and ve.tensor_tensor(out=cm3, in0=g3, in1=m1b, op=ALU.is_equal))
            w.inc("dve", ve and ve.scalar_tensor_tensor(out=comb, in0=comb, scalar=-2.0,
                                                        in1=g8all, op0=ALU.mult, op1=ALU.add))
            w.inc("dve", ve and ve.reduce_max(out=gsm[:, 48:64].rearrange("p (a b) -> p a b", b=1),
                                              in_=comb.rearrange("p (a b) -> p a b", b=8), axis=AX))
            m2b = gsm[:, 48:64].rearrange("p (a b) -> p a b", b=1).to_broadcast([128, 16, 8])
            w.inc("dve", ve and ve.tensor_add(gsm[:, 32:48], gsm[:, 32:48], gsm[:, 48:64]))
            w.inc("dve", ve and ve.tensor_scalar_add(gsm[:, 32:48], gsm[:, 32:48], 1.0e-20))
            w.inc("dve", ve and ve.reciprocal(out=gsm[:, 32:48], in_=gsm[:, 32:48]))
            rdb = gsm[:, 32:48].rearrange("p (a b) -> p a b", b=1).to_broadcast([128, 16, 8])
            w.inc("dve", ve and ve.tensor_tensor(out=cm3, in0=g3, in1=m2b, op=ALU.is_ge))
            w.inc("dve", ve and ve.tensor_mul(comb, comb, g8all))
            w.inc("dve", ve and ve.tensor_tensor(out=cm3, in0=cm3, in1=rdb, op=ALU.mult))

            if "xnt" in dbg:
                w.wait("sync", "dve")
                w.wait("sync", "act")
                w.dma("out", sy and sy.dma_start(out=dbg["xnt"], in_=xnt16))
            if "xnTf" in dbg:
                w.wait("sync", "dve")
                w.wait("sync", "act")
                w.dma("out", sy and sy.dma_start(out=dbg["xnTf"], in_=bufr))
                w.wait_all("vector", "out")
            if "comb" in dbg:
                w.wait("sync", "dve")
                w.dma("out", sy and sy.dma_start(out=dbg["comb"], in_=comb))
            if "gate8" in dbg:
                w.wait("sync", "dve")
                w.wait("sync", "act")
                w.dma("out", sy and sy.dma_start(out=dbg["gate8"][:, 0:8], in_=g8))
                w.dma("out", sy and sy.dma_start(out=dbg["gate8"][:, 8:16], in_=prb))
                w.dma("out", sy and sy.dma_start(out=dbg["gate8"][:, 16:24], in_=top8))
                w.dma("out", sy and sy.dma_start(out=dbg["gate8"][:, 24:32], in_=mskg))
                w.dma("out", sy and sy.dma_start(out=dbg["gate8"][:, 32:40], in_=cw8))
                w.dma("out", sy and sy.dma_start(out=dbg["gate8"][:, 40:48], in_=small[:, 8:16]))

            w.wait("tensor", "dve")
            w.wait("tensor", "act")
            wslot = [Slot(), Slot()]
            att_it = [0]
            att_prev = [None]
            pexp_free = [0, 0]

            def emit_tr_o(pv):
                osl = pv["osl"]
                h_, cl_ = pv["h"], pv["cl"]
                for qh in range(2):
                    PB.pre_write(w, "tensor")
                    w.wait("tensor", "dve", pv["d_px"][qh])
                    for kc6 in range(6):
                        ins = te and te.transpose(pb[:, kc6 * 128:(kc6 + 1) * 128],
                                                  pexp[qh][:, kc6 * 128:(kc6 + 1) * 128], id16)
                        if kc6 == 5:
                            PB.set_ready(w, "pe", ins)
                    PB.pre_read(w, "scalar")
                    ins = se and se.copy(out=pt16[qh], in_=pb[:, 0:768])
                    PB.set_freed(w, "act", ins)
                    d_pt = w.c["act"]
                    w.wait("tensor", "act", d_pt)
                    for kc6 in range(6):
                        kpb = cl_ * 2 + kc6
                        v0 = kpb * 1024 + h_ * 128
                        ins = te and te.matmul(osl.ap[:, qh * 128:qh * 128 + 128],
                                               v16[:, v0:v0 + 128],
                                               pt16[qh][:, kc6 * 128:(kc6 + 1) * 128],
                                               start=(kc6 == 0), stop=(kc6 == 5))
                        if kc6 == 5:
                            if qh == 1:
                                osl.set_ready(w, "pe", ins)
                            else:
                                w.inc("pe", ins)
                    pexp_free[qh] = w.c["pe"]
                osl.pre_read(w, "scalar")
                o0 = pv["h"] * 1024 + pv["cl"] * 256
                ins = se and se.copy(out=otg[:, o0:o0 + 256], in_=osl.ap[:, 0:256])
                osl.set_freed(w, "act", ins)
            for g in range(2):
                gbase = g * 1024
                cbase = g * 1024
                for qtr in range(4):
                    sq = qtr % 2
                    wslot[sq].pre_write(w, "gpsimd")
                    tok_v = w.dma("wdm", gp and gp.dma_start(
                        out=wsl[sq].rearrange("p (b n) -> p b n", b=NDB),
                        in_=wv.rearrange("(b p) n -> p b n", p=128)[:, :, qtr * 256:(qtr + 1) * 256]))
                    emit_convert(1)
                    w.wait_tok("tensor", tok_v)
                    for pbi in range(12):
                        sl = PF[pbi % 2]
                        sl.pre_write(w, "tensor")
                        for db in range(NDB):
                            c0 = xcol(db, gbase + pbi * 128)
                            ins = te and te.matmul(sl.ap[:, 0:256], xnt16[:, c0:c0 + 128],
                                                   wsl[sq][:, db * 256:(db + 1) * 256],
                                                   start=(db == 0), stop=(db == 7))
                            if db == 7:
                                sl.set_ready(w, "pe", ins)
                        sl.pre_read(w, "vector")
                        o0 = pbi * 1024 + qtr * 256
                        ins = ve and ve.tensor_tensor(out=v16[:, o0:o0 + 256], in0=sl.ap[:, 0:256],
                                                      in1=bvbc[:, qtr * 256:(qtr + 1) * 256],
                                                      op=ALU.add)
                        sl.set_freed(w, "dve", ins)
                    wslot[sq].freed = ("pe", w.c["pe"])
                for hp in range(4):
                    wslot[0].pre_write(w, "gpsimd")
                    tok_q = w.dma("wdm", gp and gp.dma_start(
                        out=wsl[0].rearrange("p (b n) -> p b n", b=NDB),
                        in_=wq.rearrange("(b p) n -> p b n", p=128)[:, :, hp * 256:(hp + 1) * 256]))
                    wslot[1].pre_write(w, "gpsimd")
                    tok_k = w.dma("wdm", gp and gp.dma_start(
                        out=wsl[1].rearrange("p (b n) -> p b n", b=NDB),
                        in_=wk.rearrange("(b p) n -> p b n", p=128)[:, :, hp * 256:(hp + 1) * 256]))
                    emit_convert(1)
                    w.wait_tok("tensor", tok_q)
                    for h2 in range(2):
                        for nt in range(2):
                            sl = PF[(h2 * 2 + nt) % 2]
                            sl.pre_write(w, "tensor")
                            for db in range(NDB):
                                c0 = xcol(db, HALO + cbase + nt * 512)
                                ins = te and te.matmul(
                                    sl.ap,
                                    wsl[0][:, db * 256 + h2 * 128:db * 256 + h2 * 128 + 128],
                                    xnt16[:, c0:c0 + 512],
                                    start=(db == 0), stop=(db == 7))
                                if db == 7:
                                    sl.set_ready(w, "pe", ins)
                            sl.pre_read(w, "scalar")
                            o0 = h2 * 1024 + nt * 512
                            ins = se and se.activation(out=q16[:, o0:o0 + 512], in_=sl.ap,
                                                       func=AF.Identity,
                                                       bias=bq[:, hp * 2 + h2:hp * 2 + h2 + 1])
                            sl.set_freed(w, "act", ins)
                    w.wait_tok("tensor", tok_k)
                    for h2 in range(2):
                        for nt in range(3):
                            sl = PF[(h2 * 3 + nt) % 2]
                            sl.pre_write(w, "tensor")
                            for db in range(NDB):
                                c0 = xcol(db, gbase + nt * 512)
                                ins = te and te.matmul(
                                    sl.ap,
                                    wsl[1][:, db * 256 + h2 * 128:db * 256 + h2 * 128 + 128],
                                    xnt16[:, c0:c0 + 512],
                                    start=(db == 0), stop=(db == 7))
                                if db == 7:
                                    sl.set_ready(w, "pe", ins)
                            sl.pre_read(w, "scalar")
                            o0 = h2 * 1536 + nt * 512
                            ins = se and se.activation(out=kt16[:, o0:o0 + 512], in_=sl.ap,
                                                       func=AF.Identity,
                                                       bias=bk[:, hp * 2 + h2:hp * 2 + h2 + 1])
                            sl.set_freed(w, "act", ins)
                    wslot[0].freed = ("pe", w.c["pe"])
                    wslot[1].freed = ("pe", w.c["pe"])
                    qk_act = w.c["act"]
                    for cl in range(4):
                        ch = g * 4 + cl
                        msk = maske0 if ch == 0 else (maske7 if ch == 7 else maskb)
                        for h2 in range(2):
                            h = hp * 2 + h2
                            it = att_it[0]
                            att_it[0] += 1
                            osl = PF[6] if it % 2 == 0 else PF[0]
                            cur = dict(h=h, cl=cl, osl=osl, d_px=[0, 0])
                            osl.pre_write(w, "tensor")
                            scp = [(PF[2], PF[4]), (PF[3], PF[5])]
                            for qh in range(2):
                                scA, scB = scp[qh]
                                scA.pre_write(w, "tensor")
                                scB.pre_write(w, "tensor")
                                w.wait("tensor", "act", qk_act)
                                q0 = h2 * 1024 + cl * 256 + qh * 128
                                qap = q16[:, q0:q0 + 128]
                                for kc in range(3):
                                    dst = scA.ap[:, kc * 256:kc * 256 + 256] if kc < 2 \
                                        else scB.ap[:, 0:256]
                                    k0 = h2 * 1536 + cl * 256 + kc * 256
                                    ins = te and te.matmul(dst, qap, kt16[:, k0:k0 + 256],
                                                           start=True, stop=True)
                                    if kc == 1:
                                        scA.set_ready(w, "pe", ins)
                                    elif kc == 2:
                                        scB.set_ready(w, "pe", ins)
                            if att_prev[0] is not None:
                                emit_tr_o(att_prev[0])
                            for qh in range(2):
                                scA, scB = scp[qh]
                                c_nm = 8 + 3 * qh
                                scA.pre_read(w, "vector")
                                scB.pre_read(w, "vector")
                                ia = ve and ve.scalar_tensor_tensor(
                                    out=pn[qh][:, 0:512], in0=scA.ap, scalar=SCALE,
                                    in1=msk[:, qh * 768:qh * 768 + 512],
                                    op0=ALU.mult, op1=ALU.add)
                                scA.set_freed(w, "dve", ia)
                                ib = ve and ve.scalar_tensor_tensor(
                                    out=pn[qh][:, 512:768], in0=scB.ap[:, 0:256], scalar=SCALE,
                                    in1=msk[:, qh * 768 + 512:qh * 768 + 768],
                                    op0=ALU.mult, op1=ALU.add)
                                scB.set_freed(w, "dve", ib)
                                d_nm = w.inc("dve", ve and ve.reduce_max(
                                    out=small[:, c_nm:c_nm + 1], in_=pn[qh], axis=AX, negate=True))
                                w.wait("scalar", "dve", d_nm)
                                w.inc("act", se and se.activation(
                                    out=pn[qh], in_=pn[qh], func=AF.Exp,
                                    bias=small[:, c_nm:c_nm + 1],
                                    accum_out=small[:, c_nm + 1:c_nm + 2]))
                            for qh in range(2):
                                c_nm = 8 + 3 * qh
                                w.wait("vector", "act")
                                w.inc("dve", ve and ve.reciprocal(
                                    out=small[:, c_nm + 2:c_nm + 3], in_=small[:, c_nm + 1:c_nm + 2]))
                                w.wait("vector", "pe", pexp_free[qh])
                                cur["d_px"][qh] = w.inc("dve", ve and ve.tensor_scalar_mul(
                                    pexp[qh], pn[qh], small[:, c_nm + 2:c_nm + 3]))
                            att_prev[0] = cur
                if att_prev[0] is not None:
                    emit_tr_o(att_prev[0])
                    att_prev[0] = None
                w.wait("tensor", "act")
                for dob in range(NDB):
                    sw = dob % 2
                    wslot[sw].pre_write(w, "gpsimd")
                    tok_o = w.dma("wdm", gp and gp.dma_start(
                        out=wsl[sw][:, 0:1024].rearrange("p (b n) -> p b n", b=NDB),
                        in_=wo.rearrange("(b p) n -> p b n", p=128)[:, :, dob * 128:(dob + 1) * 128]))
                    w.wait_tok("tensor", tok_o)
                    for tb2 in range(2):
                        sl = PF[2 + tb2]
                        sl.pre_write(w, "tensor")
                        for dib in range(NDB):
                            ins = te and te.matmul(
                                sl.ap,
                                wsl[sw][:, dib * 128:(dib + 1) * 128],
                                otg[:, dib * 1024 + tb2 * 512:dib * 1024 + tb2 * 512 + 512],
                                start=(dib == 0), stop=(dib == 7))
                            if dib == 7:
                                sl.set_ready(w, "pe", ins)
                        sl.pre_read(w, "vector")
                        dst = bufr[:, rcol(dob, cbase + tb2 * 512):rcol(dob, cbase + tb2 * 512) + 512]
                        ins = ve and ve.scalar_tensor_tensor(out=dst, in0=sl.ap,
                                                             scalar=bo[:, dob:dob + 1],
                                                             in1=dst, op0=ALU.add, op1=ALU.add)
                        sl.set_freed(w, "dve", ins)
                    wslot[sw].freed = ("pe", w.c["pe"])

            if "aT" in dbg:
                w.wait("sync", "dve")
                w.dma("out", sy and sy.dma_start(out=dbg["aT"], in_=bufr))
                w.wait_all("vector", "out")

            aT_dve = w.c["dve"]
            w.wait("scalar", "dve", aT_dve)
            w.wait("tensor", "dve", aT_dve)
            ltaslot = [Slot(), Slot()]
            for tb4 in range(4):
                PF[2].pre_write(w, "tensor")
                for db in range(NDB):
                    src = bufr[:, rcol(db, tb4 * 512):rcol(db, tb4 * 512) + 512]
                    i1 = te and te.matmul(pf[2][0:1, :], onesc32, src,
                                          start=(db == 0), stop=(db == 7))
                    p1 = w.inc("pe", i1)
                    if db == 7:
                        PF[2].ready = ("pe", p1)
                PF[3].pre_write(w, "tensor")
                for db in range(NDB):
                    sidx = db % 2
                    src = bufr[:, rcol(db, tb4 * 512):rcol(db, tb4 * 512) + 512]
                    ltaslot[sidx].pre_write(w, "scalar")
                    a_s = w.inc("act", se and se.activation(out=lta[sidx], in_=src, func=AF.Square))
                    w.wait("tensor", "act", a_s)
                    i2 = te and te.matmul(pf[3][0:1, :], onesc32, lta[sidx],
                                          start=(db == 0), stop=(db == 7))
                    p2 = w.inc("pe", i2)
                    ltaslot[sidx].freed = ("pe", p2)
                    if db == 7:
                        PF[3].ready = ("pe", p2)
                PF[2].pre_read(w, "vector")
                ins = ve and ve.tensor_copy(out=lsum[0:1, tb4 * 512:(tb4 + 1) * 512], in_=pf[2][0:1, :])
                PF[2].set_freed(w, "dve", ins)
                PF[3].pre_read(w, "vector")
                ins = ve and ve.tensor_copy(out=lsq[0:1, tb4 * 512:(tb4 + 1) * 512], in_=pf[3][0:1, :])
                PF[3].set_freed(w, "dve", ins)
            w.inc("dve", ve and ve.tensor_scalar_mul(lsum, lsum, 1.0 / D))
            w.inc("dve", ve and ve.tensor_mul(ltmp, lsum, lsum))
            w.inc("dve", ve and ve.tensor_scalar_mul(lsq, lsq, 1.0 / D))
            d_v = w.inc("dve", ve and ve.tensor_sub(lsq, lsq, ltmp))
            w.wait("scalar", "dve", d_v)
            w.inc("act", se and se.activation(out=ltmp, in_=lsq, func=AF.Ln, bias=small[0:1, 15:16]))
            a_r = w.inc("act", se and se.activation(out=lsq, in_=ltmp, func=AF.Exp, scale=-0.5))
            for tb4 in range(4):
                PF[0].pre_write(w, "tensor")
                w.wait("tensor", "act", a_r)
                i0 = te and te.matmul(pf[0], ones32, lsum[0:1, tb4 * 512:(tb4 + 1) * 512],
                                      start=True, stop=True)
                PF[0].set_ready(w, "pe", i0)
                PF[1].pre_write(w, "tensor")
                i1 = te and te.matmul(pf[1], ones32, lsq[0:1, tb4 * 512:(tb4 + 1) * 512],
                                      start=True, stop=True)
                PF[1].set_ready(w, "pe", i1)
                PF[0].pre_read(w, "vector")
                PF[1].pre_read(w, "vector")
                for db in range(NDB):
                    tA = lta[db % 2]
                    ltaslot[db % 2].pre_write(w, "vector")
                    src = bufr[:, rcol(db, tb4 * 512):rcol(db, tb4 * 512) + 512]
                    w.inc("dve", ve and ve.tensor_sub(tA, src, pf[0]))
                    w.inc("dve", ve and ve.tensor_mul(tA, tA, pf[1]))
                    d_b = w.inc("dve", ve and ve.tensor_scalar(out=src, in0=tA,
                                                               scalar1=gamma[:, db:db + 1],
                                                               scalar2=beta[:, db:db + 1],
                                                               op0=ALU.mult, op1=ALU.add))
                    w.wait("scalar", "dve", d_b)
                    a_b = w.inc("act", se and se.copy(
                        out=bt16[:, rcol(db, tb4 * 512):rcol(db, tb4 * 512) + 512], in_=src))
                    ltaslot[db % 2].freed = ("dve", d_b)
                PF[0].freed = ("dve", w.c["dve"])
                PF[1].freed = ("dve", w.c["dve"])
            bT_act = w.c["act"]
            if "bT" in dbg:
                w.wait("sync", "dve")
                w.dma("out", sy and sy.dma_start(out=dbg["bT"], in_=bufr))
                w.wait_all("vector", "out")
            if "bt16" in dbg:
                w.wait("sync", "act", bT_act)
                w.dma("out", sy and sy.dma_start(out=dbg["bt16"], in_=bt16))

            w.wait("sync", "pe")
            w.wait("sync", "dve")
            emit_convert(100)
            if cv_state["sttok"][0] is not None:
                w.wait_tok("sync", cv_state["sttok"][0])
            if cv_state["sttok"][1] is not None:
                w.wait_tok("sync", cv_state["sttok"][1])
            w.wait("gpsimd", "pe")
            w.wait("gpsimd", "dve")
            w1slot = [Slot(), Slot()]
            w2slot = [Slot(), Slot()]
            h1slot = Slot()
            for tt in range(4):
                h1slot.pre_write(w, "scalar")
                for do4 in range(8):
                    s = do4 % 2
                    w1slot[s].pre_write(w, "sync")
                    tok_w1 = w.dma("xin", sy and sy.dma_start(
                        out=w1sl[s].rearrange("p (b n) -> p b n", b=NDB),
                        in_=w1b.rearrange("(a b) -> a b", b=DFF)
                        .rearrange("(b p) n -> p b n", p=128)[:, :, do4 * 512:(do4 + 1) * 512]))
                    w.wait_tok("tensor", tok_w1)
                    w.wait("tensor", "act", bT_act)
                    for k in range(4):
                        do = do4 * 4 + k
                        sl = PF[do % 4]
                        sl.pre_write(w, "tensor")
                        for db in range(NDB):
                            ins = te and te.matmul(
                                sl.ap, w1sl[s][:, db * 512 + k * 128:db * 512 + k * 128 + 128],
                                bt16[:, rcol(db, tt * 512):rcol(db, tt * 512) + 512],
                                start=(db == 0), stop=(db == 7))
                            if db == 7:
                                sl.set_ready(w, "pe", ins)
                        sl.pre_read(w, "scalar")
                        ins = se and se.activation(out=h1[:, do * 512:(do + 1) * 512], in_=sl.ap,
                                                   func=AF.Gelu, bias=b1[:, do:do + 1])
                        sl.set_freed(w, "act", ins)
                    w1slot[s].freed = ("pe", w.c["pe"])
                h1_act = w.c["act"]
                for do2 in range(NDB):
                    sl = PF[4 + do2 % 2]
                    sl.pre_write(w, "tensor")
                    for half in range(2):
                        w2slot[half].pre_write(w, "gpsimd")
                        tok_w2 = w.dma("wdm", gp and gp.dma_start(
                            out=w2sl[half].rearrange("p (b n) -> p b n", b=16),
                            in_=w2b.rearrange("(a b) -> a b", b=D)
                            .rearrange("(b p) n -> p b n", p=128)[:, half * 16:(half + 1) * 16,
                                                                 do2 * 128:(do2 + 1) * 128]))
                        w.wait_tok("tensor", tok_w2)
                        w.wait("tensor", "act", h1_act)
                        for k in range(16):
                            dib = half * 16 + k
                            ins = te and te.matmul(sl.ap, w2sl[half][:, k * 128:(k + 1) * 128],
                                                   h1[:, dib * 512:(dib + 1) * 512],
                                                   start=(dib == 0), stop=(dib == 31))
                            if dib == 31:
                                sl.set_ready(w, "pe", ins)
                    for half in range(2):
                        w2slot[half].freed = ("pe", w.c["pe"])
                    sl.pre_read(w, "vector")
                    dst = bufr[:, rcol(do2, tt * 512):rcol(do2, tt * 512) + 512]
                    ins = ve and ve.scalar_tensor_tensor(out=dst, in0=sl.ap,
                                                         scalar=b2[:, do2:do2 + 1],
                                                         in1=dst, op0=ALU.add, op1=ALU.add)
                    sl.set_freed(w, "dve", ins)
                h1slot.freed = ("pe", w.c["pe"])
            if "ffn" in dbg:
                w.wait("sync", "dve")
                w.dma("out", sy and sy.dma_start(out=dbg["ffn"], in_=bufr))
                w.wait_all("vector", "out")
            if "bt16b" in dbg:
                w.wait("sync", "dve")
                w.dma("out", sy and sy.dma_start(out=dbg["bt16b"], in_=bt16))
                w.wait_all("vector", "out")
                w.wait_all("scalar", "out")
                w.wait_all("gpsimd", "out")
            if "h1f" in dbg:
                w.wait("sync", "act")
                w.dma("out", sy and sy.dma_start(out=dbg["h1f"], in_=h1))
                w.wait_all("gpsimd", "out")
                w.wait_all("scalar", "out")

            w.wait("gpsimd", "pe")
            w.wait("gpsimd", "dve")
            if cv_state["sttok"][0] is not None:
                w.wait_tok("gpsimd", cv_state["sttok"][0])
            if cv_state["sttok"][1] is not None:
                w.wait_tok("gpsimd", cv_state["sttok"][1])
            upslot = [Slot(), Slot()]
            gpslot = [Slot(), Slot()]
            dwslot = [Slot(), Slot()]
            tuslot = [Slot(), Slot()]
            tdslot = [Slot(), Slot()]
            h16slot = Slot()
            combt_slot = Slot()
            for th in range(2):
                tbase = th * 1024
                combt_slot.pre_write(w, "vector")
                for tb8 in range(8):
                    tbg = th * 8 + tb8
                    sl = PF[6]
                    sl.pre_write(w, "tensor")
                    w.wait("tensor", "dve")
                    ins = te and te.transpose(pf[6][0:8, 0:128], comb[:, tbg * 8:(tbg + 1) * 8], id32)
                    sl.set_ready(w, "pe", ins)
                    sl.pre_read(w, "vector")
                    ins = ve and ve.tensor_copy(out=combt[0:8, tb8 * 128:(tb8 + 1) * 128],
                                                in_=pf[6][0:8, 0:128])
                    sl.set_freed(w, "dve", ins)
                combt_dve = w.c["dve"]
                for e in range(E):
                    for tb2 in range(2):
                        sl = PF[6]
                        sl.pre_write(w, "tensor")
                        w.wait("tensor", "dve", combt_dve)
                        ins = te and te.matmul(pf[6], sel[:, e * 128:(e + 1) * 128],
                                               combt[0:8, tb2 * 512:(tb2 + 1) * 512],
                                               start=True, stop=True)
                        sl.set_ready(w, "pe", ins)
                        sl.pre_read(w, "vector")
                        ins = ve and ve.tensor_copy(out=cbuf[:, tb2 * 512:(tb2 + 1) * 512], in_=pf[6])
                        sl.set_freed(w, "dve", ins)
                    cb_dve = w.c["dve"]
                    h16slot.pre_write(w, "scalar")
                    for fb2 in range(NFB // 2):
                        s = fb2 % 2
                        upslot[s].pre_write(w, "gpsimd")
                        tok_u = w.dma("wdm", gp and gp.dma_start(
                            out=upw[s].rearrange("p (b n) -> p b n", b=NDB),
                            in_=upt[e].rearrange("(b p) f -> p b f", p=128)[:, :, fb2 * 256:(fb2 + 1) * 256]))
                        gpslot[s].pre_write(w, "gpsimd")
                        tok_f = w.dma("wdm", gp and gp.dma_start(
                            out=gpw[s].rearrange("p (b n) -> p b n", b=NDB),
                            in_=gpt[e].rearrange("(b p) f -> p b f", p=128)[:, :, fb2 * 256:(fb2 + 1) * 256]))
                        w.wait_tok("tensor", tok_u)
                        w.wait_tok("tensor", tok_f)
                        for k in range(2):
                            fb = fb2 * 2 + k
                            for tb2 in range(2):
                                r3 = (fb * 2 + tb2) % 3
                                slu = PF[(0, 1, 4)[r3]]
                                slg = PF[(2, 3, 5)[r3]]
                                slu.pre_write(w, "tensor")
                                slg.pre_write(w, "tensor")
                                for db in range(NDB):
                                    c0 = xcol(db, HALO + tbase + tb2 * 512)
                                    ins = te and te.matmul(
                                        slu.ap, upw[s][:, db * 256 + k * 128:db * 256 + k * 128 + 128],
                                        xnt16[:, c0:c0 + 512],
                                        start=(db == 0), stop=(db == 7))
                                    if db == 7:
                                        slu.set_ready(w, "pe", ins)
                                for db in range(NDB):
                                    c0 = xcol(db, HALO + tbase + tb2 * 512)
                                    ins = te and te.matmul(
                                        slg.ap, gpw[s][:, db * 256 + k * 128:db * 256 + k * 128 + 128],
                                        xnt16[:, c0:c0 + 512],
                                        start=(db == 0), stop=(db == 7))
                                    if db == 7:
                                        slg.set_ready(w, "pe", ins)
                                slu.pre_read(w, "scalar")
                                tuslot[tb2].pre_write(w, "scalar")
                                ins = se and se.activation(out=tu[tb2], in_=slu.ap, func=AF.Identity,
                                                           bias=upb[:, e * NFB + fb:e * NFB + fb + 1])
                                slu.set_freed(w, "act", ins)
                                a_u = w.c["act"]
                                slg.pre_read(w, "vector")
                                w.wait("vector", "act", a_u)
                                d_m = w.inc("dve", ve and ve.scalar_tensor_tensor(
                                    out=tu[tb2], in0=slg.ap,
                                    scalar=gpb[:, e * NFB + fb:e * NFB + fb + 1],
                                    in1=tu[tb2], op0=ALU.add, op1=ALU.mult))
                                slg.freed = ("dve", d_m)
                                w.wait("scalar", "dve", d_m)
                                o0 = fb * 1024 + tb2 * 512
                                a_h = w.inc("act", se and se.activation(out=h16[:, o0:o0 + 512],
                                                                        in_=tu[tb2], func=AF.Silu))
                                tuslot[tb2].freed = ("act", a_h)
                        upslot[s].freed = ("pe", w.c["pe"])
                        gpslot[s].freed = ("pe", w.c["pe"])
                    h_act = w.c["act"]
                    w.wait("vector", "dve", cb_dve)
                    for do in range(NDB):
                        s = do % 2
                        dwslot[s].pre_write(w, "gpsimd")
                        tok_d = w.dma("wdm", gp and gp.dma_start(
                            out=dwts[s].rearrange("p (b n) -> p b n", b=NFB),
                            in_=dwt[e].rearrange("(b p) n -> p b n", p=128)[:, :, do * 128:(do + 1) * 128]))
                        w.wait_tok("tensor", tok_d)
                        w.wait("tensor", "act", h_act)
                        for tb2 in range(2):
                            sl = PF[4 + tb2]
                            sl.pre_write(w, "tensor")
                            for fb in range(NFB):
                                o0 = fb * 1024 + tb2 * 512
                                ins = te and te.matmul(sl.ap, dwts[s][:, fb * 128:(fb + 1) * 128],
                                                       h16[:, o0:o0 + 512],
                                                       start=(fb == 0), stop=(fb == NFB - 1))
                                if fb == NFB - 1:
                                    sl.set_ready(w, "pe", ins)
                            sl.pre_read(w, "vector")
                            tdslot[tb2].pre_write(w, "vector")
                            ins = ve and ve.scalar_tensor_tensor(
                                out=td[tb2], in0=sl.ap,
                                scalar=dnb[:, e * 8 + do:e * 8 + do + 1],
                                in1=cbuf[:, tb2 * 512:(tb2 + 1) * 512],
                                op0=ALU.add, op1=ALU.mult)
                            sl.set_freed(w, "dve", ins)
                            dst = bufr[:, rcol(do, tbase + tb2 * 512):rcol(do, tbase + tb2 * 512) + 512]
                            d_add = w.inc("dve", ve and ve.tensor_add(dst, dst, td[tb2]))
                            tdslot[tb2].freed = ("dve", d_add)
                        dwslot[s].freed = ("pe", w.c["pe"])
                    h16slot.freed = ("pe", w.c["pe"])
                combt_slot.freed = ("pe", w.c["pe"])
            if "moe" in dbg:
                w.wait("sync", "dve")
                w.dma("out", sy and sy.dma_start(out=dbg["moe"], in_=bufr))
            if "h16" in dbg:
                w.wait("sync", "act")
                w.dma("out", sy and sy.dma_start(out=dbg["h16"], in_=h16))

            res_dve = w.c["dve"]
            w.wait("sync", "dve", res_dve)
            fslot = [Slot(), Slot()]
            oslot = [Slot(), Slot()]
            for i in range(16):
                s = i % 2
                fslot[s].pre_write(w, "sync")
                tok_xf = w.dma("xin", sy and sy.dma_start(out=xt[s], in_=xc[(2 + i) * 128:(3 + i) * 128, :]))
                for half in range(2):
                    sl = PF[half]
                    sl.pre_write(w, "tensor")
                    w.wait("tensor", "dve", res_dve)
                    for k in range(4):
                        db = half * 4 + k
                        c0 = rcol(db, i * 128)
                        ins = te and te.transpose(pf[half][:, k * 128:(k + 1) * 128],
                                                  bufr[:, c0:c0 + 128], id32)
                        if k == 3:
                            sl.set_ready(w, "pe", ins)
                oslot[s].pre_write(w, "vector")
                w.wait_tok("vector", tok_xf)
                for half in range(2):
                    sl = PF[half]
                    sl.pre_read(w, "vector")
                    ins = ve and ve.tensor_add(fot[s][:, half * 512:(half + 1) * 512],
                                               pf[half][:, 0:512],
                                               xt[s][:, half * 512:(half + 1) * 512])
                    sl.set_freed(w, "dve", ins)
                d_out = w.c["dve"]
                fslot[s].freed = ("dve", d_out)
                w.wait("sync", "dve", d_out)
                ins = sy and sy.dma_start(out=y[i * 128:(i + 1) * 128, :], in_=fot[s])
                oslot[s].freed = w.dma("out", ins)
            w.wait_all("sync", "out")

        engs = {}

        @block.tensor
        def _(eng):
            engs["tensor"] = eng
            flow(Wk("tensor", engs, semh))

        @block.vector
        def _(eng):
            engs["vector"] = eng
            flow(Wk("vector", engs, semh))

        @block.scalar
        def _(eng):
            engs["scalar"] = eng
            flow(Wk("scalar", engs, semh))

        @block.gpsimd
        def _(eng):
            engs["gpsimd"] = eng
            flow(Wk("gpsimd", engs, semh))

        @block.sync
        def _(eng):
            engs["sync"] = eng
            flow(Wk("sync", engs, semh))

    return nc


def _prep_inputs(L0_attn_x, L0_attn_Wq, L0_attn_bq, L0_attn_Wk, L0_attn_bk,
                 L0_attn_Wv, L0_attn_bv, L0_attn_Wo, L0_attn_bo,
                 L0_attn_gamma, L0_attn_beta, L0_attn_W1, L0_attn_b1,
                 L0_attn_W2, L0_attn_b2, L0_ffn_gate_w, L0_ffn_gate_b,
                 L0_ffn_up_w, L0_ffn_up_b, L0_ffn_gp_w, L0_ffn_gp_b,
                 L0_ffn_down_w, L0_ffn_down_b):
    f = lambda a: np.ascontiguousarray(np.asarray(a, dtype=np.float32))
    x = f(L0_attn_x).reshape(16384, D)
    xpad = np.zeros((16384 + 2 * HALO, D), np.float32)
    xpad[HALO:HALO + 16384] = x

    colb = lambda v, nb: np.ascontiguousarray(f(v).reshape(nb, 128).T)
    wq_, wk_, wv_, wo_ = f(L0_attn_Wq), f(L0_attn_Wk), f(L0_attn_Wv), f(L0_attn_Wo)
    w1_, w2_ = f(L0_attn_W1), f(L0_attn_W2)
    upt = np.zeros((E, D, FPAD), np.float32)
    upt[:, :, :F] = np.transpose(f(L0_ffn_up_w), (0, 2, 1))
    gpt = np.zeros((E, D, FPAD), np.float32)
    gpt[:, :, :F] = np.transpose(f(L0_ffn_gp_w), (0, 2, 1))
    dwt = np.zeros((E, FPAD, D), np.float32)
    dwt[:, :F, :] = np.transpose(f(L0_ffn_down_w), (0, 2, 1))
    upb = np.zeros((E, FPAD), np.float32); upb[:, :F] = f(L0_ffn_up_b)
    gpb = np.zeros((E, FPAD), np.float32); gpb[:, :F] = f(L0_ffn_gp_b)
    upb_re = np.ascontiguousarray(np.transpose(upb.reshape(E, NFB, 128), (2, 0, 1)).reshape(128, E * NFB))
    gpb_re = np.ascontiguousarray(np.transpose(gpb.reshape(E, NFB, 128), (2, 0, 1)).reshape(128, E * NFB))
    dnb_re = np.ascontiguousarray(np.transpose(f(L0_ffn_down_b).reshape(E, 8, 128), (2, 0, 1)).reshape(128, E * 8))
    gw = f(L0_ffn_gate_w)
    gwt_re = np.ascontiguousarray(np.transpose(gw.T.reshape(8, 128, E), (1, 0, 2)).reshape(128, 8 * E))
    gbb = np.ascontiguousarray(np.tile(f(L0_ffn_gate_b)[None, :], (128, 1)))
    bvbc = np.ascontiguousarray(np.tile(f(L0_attn_bv)[None, :], (128, 1)))
    selm = np.zeros((8, 1024), np.float32)
    for e in range(8):
        selm[e, e * 128:(e + 1) * 128] = 1.0

    qi = np.arange(256)[:, None]
    kj = np.arange(768)[None, :]
    band = (kj >= qi) & (kj <= qi + 512)
    mb = np.where(band, 0.0, NEG).astype(np.float32)
    m_int = np.concatenate([mb[0:128], mb[128:256]], axis=1)
    kill_first = np.where(np.arange(768) < 256, NEG, 0.0).astype(np.float32)
    kill_last = np.where(np.arange(768) >= 512, NEG, 0.0).astype(np.float32)

    per_core = []
    for c in range(8):
        m_e0 = m_int.copy()
        m_e7 = m_int.copy()
        if c == 0:
            for qh in range(2):
                m_e0[:, qh * 768:(qh + 1) * 768] = np.minimum(
                    m_e0[:, qh * 768:(qh + 1) * 768], kill_first[None, :])
        if c == 7:
            for qh in range(2):
                m_e7[:, qh * 768:(qh + 1) * 768] = np.minimum(
                    m_e7[:, qh * 768:(qh + 1) * 768], kill_last[None, :])
        per_core.append(dict(
            xc=np.ascontiguousarray(xpad[c * T:c * T + TH]),
            wq=wq_, wk=wk_, wv=wv_, wo=wo_, w1=w1_, w2=w2_,
            upt=upt, gpt=gpt, dwt=dwt,
            bq=colb(L0_attn_bq, 8), bk=colb(L0_attn_bk, 8),
            bo=colb(L0_attn_bo, 8), b2=colb(L0_attn_b2, 8),
            gamma=colb(L0_attn_gamma, 8), beta=colb(L0_attn_beta, 8),
            b1=colb(L0_attn_b1, 32),
            gwt=gwt_re, gbb=gbb, upb=upb_re, gpb=gpb_re, dnb=dnb_re,
            bvbc=bvbc, sel=selm,
            maskb=m_int.astype(BFNP), maske0=m_e0.astype(BFNP),
            maske7=m_e7.astype(BFNP),
        ))
    return per_core


def kernel(**inputs):
    in_maps = _prep_inputs(**inputs)
    nc = build_program()
    res = run_bass_kernel_spmd(nc, in_maps, core_ids=list(range(8)))
    out = np.concatenate([np.asarray(r["y"]) for r in res.results], axis=0)
    return out.reshape(1, 16384, D).astype(np.float32)


# version 10
# speedup vs baseline: 1.1915x; 1.1456x over previous
import numpy as np
import ml_dtypes
import concourse.bass as bass
import concourse.mybir as mybir
from concourse.bass_utils import run_bass_kernel_spmd

F32 = mybir.dt.float32
BF16 = mybir.dt.bfloat16
U8 = mybir.dt.uint8
AF = mybir.ActivationFunctionType
ALU = mybir.AluOpType
AX = mybir.AxisListType.X
BFNP = ml_dtypes.bfloat16

D = 1024
NDB = 8
T = 2048
HALO = 256
TH = T + 2 * HALO
NTILES = TH // 128
H = 8
E = 8
F = 3264
FPAD = 3328
NFB = FPAD // 128
DFF = 4096
SCALE = float(1.0 / np.sqrt(128.0))
NEG = -1.0e30
EPS = 1.0e-5

P_ID32 = 0
P_ID16 = 512
P_ONES32 = 768
P_ONESC32 = 1280
P_BQ = 1312
P_BK = 1344
P_BO = 1376
P_B2 = 1408
P_GAMMA = 1440
P_BETA = 1472
P_B1 = 1504
P_GWT = 1632
P_GBB = 1888
P_UPB = 1920
P_GPB = 2752
P_DNB = 3584
P_COMB = 3840
P_SMALL = 4352
P_SEL = 4416
P_XNT16 = 8512
P_BUFR = 49472
S = 115008
A_XT0 = S + 0
A_XT1 = S + 4096
A_XN0 = S + 8192
A_XN1 = S + 12288
A_XN16_0 = S + 16384
A_XN16_1 = S + 18432
A_G8 = S + 20480
A_P8 = A_G8 + 32
A_PRB = A_G8 + 64
A_TOP8 = A_G8 + 96
A_MSK = A_G8 + 128
A_CW = A_G8 + 160
A_ST6 = A_G8 + 192
FI_OT0 = S + 21504
FI_OT1 = S + 25600
AT_V16 = S + 0
AT_BVBC = S + 24576
AT_MB = S + 28672
AT_ME0 = S + 31744
AT_ME7 = S + 34816
AT_OTG = S + 37888
AT_KT = S + 54272
AT_Q16 = S + 60416
AT_WSL0 = S + 64512
AT_WSL1 = S + 68608
AT_PN0 = S + 72704
AT_PN1 = S + 75776
AT_PEXPA = S + 78848
AT_PEXPB = S + 80384
AT_PTA = S + 81920
AT_PTB = S + 83456
L_SUM = S + 0
L_SQ = S + 8192
L_TA = S + 20480
L_TMP = S + 70656
FF_W1SL = S + 0
FF_W2SL = S + 16384
FF_BT16 = S + 24576
FF_H1 = S + 57344
M_COMBT = S + 0
M_CB = S + 4096
M_TU = S + 6144
M_TD = S + 10240
M_UPW = S + 14336
M_GPW = S + 22528
M_DWT = S + 30720
M_H16 = S + 44032
ARENA = S + 97280

DMAQ = {"xin": 4, "wdm": 4, "out": 3, "cst": 1, "cvl": 2, "cvs": 2}
SEMS = tuple(f"{q}{j}" for q, n in DMAQ.items() for j in range(n)) + ("pe", "dve", "act", "gps")


class Wk:
    def __init__(self, which, engs, sems):
        self.which = which
        self.engs = engs
        self.sems = sems
        self.c = {k: 0 for k in sems}
        self.hw = {}
        self.qidx = {q: 0 for q in DMAQ}

    def dma(self, q, instr):
        i = self.qidx[q]
        self.qidx[q] = i + 1
        ns = DMAQ[q]
        sname = f"{q}{i % ns}"
        v = 16 * (i // ns + 1)
        self.c[sname] = v
        if instr is not None:
            instr.then_inc(self.sems[sname], 16)
        return (sname, v)

    def wait_tok(self, engname, tok):
        self.wait(engname, tok[0], tok[1])

    def wait_all(self, engname, q):
        ns = DMAQ[q]
        tot = self.qidx[q]
        for j in range(ns):
            uses = (tot - j + ns - 1) // ns
            if uses > 0:
                self.wait(engname, f"{q}{j}", 16 * uses)

    def eng(self, name):
        return self.engs[name] if self.which == name else None

    SELFW = {"dve": "vector", "act": "scalar", "gps": "gpsimd"}

    def inc(self, sem, instr, k=1):
        self.c[sem] += k
        if instr is not None:
            instr.then_inc(self.sems[sem], k)
        eng = self.SELFW.get(sem)
        if eng is not None and self.which == eng:
            v = self.c[sem]
            if self.hw.get(sem, -1) < v:
                self.hw[sem] = v
                self.engs[eng].wait_ge(self.sems[sem], v)
        return self.c[sem]

    def wait(self, engname, sem, val=None):
        v = self.c[sem] if val is None else val
        if v <= 0:
            return
        if self.which == engname:
            key = sem
            if self.hw.get(key, -1) >= v:
                return
            self.hw[key] = v
            self.engs[engname].wait_ge(self.sems[sem], v)


class Slot:
    def __init__(self, ap=None):
        self.ap = ap
        self.ready = None
        self.freed = None

    def pre_write(self, w, eng):
        if self.freed is not None:
            w.wait(eng, *self.freed)

    def set_ready(self, w, sem, instr, k=1):
        self.ready = (sem, w.inc(sem, instr, k))

    def pre_read(self, w, eng):
        if self.ready is not None:
            w.wait(eng, *self.ready)

    def set_freed(self, w, sem, instr, k=1):
        self.freed = (sem, w.inc(sem, instr, k))


def build_program(debug=()):
    nc = bass.Bass("TRN2", target_bir_lowering=False)

    def din(name, shape, dt=F32):
        return nc.dram_tensor(name, list(shape), dt, kind="ExternalInput").ap()

    xc = din("xc", [TH, D])
    wq = din("wq", [D, D]); wk = din("wk", [D, D])
    wv = din("wv", [D, D]); wo = din("wo", [D, D])
    w1 = din("w1", [D, DFF]); w2 = din("w2", [DFF, D])
    upt = din("upt", [E, D, FPAD]); gpt = din("gpt", [E, D, FPAD])
    dwt = din("dwt", [E, FPAD, D])
    bq_i = din("bq", [128, 8]); bk_i = din("bk", [128, 8])
    bo_i = din("bo", [128, 8]); b2_i = din("b2", [128, 8])
    gam_i = din("gamma", [128, 8]); bet_i = din("beta", [128, 8])
    b1_i = din("b1", [128, 32])
    gwt_i = din("gwt", [128, 64]); gbb_i = din("gbb", [128, 8])
    upb_i = din("upb", [128, E * NFB]); gpb_i = din("gpb", [128, E * NFB])
    dnb_i = din("dnb", [128, E * 8])
    bvbc_i = din("bvbc", [128, D])
    sel_i = din("sel", [8, 1024])
    mb_i = din("maskb", [128, 1536], BF16)
    me0_i = din("maske0", [128, 1536], BF16)
    me7_i = din("maske7", [128, 1536], BF16)

    w1b = nc.dram_tensor("w1b", [D * DFF], BF16, kind="Internal").ap()
    w2b = nc.dram_tensor("w2b", [DFF * D], BF16, kind="Internal").ap()
    y = nc.dram_tensor("y", [T, D], F32, kind="ExternalOutput").ap()
    dbg = {}
    for name in debug:
        if name == "xnt":
            dbg[name] = nc.dram_tensor("d_xnt", [128, NDB * TH], BF16, kind="ExternalOutput").ap()
        elif name in ("xnTf", "aT", "bT", "ffn", "moe"):
            dbg[name] = nc.dram_tensor("d_" + name, [128, NDB * T], F32, kind="ExternalOutput").ap()
        elif name == "comb":
            dbg[name] = nc.dram_tensor("d_comb", [128, 128], F32, kind="ExternalOutput").ap()
        elif name == "gate8":
            dbg[name] = nc.dram_tensor("d_gate8", [128, 48], F32, kind="ExternalOutput").ap()
        elif name in ("bt16", "bt16b"):
            dbg[name] = nc.dram_tensor("d_" + name, [128, NDB * T], BF16, kind="ExternalOutput").ap()
        elif name == "h1f":
            dbg[name] = nc.dram_tensor("d_h1f", [128, 32 * 512], BF16, kind="ExternalOutput").ap()
        elif name == "h16":
            dbg[name] = nc.dram_tensor("d_h16", [128, NFB * 1024], BF16, kind="ExternalOutput").ap()

    arena = nc.alloc_sbuf_tensor("arena", [128, ARENA], U8)
    arena_addr = nc.lookup_mloc(arena).addr

    def place(name, off, shape, dt):
        h = nc.virtual_sbuf_tensor(name, list(shape), dt)
        nc._tensor(h.name, list(shape), dt, type="SB")
        mloc = nc.lookup_mloc(h)
        mloc.allocated = True
        mloc.addr = arena_addr + off
        return h.ap()

    id32 = place("id32", P_ID32, [128, 128], F32)
    id16 = place("id16", P_ID16, [128, 128], BF16)
    ones32 = place("ones32", P_ONES32, [1, 128], F32)
    onesc32 = place("onesc32", P_ONESC32, [128, 1], F32)
    bq = place("bq_s", P_BQ, [128, 8], F32)
    bk = place("bk_s", P_BK, [128, 8], F32)
    bo = place("bo_s", P_BO, [128, 8], F32)
    b2 = place("b2_s", P_B2, [128, 8], F32)
    gamma = place("gamma_s", P_GAMMA, [128, 8], F32)
    beta = place("beta_s", P_BETA, [128, 8], F32)
    b1 = place("b1_s", P_B1, [128, 32], F32)
    gwt = place("gwt_s", P_GWT, [128, 64], F32)
    gbb = place("gbb_s", P_GBB, [128, 8], F32)
    upb = place("upb_s", P_UPB, [128, E * NFB], F32)
    gpb = place("gpb_s", P_GPB, [128, E * NFB], F32)
    dnb = place("dnb_s", P_DNB, [128, E * 8], F32)
    comb = place("comb_s", P_COMB, [128, 128], F32)
    small = place("small_s", P_SMALL, [128, 16], F32)
    sel = place("sel_s", P_SEL, [8, 1024], F32)
    xnt16 = place("xnt16", P_XNT16, [128, NDB * TH], BF16)
    bufr = place("bufr", P_BUFR, [128, NDB * T], F32)
    xt = [place("xt0", A_XT0, [128, D], F32), place("xt1", A_XT1, [128, D], F32)]
    xn32 = [place("xn0", A_XN0, [128, D], F32), place("xn1", A_XN1, [128, D], F32)]
    xn16 = [place("xn16_0", A_XN16_0, [128, D], BF16), place("xn16_1", A_XN16_1, [128, D], BF16)]
    g8 = place("g8_s", A_G8, [128, 8], F32)
    g8all = place("g8all", FI_OT0, [128, 128], F32)
    gsm = place("gsm", FI_OT0 + 512, [128, 64], F32)
    p8 = place("p8_s", A_P8, [128, 8], F32)
    prb = place("prb_s", A_PRB, [128, 8], F32)
    top8 = place("top8_s", A_TOP8, [128, 8], F32)
    mskg = place("mskg_s", A_MSK, [128, 8], F32)
    cw8 = place("cw8_s", A_CW, [128, 8], F32)
    st6 = place("st6_s", A_ST6, [128, 12], F32)
    fot = [place("fot0", FI_OT0, [128, D], F32), place("fot1", FI_OT1, [128, D], F32)]
    v16 = place("v16", AT_V16, [128, 12 * 1024], BF16)
    bvbc = place("bvbc_s", AT_BVBC, [128, D], F32)
    maskb = place("maskb_s", AT_MB, [128, 1536], BF16)
    maske0 = place("maske0_s", AT_ME0, [128, 1536], BF16)
    maske7 = place("maske7_s", AT_ME7, [128, 1536], BF16)
    otg = place("otg", AT_OTG, [128, H * 1024], BF16)
    kt16 = place("kt16", AT_KT, [128, 2 * 1536], BF16)
    q16 = place("q16", AT_Q16, [128, 2 * 1024], BF16)
    wsl = [place("wsl0", AT_WSL0, [128, 2048], BF16), place("wsl1", AT_WSL1, [128, 2048], BF16)]
    pn = [place("pn0", AT_PN0, [128, 768], F32), place("pn1", AT_PN1, [128, 768], F32)]
    pexp = [place("pexpA", AT_PEXPA, [128, 768], BF16), place("pexpB", AT_PEXPB, [128, 768], BF16)]
    pt16 = [place("ptA", AT_PTA, [128, 768], BF16), place("ptB", AT_PTB, [128, 768], BF16)]
    lsum = place("lsum", L_SUM, [1, 2048], F32)
    lsq = place("lsq", L_SQ, [1, 2048], F32)
    ltmp = place("ltmp", L_TMP, [1, 2048], F32)
    lta = [place("lta0", L_TA, [128, 512], F32), place("lta1", L_TA + 2048, [128, 512], F32)]
    w1sl = [place("w1sl0", FF_W1SL, [128, 4096], BF16), place("w1sl1", FF_W1SL + 8192, [128, 4096], BF16)]
    w2sl = [place("w2sl0", FF_W2SL, [128, 2048], BF16), place("w2sl1", FF_W2SL + 4096, [128, 2048], BF16)]
    bt16 = place("bt16", FF_BT16, [128, NDB * T], BF16)
    h1 = place("h1", FF_H1, [128, 32 * 512], BF16)
    combt = place("combt", M_COMBT, [8, 1024], F32)
    cbuf = place("cbuf", M_CB, [128, 1024], BF16)
    tu = [place("tu0", M_TU, [128, 512], F32), place("tu1", M_TU + 2048, [128, 512], F32)]
    td = [place("td0", M_TD, [128, 512], F32), place("td1", M_TD + 2048, [128, 512], F32)]
    upw = [place("upw0", M_UPW, [128, 2048], BF16), place("upw1", M_UPW + 4096, [128, 2048], BF16)]
    cvb = [place("cvb0", S + 84992, [128, 2048], BF16), place("cvb1", S + 89088, [128, 2048], BF16)]
    gpw = [place("gpw0", M_GPW, [128, 2048], BF16), place("gpw1", M_GPW + 4096, [128, 2048], BF16)]
    dwts = [place("dwt0", M_DWT, [128, 3328], BF16), place("dwt1", M_DWT + 6656, [128, 3328], BF16)]
    h16 = place("h16", M_H16, [128, NFB * 1024], BF16)

    pf = [nc.alloc_psum_tensor(f"pf{i}", [128, 512], F32).ap() for i in range(7)]
    pb = nc.alloc_psum_tensor("pb", [128, 1024], BF16).ap()

    from contextlib import ExitStack
    with ExitStack() as ctx:
        semh = {s: ctx.enter_context(nc.semaphore(s)) for s in SEMS}
        block = ctx.enter_context(nc.Block())

        def flow(w: Wk):
            te = w.eng("tensor"); ve = w.eng("vector"); se = w.eng("scalar")
            gp = w.eng("gpsimd"); sy = w.eng("sync")
            PF = [Slot(p) for p in pf]
            PB = Slot(pb)
            xcol = lambda db, t0: db * TH + t0
            rcol = lambda db, t0: db * T + t0

            for dst, src in ((bq, bq_i), (bk, bk_i), (bo, bo_i), (b2, b2_i),
                             (gamma, gam_i), (beta, bet_i), (b1, b1_i),
                             (gwt, gwt_i), (gbb, gbb_i), (upb, upb_i),
                             (gpb, gpb_i), (dnb, dnb_i), (bvbc, bvbc_i),
                             (sel, sel_i), (maskb, mb_i), (maske0, me0_i),
                             (maske7, me7_i)):
                w.dma("cst", sy and sy.dma_start(out=dst, in_=src))
            w.inc("gps", gp and gp.iota(xn32[0][:, 0:128], pattern=[[1, 128]], base=0,
                                        channel_multiplier=-1,
                                        allow_small_or_imprecise_dtypes=True), 1)
            w.wait("vector", "gps")
            w.inc("dve", ve and ve.memset(small[:, 15:16], EPS))
            w.inc("dve", ve and ve.tensor_scalar(out=id32, in0=xn32[0][:, 0:128],
                                                 scalar1=0.0, scalar2=None,
                                                 op0=ALU.is_equal))
            w.inc("dve", ve and ve.tensor_copy(out=id16, in_=id32))
            w.inc("dve", ve and ve.memset(ones32, 1.0))
            w.inc("dve", ve and ve.memset(onesc32, 1.0))
            w.inc("dve", ve and ve.tensor_copy(out=small[:, 14:15], in_=small[:, 15:16]))
            ident_dve = w.c["dve"]

            xslot = [Slot(), Slot()]
            xnslot = [Slot(), Slot()]
            xn16slot = [Slot(), Slot()]
            w.wait("scalar", "dve", ident_dve)
            cv_jobs = []
            for j in range(16):
                cv_jobs.append((w1.rearrange("a b -> (a b)"), w1b, j))
            for j in range(16):
                cv_jobs.append((w2.rearrange("a b -> (a b)"), w2b, j))
            cv_state = {"i": 0, "tok": [None, None], "sttok": [None, None]}

            def emit_convert(n):
                CHUNK = 128 * 2048
                while cv_jobs and n > 0:
                    srcf, dstb, j = cv_jobs.pop(0)
                    i = cv_state["i"]; cv_state["i"] += 1
                    sl = i % 2
                    if cv_state["sttok"][sl] is not None:
                        w.wait_tok("gpsimd", cv_state["sttok"][sl])
                    cv_state["tok"][sl] = w.dma("cvl", gp and gp.dma_start(
                        out=cvb[sl],
                        in_=srcf[j * CHUNK:(j + 1) * CHUNK].rearrange("(p n) -> p n", p=128)))
                    w.wait_tok("sync", cv_state["tok"][sl])
                    cv_state["sttok"][sl] = w.dma("cvs", sy and sy.dma_start(
                        out=dstb[j * CHUNK:(j + 1) * CHUNK].rearrange("(p n) -> p n", p=128),
                        in_=cvb[sl]))
                    n -= 1
            w.wait_all("vector", "cst")
            w.wait_all("scalar", "cst")
            w.wait_all("tensor", "cst")
            for i in range(NTILES):
                s = i % 2
                xslot[s].pre_write(w, "sync")
                tok_x = w.dma("xin", sy and sy.dma_start(out=xt[s], in_=xc[i * 128:(i + 1) * 128, :]))
                emit_convert(1)
                w.wait_tok("vector", tok_x)
                xnslot[s].pre_write(w, "vector")
                w.inc("dve", ve and ve.reduce_sum(out=small[:, 0:1], in_=xt[s], axis=AX))
                w.inc("dve", ve and ve.tensor_mul(xn32[s], xt[s], xt[s]))
                w.inc("dve", ve and ve.reduce_sum(out=small[:, 1:2], in_=xn32[s], axis=AX))
                w.inc("dve", ve and ve.tensor_scalar_mul(small[:, 2:3], small[:, 0:1], 1.0 / D))
                w.inc("dve", ve and ve.tensor_scalar_mul(small[:, 3:4], small[:, 1:2], 1.0 / D))
                w.inc("dve", ve and ve.tensor_mul(small[:, 4:5], small[:, 2:3], small[:, 2:3]))
                d_var = w.inc("dve", ve and ve.tensor_sub(small[:, 3:4], small[:, 3:4], small[:, 4:5]))
                w.wait("scalar", "dve", d_var)
                w.inc("act", se and se.activation(out=small[:, 5:6], in_=small[:, 3:4], func=AF.Ln, bias=small[:, 15:16]))
                a_rstd = w.inc("act", se and se.activation(out=small[:, 6:7], in_=small[:, 5:6],
                                                           func=AF.Exp, scale=-0.5))
                w.wait("vector", "act", a_rstd)
                d_xn = w.inc("dve", ve and ve.tensor_scalar(out=xn32[s], in0=xt[s],
                                                            scalar1=small[:, 2:3],
                                                            scalar2=small[:, 6:7],
                                                            op0=ALU.subtract, op1=ALU.mult))
                xslot[s].freed = ("dve", d_xn)
                w.wait("scalar", "dve", d_xn)
                xn16slot[s].pre_write(w, "scalar")
                a_xn16 = w.inc("act", se and se.copy(out=xn16[s], in_=xn32[s]))
                is_center = 2 <= i < 18
                tb = i - 2
                w.wait("tensor", "act", a_xn16)
                w.wait("tensor", "dve", d_xn)
                if is_center:
                    for half in range(2):
                        PF[half].pre_write(w, "tensor")
                        for k in range(4):
                            db = half * 4 + k
                            ins = te and te.transpose(pf[half][:, k * 128:(k + 1) * 128],
                                                      xn32[s][:, db * 128:(db + 1) * 128], id32)
                            if k == 3:
                                PF[half].set_ready(w, "pe", ins)
                    xnslot[s].freed = ("pe", w.c["pe"])
                    for half in range(2):
                        PF[half].pre_read(w, "scalar")
                        dst = bufr.rearrange("p (b t) -> p b t", b=NDB)[:, half * 4:(half + 1) * 4,
                                                                       tb * 128:(tb + 1) * 128]
                        src = pf[half].rearrange("p (k t) -> p k t", k=4)
                        ins = se and se.copy(out=dst, in_=src)
                        PF[half].set_freed(w, "act", ins)
                PB.pre_write(w, "tensor")
                for db in range(NDB):
                    ins = te and te.transpose(pb[:, db * 128:(db + 1) * 128],
                                              xn16[s][:, db * 128:(db + 1) * 128], id16)
                    if db == 7:
                        PB.set_ready(w, "pe", ins)
                xn16slot[s].freed = ("pe", w.c["pe"])
                if not is_center:
                    xnslot[s].freed = ("pe", w.c["pe"])
                PB.pre_read(w, "scalar")
                dstx = xnt16.rearrange("p (b t) -> p b t", b=NDB)[:, :, i * 128:(i + 1) * 128]
                ins = se and se.copy(out=dstx, in_=pb.rearrange("p (k t) -> p k t", k=NDB))
                PB.set_freed(w, "act", ins)
                if is_center:
                    PF[2].pre_write(w, "tensor")
                    w.wait("tensor", "dve")
                    w.wait("tensor", "act")
                    for db in range(NDB):
                        c0 = rcol(db, tb * 128)
                        ins = te and te.matmul(pf[2][:, 0:8], bufr[:, c0:c0 + 128],
                                               gwt[:, db * 8:(db + 1) * 8],
                                               start=(db == 0), stop=(db == 7))
                        if db == 7:
                            PF[2].set_ready(w, "pe", ins)
                    PF[2].pre_read(w, "vector")
                    d1 = w.inc("dve", ve and ve.tensor_add(g8all[:, tb * 8:(tb + 1) * 8],
                                                           pf[2][:, 0:8], gbb))
                    PF[2].freed = ("dve", d1)

            g3 = g8all.rearrange("p (a b) -> p a b", b=8)
            w.inc("dve", ve and ve.reduce_max(out=gsm[:, 0:16].rearrange("p (a b) -> p a b", b=1),
                                              in_=g3, axis=AX, negate=True))
            w.inc("dve", ve and ve.tensor_tensor(out=g3, in0=g3,
                                                 in1=gsm[:, 0:16].rearrange("p (a b) -> p a b", b=1)
                                                 .to_broadcast([128, 16, 8]),
                                                 op=ALU.add))
            d_g = w.c["dve"]
            w.wait("scalar", "dve", d_g)
            a_g = w.inc("act", se and se.activation(out=g8all, in_=g8all, func=AF.Exp))
            w.wait("vector", "act", a_g)
            w.inc("dve", ve and ve.reduce_sum(out=gsm[:, 16:32].rearrange("p (a b) -> p a b", b=1),
                                              in_=g3, axis=AX))
            w.inc("dve", ve and ve.reciprocal(out=gsm[:, 16:32], in_=gsm[:, 16:32]))
            w.inc("dve", ve and ve.tensor_mul(g3, g3,
                                              gsm[:, 16:32].rearrange("p (a b) -> p a b", b=1)
                                              .to_broadcast([128, 16, 8])))
            w.inc("dve", ve and ve.reduce_max(out=gsm[:, 32:48].rearrange("p (a b) -> p a b", b=1),
                                              in_=g3, axis=AX))
            cm3 = comb.rearrange("p (a b) -> p a b", b=8)
            m1b = gsm[:, 32:48].rearrange("p (a b) -> p a b", b=1).to_broadcast([128, 16, 8])
            w.inc("dve", ve and ve.tensor_tensor(out=cm3, in0=g3, in1=m1b, op=ALU.is_equal))
            w.inc("dve", ve and ve.scalar_tensor_tensor(out=comb, in0=comb, scalar=-2.0,
                                                        in1=g8all, op0=ALU.mult, op1=ALU.add))
            w.inc("dve", ve and ve.reduce_max(out=gsm[:, 48:64].rearrange("p (a b) -> p a b", b=1),
                                              in_=comb.rearrange("p (a b) -> p a b", b=8), axis=AX))
            m2b = gsm[:, 48:64].rearrange("p (a b) -> p a b", b=1).to_broadcast([128, 16, 8])
            w.inc("dve", ve and ve.tensor_add(gsm[:, 32:48], gsm[:, 32:48], gsm[:, 48:64]))
            w.inc("dve", ve and ve.tensor_scalar_add(gsm[:, 32:48], gsm[:, 32:48], 1.0e-20))
            w.inc("dve", ve and ve.reciprocal(out=gsm[:, 32:48], in_=gsm[:, 32:48]))
            rdb = gsm[:, 32:48].rearrange("p (a b) -> p a b", b=1).to_broadcast([128, 16, 8])
            w.inc("dve", ve and ve.tensor_tensor(out=cm3, in0=g3, in1=m2b, op=ALU.is_ge))
            w.inc("dve", ve and ve.tensor_mul(comb, comb, g8all))
            w.inc("dve", ve and ve.tensor_tensor(out=cm3, in0=cm3, in1=rdb, op=ALU.mult))

            if "xnt" in dbg:
                w.wait("sync", "dve")
                w.wait("sync", "act")
                w.dma("out", sy and sy.dma_start(out=dbg["xnt"], in_=xnt16))
            if "xnTf" in dbg:
                w.wait("sync", "dve")
                w.wait("sync", "act")
                w.dma("out", sy and sy.dma_start(out=dbg["xnTf"], in_=bufr))
                w.wait_all("vector", "out")
            if "comb" in dbg:
                w.wait("sync", "dve")
                w.dma("out", sy and sy.dma_start(out=dbg["comb"], in_=comb))
            if "gate8" in dbg:
                w.wait("sync", "dve")
                w.wait("sync", "act")
                w.dma("out", sy and sy.dma_start(out=dbg["gate8"][:, 0:8], in_=g8))
                w.dma("out", sy and sy.dma_start(out=dbg["gate8"][:, 8:16], in_=prb))
                w.dma("out", sy and sy.dma_start(out=dbg["gate8"][:, 16:24], in_=top8))
                w.dma("out", sy and sy.dma_start(out=dbg["gate8"][:, 24:32], in_=mskg))
                w.dma("out", sy and sy.dma_start(out=dbg["gate8"][:, 32:40], in_=cw8))
                w.dma("out", sy and sy.dma_start(out=dbg["gate8"][:, 40:48], in_=small[:, 8:16]))

            w.wait("tensor", "dve")
            w.wait("tensor", "act")
            wslot = [Slot(), Slot()]
            att_it = [0]
            att_prev = [None]
            pexp_free = [0, 0]

            def emit_tr_o(pv):
                osl = pv["osl"]
                h_, cl_ = pv["h"], pv["cl"]
                for qh in range(2):
                    PB.pre_write(w, "tensor")
                    w.wait("tensor", "dve", pv["d_px"][qh])
                    for kc6 in range(6):
                        ins = te and te.transpose(pb[:, kc6 * 128:(kc6 + 1) * 128],
                                                  pexp[qh][:, kc6 * 128:(kc6 + 1) * 128], id16)
                        if kc6 == 5:
                            PB.set_ready(w, "pe", ins)
                    PB.pre_read(w, "scalar")
                    ins = se and se.copy(out=pt16[qh], in_=pb[:, 0:768])
                    PB.set_freed(w, "act", ins)
                    d_pt = w.c["act"]
                    w.wait("tensor", "act", d_pt)
                    for kc6 in range(6):
                        kpb = cl_ * 2 + kc6
                        v0 = kpb * 1024 + h_ * 128
                        ins = te and te.matmul(osl.ap[:, qh * 128:qh * 128 + 128],
                                               v16[:, v0:v0 + 128],
                                               pt16[qh][:, kc6 * 128:(kc6 + 1) * 128],
                                               start=(kc6 == 0), stop=(kc6 == 5))
                        if kc6 == 5:
                            if qh == 1:
                                osl.set_ready(w, "pe", ins)
                            else:
                                w.inc("pe", ins)
                    pexp_free[qh] = w.c["pe"]
                osl.pre_read(w, "scalar")
                o0 = pv["h"] * 1024 + pv["cl"] * 256
                ins = se and se.copy(out=otg[:, o0:o0 + 256], in_=osl.ap[:, 0:256])
                osl.set_freed(w, "act", ins)
            for g in range(2):
                gbase = g * 1024
                cbase = g * 1024
                for qtr in range(4):
                    sq = qtr % 2
                    wslot[sq].pre_write(w, "gpsimd")
                    tok_v = w.dma("wdm", gp and gp.dma_start(
                        out=wsl[sq].rearrange("p (b n) -> p b n", b=NDB),
                        in_=wv.rearrange("(b p) n -> p b n", p=128)[:, :, qtr * 256:(qtr + 1) * 256]))
                    emit_convert(1)
                    w.wait_tok("tensor", tok_v)
                    for pbi in range(12):
                        sl = PF[pbi % 2]
                        sl.pre_write(w, "tensor")
                        for db in range(NDB):
                            c0 = xcol(db, gbase + pbi * 128)
                            ins = te and te.matmul(sl.ap[:, 0:256], xnt16[:, c0:c0 + 128],
                                                   wsl[sq][:, db * 256:(db + 1) * 256],
                                                   start=(db == 0), stop=(db == 7))
                            if db == 7:
                                sl.set_ready(w, "pe", ins)
                        sl.pre_read(w, "vector")
                        o0 = pbi * 1024 + qtr * 256
                        ins = ve and ve.tensor_tensor(out=v16[:, o0:o0 + 256], in0=sl.ap[:, 0:256],
                                                      in1=bvbc[:, qtr * 256:(qtr + 1) * 256],
                                                      op=ALU.add)
                        sl.set_freed(w, "dve", ins)
                    wslot[sq].freed = ("pe", w.c["pe"])
                for hp in range(4):
                    wslot[0].pre_write(w, "gpsimd")
                    tok_q = w.dma("wdm", gp and gp.dma_start(
                        out=wsl[0].rearrange("p (b n) -> p b n", b=NDB),
                        in_=wq.rearrange("(b p) n -> p b n", p=128)[:, :, hp * 256:(hp + 1) * 256]))
                    wslot[1].pre_write(w, "gpsimd")
                    tok_k = w.dma("wdm", gp and gp.dma_start(
                        out=wsl[1].rearrange("p (b n) -> p b n", b=NDB),
                        in_=wk.rearrange("(b p) n -> p b n", p=128)[:, :, hp * 256:(hp + 1) * 256]))
                    emit_convert(1)
                    w.wait_tok("tensor", tok_q)
                    for h2 in range(2):
                        for nt in range(2):
                            sl = PF[(h2 * 2 + nt) % 2]
                            sl.pre_write(w, "tensor")
                            for db in range(NDB):
                                c0 = xcol(db, HALO + cbase + nt * 512)
                                ins = te and te.matmul(
                                    sl.ap,
                                    wsl[0][:, db * 256 + h2 * 128:db * 256 + h2 * 128 + 128],
                                    xnt16[:, c0:c0 + 512],
                                    start=(db == 0), stop=(db == 7))
                                if db == 7:
                                    sl.set_ready(w, "pe", ins)
                            sl.pre_read(w, "scalar")
                            o0 = h2 * 1024 + nt * 512
                            ins = se and se.activation(out=q16[:, o0:o0 + 512], in_=sl.ap,
                                                       func=AF.Identity,
                                                       bias=bq[:, hp * 2 + h2:hp * 2 + h2 + 1])
                            sl.set_freed(w, "act", ins)
                    w.wait_tok("tensor", tok_k)
                    for h2 in range(2):
                        for nt in range(3):
                            sl = PF[(h2 * 3 + nt) % 2]
                            sl.pre_write(w, "tensor")
                            for db in range(NDB):
                                c0 = xcol(db, gbase + nt * 512)
                                ins = te and te.matmul(
                                    sl.ap,
                                    wsl[1][:, db * 256 + h2 * 128:db * 256 + h2 * 128 + 128],
                                    xnt16[:, c0:c0 + 512],
                                    start=(db == 0), stop=(db == 7))
                                if db == 7:
                                    sl.set_ready(w, "pe", ins)
                            sl.pre_read(w, "scalar")
                            o0 = h2 * 1536 + nt * 512
                            ins = se and se.activation(out=kt16[:, o0:o0 + 512], in_=sl.ap,
                                                       func=AF.Identity,
                                                       bias=bk[:, hp * 2 + h2:hp * 2 + h2 + 1])
                            sl.set_freed(w, "act", ins)
                    wslot[0].freed = ("pe", w.c["pe"])
                    wslot[1].freed = ("pe", w.c["pe"])
                    qk_act = w.c["act"]
                    for cl in range(4):
                        ch = g * 4 + cl
                        msk = maske0 if ch == 0 else (maske7 if ch == 7 else maskb)
                        for h2 in range(2):
                            h = hp * 2 + h2
                            it = att_it[0]
                            att_it[0] += 1
                            osl = PF[6] if it % 2 == 0 else PF[0]
                            cur = dict(h=h, cl=cl, osl=osl, d_px=[0, 0])
                            osl.pre_write(w, "tensor")
                            scp = [(PF[2], PF[4]), (PF[3], PF[5])]
                            for qh in range(2):
                                scA, scB = scp[qh]
                                scA.pre_write(w, "tensor")
                                scB.pre_write(w, "tensor")
                                w.wait("tensor", "act", qk_act)
                                q0 = h2 * 1024 + cl * 256 + qh * 128
                                qap = q16[:, q0:q0 + 128]
                                for kc in range(3):
                                    dst = scA.ap[:, kc * 256:kc * 256 + 256] if kc < 2 \
                                        else scB.ap[:, 0:256]
                                    k0 = h2 * 1536 + cl * 256 + kc * 256
                                    ins = te and te.matmul(dst, qap, kt16[:, k0:k0 + 256],
                                                           start=True, stop=True)
                                    if kc == 1:
                                        scA.set_ready(w, "pe", ins)
                                    elif kc == 2:
                                        scB.set_ready(w, "pe", ins)
                            if att_prev[0] is not None:
                                emit_tr_o(att_prev[0])
                            for qh in range(2):
                                scA, scB = scp[qh]
                                c_nm = 8 + 3 * qh
                                scA.pre_read(w, "vector")
                                scB.pre_read(w, "vector")
                                ia = ve and ve.scalar_tensor_tensor(
                                    out=pn[qh][:, 0:512], in0=scA.ap, scalar=SCALE,
                                    in1=msk[:, qh * 768:qh * 768 + 512],
                                    op0=ALU.mult, op1=ALU.add)
                                scA.set_freed(w, "dve", ia)
                                ib = ve and ve.scalar_tensor_tensor(
                                    out=pn[qh][:, 512:768], in0=scB.ap[:, 0:256], scalar=SCALE,
                                    in1=msk[:, qh * 768 + 512:qh * 768 + 768],
                                    op0=ALU.mult, op1=ALU.add)
                                scB.set_freed(w, "dve", ib)
                                d_nm = w.inc("dve", ve and ve.reduce_max(
                                    out=small[:, c_nm:c_nm + 1], in_=pn[qh], axis=AX, negate=True))
                                w.wait("scalar", "dve", d_nm)
                                w.inc("act", se and se.activation(
                                    out=pn[qh], in_=pn[qh], func=AF.Exp,
                                    bias=small[:, c_nm:c_nm + 1],
                                    accum_out=small[:, c_nm + 1:c_nm + 2]))
                            for qh in range(2):
                                c_nm = 8 + 3 * qh
                                w.wait("vector", "act")
                                w.inc("dve", ve and ve.reciprocal(
                                    out=small[:, c_nm + 2:c_nm + 3], in_=small[:, c_nm + 1:c_nm + 2]))
                                w.wait("vector", "pe", pexp_free[qh])
                                cur["d_px"][qh] = w.inc("dve", ve and ve.tensor_scalar_mul(
                                    pexp[qh], pn[qh], small[:, c_nm + 2:c_nm + 3]))
                            att_prev[0] = cur
                if att_prev[0] is not None:
                    emit_tr_o(att_prev[0])
                    att_prev[0] = None
                w.wait("tensor", "act")
                for dob in range(NDB):
                    sw = dob % 2
                    wslot[sw].pre_write(w, "gpsimd")
                    tok_o = w.dma("wdm", gp and gp.dma_start(
                        out=wsl[sw][:, 0:1024].rearrange("p (b n) -> p b n", b=NDB),
                        in_=wo.rearrange("(b p) n -> p b n", p=128)[:, :, dob * 128:(dob + 1) * 128]))
                    w.wait_tok("tensor", tok_o)
                    for tb2 in range(2):
                        sl = PF[2 + tb2]
                        sl.pre_write(w, "tensor")
                        for dib in range(NDB):
                            ins = te and te.matmul(
                                sl.ap,
                                wsl[sw][:, dib * 128:(dib + 1) * 128],
                                otg[:, dib * 1024 + tb2 * 512:dib * 1024 + tb2 * 512 + 512],
                                start=(dib == 0), stop=(dib == 7))
                            if dib == 7:
                                sl.set_ready(w, "pe", ins)
                        sl.pre_read(w, "vector")
                        dst = bufr[:, rcol(dob, cbase + tb2 * 512):rcol(dob, cbase + tb2 * 512) + 512]
                        ins = ve and ve.scalar_tensor_tensor(out=dst, in0=sl.ap,
                                                             scalar=bo[:, dob:dob + 1],
                                                             in1=dst, op0=ALU.add, op1=ALU.add)
                        sl.set_freed(w, "dve", ins)
                    wslot[sw].freed = ("pe", w.c["pe"])

            if "aT" in dbg:
                w.wait("sync", "dve")
                w.dma("out", sy and sy.dma_start(out=dbg["aT"], in_=bufr))
                w.wait_all("vector", "out")

            aT_dve = w.c["dve"]
            w.wait("scalar", "dve", aT_dve)
            w.wait("tensor", "dve", aT_dve)
            ltaslot = [Slot(), Slot()]
            for tb4 in range(4):
                PF[2].pre_write(w, "tensor")
                for db in range(NDB):
                    src = bufr[:, rcol(db, tb4 * 512):rcol(db, tb4 * 512) + 512]
                    i1 = te and te.matmul(pf[2][0:1, :], onesc32, src,
                                          start=(db == 0), stop=(db == 7))
                    p1 = w.inc("pe", i1)
                    if db == 7:
                        PF[2].ready = ("pe", p1)
                PF[3].pre_write(w, "tensor")
                for db in range(NDB):
                    sidx = db % 2
                    src = bufr[:, rcol(db, tb4 * 512):rcol(db, tb4 * 512) + 512]
                    ltaslot[sidx].pre_write(w, "scalar")
                    a_s = w.inc("act", se and se.activation(out=lta[sidx], in_=src, func=AF.Square))
                    w.wait("tensor", "act", a_s)
                    i2 = te and te.matmul(pf[3][0:1, :], onesc32, lta[sidx],
                                          start=(db == 0), stop=(db == 7))
                    p2 = w.inc("pe", i2)
                    ltaslot[sidx].freed = ("pe", p2)
                    if db == 7:
                        PF[3].ready = ("pe", p2)
                PF[2].pre_read(w, "vector")
                ins = ve and ve.tensor_copy(out=lsum[0:1, tb4 * 512:(tb4 + 1) * 512], in_=pf[2][0:1, :])
                PF[2].set_freed(w, "dve", ins)
                PF[3].pre_read(w, "vector")
                ins = ve and ve.tensor_copy(out=lsq[0:1, tb4 * 512:(tb4 + 1) * 512], in_=pf[3][0:1, :])
                PF[3].set_freed(w, "dve", ins)
            w.inc("dve", ve and ve.tensor_scalar_mul(lsum, lsum, 1.0 / D))
            w.inc("dve", ve and ve.tensor_mul(ltmp, lsum, lsum))
            w.inc("dve", ve and ve.tensor_scalar_mul(lsq, lsq, 1.0 / D))
            d_v = w.inc("dve", ve and ve.tensor_sub(lsq, lsq, ltmp))
            w.wait("scalar", "dve", d_v)
            w.inc("act", se and se.activation(out=ltmp, in_=lsq, func=AF.Ln, bias=small[0:1, 15:16]))
            a_r = w.inc("act", se and se.activation(out=lsq, in_=ltmp, func=AF.Exp, scale=-0.5))
            for tb4 in range(4):
                PF[0].pre_write(w, "tensor")
                w.wait("tensor", "act", a_r)
                i0 = te and te.matmul(pf[0], ones32, lsum[0:1, tb4 * 512:(tb4 + 1) * 512],
                                      start=True, stop=True)
                PF[0].set_ready(w, "pe", i0)
                PF[1].pre_write(w, "tensor")
                i1 = te and te.matmul(pf[1], ones32, lsq[0:1, tb4 * 512:(tb4 + 1) * 512],
                                      start=True, stop=True)
                PF[1].set_ready(w, "pe", i1)
                PF[0].pre_read(w, "vector")
                PF[1].pre_read(w, "vector")
                for db in range(NDB):
                    tA = lta[db % 2]
                    ltaslot[db % 2].pre_write(w, "vector")
                    src = bufr[:, rcol(db, tb4 * 512):rcol(db, tb4 * 512) + 512]
                    w.inc("dve", ve and ve.tensor_sub(tA, src, pf[0]))
                    w.inc("dve", ve and ve.tensor_mul(tA, tA, pf[1]))
                    d_b = w.inc("dve", ve and ve.tensor_scalar(out=src, in0=tA,
                                                               scalar1=gamma[:, db:db + 1],
                                                               scalar2=beta[:, db:db + 1],
                                                               op0=ALU.mult, op1=ALU.add))
                    w.wait("scalar", "dve", d_b)
                    a_b = w.inc("act", se and se.copy(
                        out=bt16[:, rcol(db, tb4 * 512):rcol(db, tb4 * 512) + 512], in_=src))
                    ltaslot[db % 2].freed = ("dve", d_b)
                PF[0].freed = ("dve", w.c["dve"])
                PF[1].freed = ("dve", w.c["dve"])
            bT_act = w.c["act"]
            if "bT" in dbg:
                w.wait("sync", "dve")
                w.dma("out", sy and sy.dma_start(out=dbg["bT"], in_=bufr))
                w.wait_all("vector", "out")
            if "bt16" in dbg:
                w.wait("sync", "act", bT_act)
                w.dma("out", sy and sy.dma_start(out=dbg["bt16"], in_=bt16))

            w.wait("sync", "pe")
            w.wait("sync", "dve")
            emit_convert(100)
            for lane in range(2):
                if cv_state["sttok"][lane] is not None:
                    w.wait_tok("sync", cv_state["sttok"][lane])
                    w.wait_tok("gpsimd", cv_state["sttok"][lane])
            w.wait("gpsimd", "pe")
            w.wait("gpsimd", "dve")
            w1slot = [Slot(), Slot()]
            w2slot = [Slot(), Slot()]
            h1slot = Slot()
            for tt in range(4):
                h1slot.pre_write(w, "scalar")
                for do4 in range(8):
                    s = do4 % 2
                    w1slot[s].pre_write(w, "sync")
                    tok_w1 = w.dma("xin", sy and sy.dma_start(
                        out=w1sl[s].rearrange("p (b n) -> p b n", b=NDB),
                        in_=w1b.rearrange("(a b) -> a b", b=DFF)
                        .rearrange("(b p) n -> p b n", p=128)[:, :, do4 * 512:(do4 + 1) * 512]))
                    w.wait_tok("tensor", tok_w1)
                    w.wait("tensor", "act", bT_act)
                    for k in range(4):
                        do = do4 * 4 + k
                        sl = PF[do % 4]
                        sl.pre_write(w, "tensor")
                        for db in range(NDB):
                            ins = te and te.matmul(
                                sl.ap, w1sl[s][:, db * 512 + k * 128:db * 512 + k * 128 + 128],
                                bt16[:, rcol(db, tt * 512):rcol(db, tt * 512) + 512],
                                start=(db == 0), stop=(db == 7))
                            if db == 7:
                                sl.set_ready(w, "pe", ins)
                        sl.pre_read(w, "scalar")
                        ins = se and se.activation(out=h1[:, do * 512:(do + 1) * 512], in_=sl.ap,
                                                   func=AF.Gelu, bias=b1[:, do:do + 1])
                        sl.set_freed(w, "act", ins)
                    w1slot[s].freed = ("pe", w.c["pe"])
                h1_act = w.c["act"]
                for do2 in range(NDB):
                    sl = PF[4 + do2 % 2]
                    sl.pre_write(w, "tensor")
                    for half in range(2):
                        w2slot[half].pre_write(w, "gpsimd")
                        tok_w2 = w.dma("wdm", gp and gp.dma_start(
                            out=w2sl[half].rearrange("p (b n) -> p b n", b=16),
                            in_=w2b.rearrange("(a b) -> a b", b=D)
                            .rearrange("(b p) n -> p b n", p=128)[:, half * 16:(half + 1) * 16,
                                                                 do2 * 128:(do2 + 1) * 128]))
                        w.wait_tok("tensor", tok_w2)
                        w.wait("tensor", "act", h1_act)
                        for k in range(16):
                            dib = half * 16 + k
                            ins = te and te.matmul(sl.ap, w2sl[half][:, k * 128:(k + 1) * 128],
                                                   h1[:, dib * 512:(dib + 1) * 512],
                                                   start=(dib == 0), stop=(dib == 31))
                            if dib == 31:
                                sl.set_ready(w, "pe", ins)
                    for half in range(2):
                        w2slot[half].freed = ("pe", w.c["pe"])
                    sl.pre_read(w, "vector")
                    dst = bufr[:, rcol(do2, tt * 512):rcol(do2, tt * 512) + 512]
                    ins = ve and ve.scalar_tensor_tensor(out=dst, in0=sl.ap,
                                                         scalar=b2[:, do2:do2 + 1],
                                                         in1=dst, op0=ALU.add, op1=ALU.add)
                    sl.set_freed(w, "dve", ins)
                h1slot.freed = ("pe", w.c["pe"])
            if "ffn" in dbg:
                w.wait("sync", "dve")
                w.dma("out", sy and sy.dma_start(out=dbg["ffn"], in_=bufr))
                w.wait_all("vector", "out")
            if "bt16b" in dbg:
                w.wait("sync", "dve")
                w.dma("out", sy and sy.dma_start(out=dbg["bt16b"], in_=bt16))
                w.wait_all("vector", "out")
                w.wait_all("scalar", "out")
                w.wait_all("gpsimd", "out")
            if "h1f" in dbg:
                w.wait("sync", "act")
                w.dma("out", sy and sy.dma_start(out=dbg["h1f"], in_=h1))
                w.wait_all("gpsimd", "out")
                w.wait_all("scalar", "out")

            w.wait("gpsimd", "pe")
            w.wait("gpsimd", "dve")
            if cv_state["sttok"][0] is not None:
                w.wait_tok("gpsimd", cv_state["sttok"][0])
            if cv_state["sttok"][1] is not None:
                w.wait_tok("gpsimd", cv_state["sttok"][1])
            upslot = [Slot(), Slot()]
            gpslot = [Slot(), Slot()]
            dwslot = [Slot(), Slot()]
            tuslot = [Slot(), Slot()]
            tdslot = [Slot(), Slot()]
            h16slot = Slot()
            combt_slot = Slot()
            for th in range(2):
                tbase = th * 1024
                combt_slot.pre_write(w, "vector")
                for tb8 in range(8):
                    tbg = th * 8 + tb8
                    sl = PF[6]
                    sl.pre_write(w, "tensor")
                    w.wait("tensor", "dve")
                    ins = te and te.transpose(pf[6][0:8, 0:128], comb[:, tbg * 8:(tbg + 1) * 8], id32)
                    sl.set_ready(w, "pe", ins)
                    sl.pre_read(w, "vector")
                    ins = ve and ve.tensor_copy(out=combt[0:8, tb8 * 128:(tb8 + 1) * 128],
                                                in_=pf[6][0:8, 0:128])
                    sl.set_freed(w, "dve", ins)
                combt_dve = w.c["dve"]
                for e in range(E):
                    for tb2 in range(2):
                        sl = PF[6]
                        sl.pre_write(w, "tensor")
                        w.wait("tensor", "dve", combt_dve)
                        ins = te and te.matmul(pf[6], sel[:, e * 128:(e + 1) * 128],
                                               combt[0:8, tb2 * 512:(tb2 + 1) * 512],
                                               start=True, stop=True)
                        sl.set_ready(w, "pe", ins)
                        sl.pre_read(w, "vector")
                        ins = ve and ve.tensor_copy(out=cbuf[:, tb2 * 512:(tb2 + 1) * 512], in_=pf[6])
                        sl.set_freed(w, "dve", ins)
                    cb_dve = w.c["dve"]
                    h16slot.pre_write(w, "scalar")
                    for fb2 in range(NFB // 2):
                        s = fb2 % 2
                        upslot[s].pre_write(w, "gpsimd")
                        tok_u = w.dma("wdm", gp and gp.dma_start(
                            out=upw[s].rearrange("p (b n) -> p b n", b=NDB),
                            in_=upt[e].rearrange("(b p) f -> p b f", p=128)[:, :, fb2 * 256:(fb2 + 1) * 256]))
                        gpslot[s].pre_write(w, "gpsimd")
                        tok_f = w.dma("wdm", gp and gp.dma_start(
                            out=gpw[s].rearrange("p (b n) -> p b n", b=NDB),
                            in_=gpt[e].rearrange("(b p) f -> p b f", p=128)[:, :, fb2 * 256:(fb2 + 1) * 256]))
                        w.wait_tok("tensor", tok_u)
                        w.wait_tok("tensor", tok_f)
                        for k in range(2):
                            fb = fb2 * 2 + k
                            for tb2 in range(2):
                                r3 = (fb * 2 + tb2) % 3
                                slu = PF[(0, 1, 4)[r3]]
                                slg = PF[(2, 3, 5)[r3]]
                                slu.pre_write(w, "tensor")
                                slg.pre_write(w, "tensor")
                                for db in range(NDB):
                                    c0 = xcol(db, HALO + tbase + tb2 * 512)
                                    ins = te and te.matmul(
                                        slu.ap, upw[s][:, db * 256 + k * 128:db * 256 + k * 128 + 128],
                                        xnt16[:, c0:c0 + 512],
                                        start=(db == 0), stop=(db == 7))
                                    if db == 7:
                                        slu.set_ready(w, "pe", ins)
                                for db in range(NDB):
                                    c0 = xcol(db, HALO + tbase + tb2 * 512)
                                    ins = te and te.matmul(
                                        slg.ap, gpw[s][:, db * 256 + k * 128:db * 256 + k * 128 + 128],
                                        xnt16[:, c0:c0 + 512],
                                        start=(db == 0), stop=(db == 7))
                                    if db == 7:
                                        slg.set_ready(w, "pe", ins)
                                slu.pre_read(w, "scalar")
                                tuslot[tb2].pre_write(w, "scalar")
                                ins = se and se.activation(out=tu[tb2], in_=slu.ap, func=AF.Identity,
                                                           bias=upb[:, e * NFB + fb:e * NFB + fb + 1])
                                slu.set_freed(w, "act", ins)
                                a_u = w.c["act"]
                                slg.pre_read(w, "vector")
                                w.wait("vector", "act", a_u)
                                d_m = w.inc("dve", ve and ve.scalar_tensor_tensor(
                                    out=tu[tb2], in0=slg.ap,
                                    scalar=gpb[:, e * NFB + fb:e * NFB + fb + 1],
                                    in1=tu[tb2], op0=ALU.add, op1=ALU.mult))
                                slg.freed = ("dve", d_m)
                                w.wait("scalar", "dve", d_m)
                                o0 = fb * 1024 + tb2 * 512
                                a_h = w.inc("act", se and se.activation(out=h16[:, o0:o0 + 512],
                                                                        in_=tu[tb2], func=AF.Silu))
                                tuslot[tb2].freed = ("act", a_h)
                        upslot[s].freed = ("pe", w.c["pe"])
                        gpslot[s].freed = ("pe", w.c["pe"])
                    h_act = w.c["act"]
                    w.wait("vector", "dve", cb_dve)
                    for do in range(NDB):
                        s = do % 2
                        dwslot[s].pre_write(w, "gpsimd")
                        tok_d = w.dma("wdm", gp and gp.dma_start(
                            out=dwts[s].rearrange("p (b n) -> p b n", b=NFB),
                            in_=dwt[e].rearrange("(b p) n -> p b n", p=128)[:, :, do * 128:(do + 1) * 128]))
                        w.wait_tok("tensor", tok_d)
                        w.wait("tensor", "act", h_act)
                        for tb2 in range(2):
                            sl = PF[4 + tb2]
                            sl.pre_write(w, "tensor")
                            for fb in range(NFB):
                                o0 = fb * 1024 + tb2 * 512
                                ins = te and te.matmul(sl.ap, dwts[s][:, fb * 128:(fb + 1) * 128],
                                                       h16[:, o0:o0 + 512],
                                                       start=(fb == 0), stop=(fb == NFB - 1))
                                if fb == NFB - 1:
                                    sl.set_ready(w, "pe", ins)
                            sl.pre_read(w, "vector")
                            tdslot[tb2].pre_write(w, "vector")
                            ins = ve and ve.scalar_tensor_tensor(
                                out=td[tb2], in0=sl.ap,
                                scalar=dnb[:, e * 8 + do:e * 8 + do + 1],
                                in1=cbuf[:, tb2 * 512:(tb2 + 1) * 512],
                                op0=ALU.add, op1=ALU.mult)
                            sl.set_freed(w, "dve", ins)
                            dst = bufr[:, rcol(do, tbase + tb2 * 512):rcol(do, tbase + tb2 * 512) + 512]
                            d_add = w.inc("dve", ve and ve.tensor_add(dst, dst, td[tb2]))
                            tdslot[tb2].freed = ("dve", d_add)
                        dwslot[s].freed = ("pe", w.c["pe"])
                    h16slot.freed = ("pe", w.c["pe"])
                combt_slot.freed = ("pe", w.c["pe"])
            if "moe" in dbg:
                w.wait("sync", "dve")
                w.dma("out", sy and sy.dma_start(out=dbg["moe"], in_=bufr))
            if "h16" in dbg:
                w.wait("sync", "act")
                w.dma("out", sy and sy.dma_start(out=dbg["h16"], in_=h16))

            res_dve = w.c["dve"]
            w.wait("sync", "dve", res_dve)
            fslot = [Slot(), Slot()]
            oslot = [Slot(), Slot()]
            for i in range(16):
                s = i % 2
                fslot[s].pre_write(w, "sync")
                tok_xf = w.dma("xin", sy and sy.dma_start(out=xt[s], in_=xc[(2 + i) * 128:(3 + i) * 128, :]))
                for half in range(2):
                    sl = PF[half]
                    sl.pre_write(w, "tensor")
                    w.wait("tensor", "dve", res_dve)
                    for k in range(4):
                        db = half * 4 + k
                        c0 = rcol(db, i * 128)
                        ins = te and te.transpose(pf[half][:, k * 128:(k + 1) * 128],
                                                  bufr[:, c0:c0 + 128], id32)
                        if k == 3:
                            sl.set_ready(w, "pe", ins)
                oslot[s].pre_write(w, "vector")
                w.wait_tok("vector", tok_xf)
                for half in range(2):
                    sl = PF[half]
                    sl.pre_read(w, "vector")
                    ins = ve and ve.tensor_add(fot[s][:, half * 512:(half + 1) * 512],
                                               pf[half][:, 0:512],
                                               xt[s][:, half * 512:(half + 1) * 512])
                    sl.set_freed(w, "dve", ins)
                d_out = w.c["dve"]
                fslot[s].freed = ("dve", d_out)
                w.wait("sync", "dve", d_out)
                ins = sy and sy.dma_start(out=y[i * 128:(i + 1) * 128, :], in_=fot[s])
                oslot[s].freed = w.dma("out", ins)
            w.wait_all("sync", "out")

        engs = {}

        @block.tensor
        def _(eng):
            engs["tensor"] = eng
            flow(Wk("tensor", engs, semh))

        @block.vector
        def _(eng):
            engs["vector"] = eng
            flow(Wk("vector", engs, semh))

        @block.scalar
        def _(eng):
            engs["scalar"] = eng
            flow(Wk("scalar", engs, semh))

        @block.gpsimd
        def _(eng):
            engs["gpsimd"] = eng
            flow(Wk("gpsimd", engs, semh))

        @block.sync
        def _(eng):
            engs["sync"] = eng
            flow(Wk("sync", engs, semh))

    return nc


def _prep_inputs(L0_attn_x, L0_attn_Wq, L0_attn_bq, L0_attn_Wk, L0_attn_bk,
                 L0_attn_Wv, L0_attn_bv, L0_attn_Wo, L0_attn_bo,
                 L0_attn_gamma, L0_attn_beta, L0_attn_W1, L0_attn_b1,
                 L0_attn_W2, L0_attn_b2, L0_ffn_gate_w, L0_ffn_gate_b,
                 L0_ffn_up_w, L0_ffn_up_b, L0_ffn_gp_w, L0_ffn_gp_b,
                 L0_ffn_down_w, L0_ffn_down_b):
    f = lambda a: np.ascontiguousarray(np.asarray(a, dtype=np.float32))
    x = f(L0_attn_x).reshape(16384, D)
    xpad = np.zeros((16384 + 2 * HALO, D), np.float32)
    xpad[HALO:HALO + 16384] = x

    colb = lambda v, nb: np.ascontiguousarray(f(v).reshape(nb, 128).T)
    wq_, wk_, wv_, wo_ = f(L0_attn_Wq), f(L0_attn_Wk), f(L0_attn_Wv), f(L0_attn_Wo)
    w1_, w2_ = f(L0_attn_W1), f(L0_attn_W2)
    upt = np.zeros((E, D, FPAD), np.float32)
    upt[:, :, :F] = np.transpose(f(L0_ffn_up_w), (0, 2, 1))
    gpt = np.zeros((E, D, FPAD), np.float32)
    gpt[:, :, :F] = np.transpose(f(L0_ffn_gp_w), (0, 2, 1))
    dwt = np.zeros((E, FPAD, D), np.float32)
    dwt[:, :F, :] = np.transpose(f(L0_ffn_down_w), (0, 2, 1))
    upb = np.zeros((E, FPAD), np.float32); upb[:, :F] = f(L0_ffn_up_b)
    gpb = np.zeros((E, FPAD), np.float32); gpb[:, :F] = f(L0_ffn_gp_b)
    upb_re = np.ascontiguousarray(np.transpose(upb.reshape(E, NFB, 128), (2, 0, 1)).reshape(128, E * NFB))
    gpb_re = np.ascontiguousarray(np.transpose(gpb.reshape(E, NFB, 128), (2, 0, 1)).reshape(128, E * NFB))
    dnb_re = np.ascontiguousarray(np.transpose(f(L0_ffn_down_b).reshape(E, 8, 128), (2, 0, 1)).reshape(128, E * 8))
    gw = f(L0_ffn_gate_w)
    gwt_re = np.ascontiguousarray(np.transpose(gw.T.reshape(8, 128, E), (1, 0, 2)).reshape(128, 8 * E))
    gbb = np.ascontiguousarray(np.tile(f(L0_ffn_gate_b)[None, :], (128, 1)))
    bvbc = np.ascontiguousarray(np.tile(f(L0_attn_bv)[None, :], (128, 1)))
    selm = np.zeros((8, 1024), np.float32)
    for e in range(8):
        selm[e, e * 128:(e + 1) * 128] = 1.0

    qi = np.arange(256)[:, None]
    kj = np.arange(768)[None, :]
    band = (kj >= qi) & (kj <= qi + 512)
    mb = np.where(band, 0.0, NEG).astype(np.float32)
    m_int = np.concatenate([mb[0:128], mb[128:256]], axis=1)
    kill_first = np.where(np.arange(768) < 256, NEG, 0.0).astype(np.float32)
    kill_last = np.where(np.arange(768) >= 512, NEG, 0.0).astype(np.float32)

    per_core = []
    for c in range(8):
        m_e0 = m_int.copy()
        m_e7 = m_int.copy()
        if c == 0:
            for qh in range(2):
                m_e0[:, qh * 768:(qh + 1) * 768] = np.minimum(
                    m_e0[:, qh * 768:(qh + 1) * 768], kill_first[None, :])
        if c == 7:
            for qh in range(2):
                m_e7[:, qh * 768:(qh + 1) * 768] = np.minimum(
                    m_e7[:, qh * 768:(qh + 1) * 768], kill_last[None, :])
        per_core.append(dict(
            xc=np.ascontiguousarray(xpad[c * T:c * T + TH]),
            wq=wq_, wk=wk_, wv=wv_, wo=wo_, w1=w1_, w2=w2_,
            upt=upt, gpt=gpt, dwt=dwt,
            bq=colb(L0_attn_bq, 8), bk=colb(L0_attn_bk, 8),
            bo=colb(L0_attn_bo, 8), b2=colb(L0_attn_b2, 8),
            gamma=colb(L0_attn_gamma, 8), beta=colb(L0_attn_beta, 8),
            b1=colb(L0_attn_b1, 32),
            gwt=gwt_re, gbb=gbb, upb=upb_re, gpb=gpb_re, dnb=dnb_re,
            bvbc=bvbc, sel=selm,
            maskb=m_int.astype(BFNP), maske0=m_e0.astype(BFNP),
            maske7=m_e7.astype(BFNP),
        ))
    return per_core


def kernel(**inputs):
    in_maps = _prep_inputs(**inputs)
    nc = build_program()
    res = run_bass_kernel_spmd(nc, in_maps, core_ids=list(range(8)))
    out = np.concatenate([np.asarray(r["y"]) for r in res.results], axis=0)
    return out.reshape(1, 16384, D).astype(np.float32)


# version 14
# speedup vs baseline: 1.2353x; 1.0368x over previous
import numpy as np
import ml_dtypes
import concourse.bass as bass
import concourse.mybir as mybir
from concourse.bass_utils import run_bass_kernel_spmd

F32 = mybir.dt.float32
BF16 = mybir.dt.bfloat16
U8 = mybir.dt.uint8
AF = mybir.ActivationFunctionType
ALU = mybir.AluOpType
AX = mybir.AxisListType.X
BFNP = ml_dtypes.bfloat16

D = 1024
NDB = 8
T = 2048
HALO = 256
TH = T + 2 * HALO
NTILES = TH // 128
H = 8
E = 8
F = 3264
FPAD = 3328
NFB = FPAD // 128
DFF = 4096
SCALE = float(1.0 / np.sqrt(128.0))
NEG = -1.0e30
EPS = 1.0e-5

P_ID32 = 0
P_ID16 = 512
P_ONES32 = 768
P_ONESC32 = 1280
P_BQ = 1312
P_BK = 1344
P_BO = 1376
P_B2 = 1408
P_GAMMA = 1440
P_BETA = 1472
P_B1 = 1504
P_GWT = 1632
P_GBB = 1888
P_UPB = 1920
P_GPB = 2752
P_DNB = 3584
P_COMB = 3840
P_SMALL = 4352
P_SEL = 4416
P_XNT16 = 8512
P_BUFR = 49472
S = 115008
A_XT0 = S + 0
A_XT1 = S + 4096
A_XN0 = S + 8192
A_XN1 = S + 12288
A_XN16_0 = S + 16384
A_XN16_1 = S + 18432
A_G8 = S + 20480
A_P8 = A_G8 + 32
A_PRB = A_G8 + 64
A_TOP8 = A_G8 + 96
A_MSK = A_G8 + 128
A_CW = A_G8 + 160
A_ST6 = A_G8 + 192
FI_OT0 = S + 21504
FI_OT1 = S + 25600
AT_V16 = S + 0
AT_BVBC = S + 24576
AT_MB = S + 28672
AT_ME0 = S + 31744
AT_ME7 = S + 34816
AT_OTG = S + 37888
AT_KT = S + 54272
AT_Q16 = S + 60416
AT_WSL0 = S + 64512
AT_WSL1 = S + 68608
AT_PN0 = S + 72704
AT_PN1 = S + 75776
AT_PEXPA = S + 78848
AT_PEXPB = S + 80384
AT_PTA = S + 81920
AT_PTB = S + 83456
L_SUM = S + 0
L_SQ = S + 8192
L_TA = S + 20480
L_TMP = S + 70656
FF_W1SL = S + 0
FF_W2SL = S + 16384
FF_BT16 = S + 24576
FF_H1 = S + 57344
M_COMBT = S + 0
M_CB = S + 4096
M_TU = S + 6144
M_TD = S + 10240
M_UPW = S + 14336
M_GPW = S + 22528
M_DWT = S + 30720
M_H16 = S + 44032
ARENA = S + 97280

DMAQ = {"xin": 6, "wdm": 8, "out": 3, "cst": 1, "cvl": 2, "cvs": 2, "wsc": 3, "wve": 3}
SEMS = tuple(f"{q}{j}" for q, n in DMAQ.items() for j in range(n)) + ("pe", "dve", "act", "gps")


class Wk:
    def __init__(self, which, engs, sems):
        self.which = which
        self.engs = engs
        self.sems = sems
        self.c = {k: 0 for k in sems}
        self.hw = {}
        self.qidx = {q: 0 for q in DMAQ}

    def dma(self, q, instr):
        i = self.qidx[q]
        self.qidx[q] = i + 1
        ns = DMAQ[q]
        sname = f"{q}{i % ns}"
        v = 16 * (i // ns + 1)
        self.c[sname] = v
        if instr is not None:
            instr.then_inc(self.sems[sname], 16)
        return (sname, v)

    def wait_tok(self, engname, tok):
        self.wait(engname, tok[0], tok[1])

    def wait_all(self, engname, q):
        ns = DMAQ[q]
        tot = self.qidx[q]
        for j in range(ns):
            uses = (tot - j + ns - 1) // ns
            if uses > 0:
                self.wait(engname, f"{q}{j}", 16 * uses)

    def eng(self, name):
        return self.engs[name] if self.which == name else None

    SELFW = {"dve": "vector", "act": "scalar", "gps": "gpsimd"}

    def inc(self, sem, instr, k=1):
        self.c[sem] += k
        if instr is not None:
            instr.then_inc(self.sems[sem], k)
        eng = self.SELFW.get(sem)
        if eng is not None and self.which == eng:
            v = self.c[sem]
            if self.hw.get(sem, -1) < v:
                self.hw[sem] = v
                self.engs[eng].wait_ge(self.sems[sem], v)
        return self.c[sem]

    def wait(self, engname, sem, val=None):
        v = self.c[sem] if val is None else val
        if v <= 0:
            return
        if self.which == engname:
            key = sem
            if self.hw.get(key, -1) >= v:
                return
            self.hw[key] = v
            self.engs[engname].wait_ge(self.sems[sem], v)


class Slot:
    def __init__(self, ap=None):
        self.ap = ap
        self.ready = None
        self.freed = None

    def pre_write(self, w, eng):
        if self.freed is not None:
            w.wait(eng, *self.freed)

    def set_ready(self, w, sem, instr, k=1):
        self.ready = (sem, w.inc(sem, instr, k))

    def pre_read(self, w, eng):
        if self.ready is not None:
            w.wait(eng, *self.ready)

    def set_freed(self, w, sem, instr, k=1):
        self.freed = (sem, w.inc(sem, instr, k))


def build_program(debug=()):
    nc = bass.Bass("TRN2", target_bir_lowering=False)

    def din(name, shape, dt=F32):
        return nc.dram_tensor(name, list(shape), dt, kind="ExternalInput").ap()

    xc = din("xc", [TH, D])
    wq = din("wq", [D, D]); wk = din("wk", [D, D])
    wv = din("wv", [D, D]); wo = din("wo", [D, D])
    w1 = din("w1", [D, DFF]); w2 = din("w2", [DFF, D])
    upt = din("upt", [E, D, FPAD]); gpt = din("gpt", [E, D, FPAD])
    dwt = din("dwt", [E, FPAD, D])
    bq_i = din("bq", [128, 8]); bk_i = din("bk", [128, 8])
    bo_i = din("bo", [128, 8]); b2_i = din("b2", [128, 8])
    gam_i = din("gamma", [128, 8]); bet_i = din("beta", [128, 8])
    b1_i = din("b1", [128, 32])
    gwt_i = din("gwt", [128, 64]); gbb_i = din("gbb", [128, 8])
    upb_i = din("upb", [128, E * NFB]); gpb_i = din("gpb", [128, E * NFB])
    dnb_i = din("dnb", [128, E * 8])
    bvbc_i = din("bvbc", [128, D])
    sel_i = din("sel", [8, 1024])
    mb_i = din("maskb", [128, 1536], BF16)
    me0_i = din("maske0", [128, 1536], BF16)
    me7_i = din("maske7", [128, 1536], BF16)

    w1b = nc.dram_tensor("w1b", [D * DFF], BF16, kind="Internal").ap()
    w2b = nc.dram_tensor("w2b", [DFF * D], BF16, kind="Internal").ap()
    y = nc.dram_tensor("y", [T, D], F32, kind="ExternalOutput").ap()
    dbg = {}
    for name in debug:
        if name == "xnt":
            dbg[name] = nc.dram_tensor("d_xnt", [128, NDB * TH], BF16, kind="ExternalOutput").ap()
        elif name in ("xnTf", "aT", "bT", "ffn", "moe"):
            dbg[name] = nc.dram_tensor("d_" + name, [128, NDB * T], F32, kind="ExternalOutput").ap()
        elif name == "comb":
            dbg[name] = nc.dram_tensor("d_comb", [128, 128], F32, kind="ExternalOutput").ap()
        elif name == "gate8":
            dbg[name] = nc.dram_tensor("d_gate8", [128, 48], F32, kind="ExternalOutput").ap()
        elif name in ("bt16", "bt16b"):
            dbg[name] = nc.dram_tensor("d_" + name, [128, NDB * T], BF16, kind="ExternalOutput").ap()
        elif name == "h1f":
            dbg[name] = nc.dram_tensor("d_h1f", [128, 32 * 512], BF16, kind="ExternalOutput").ap()
        elif name == "h16":
            dbg[name] = nc.dram_tensor("d_h16", [128, NFB * 1024], BF16, kind="ExternalOutput").ap()

    arena = nc.alloc_sbuf_tensor("arena", [128, ARENA], U8)
    arena_addr = nc.lookup_mloc(arena).addr

    def place(name, off, shape, dt):
        h = nc.virtual_sbuf_tensor(name, list(shape), dt)
        nc._tensor(h.name, list(shape), dt, type="SB")
        mloc = nc.lookup_mloc(h)
        mloc.allocated = True
        mloc.addr = arena_addr + off
        return h.ap()

    id32 = place("id32", P_ID32, [128, 128], F32)
    id16 = place("id16", P_ID16, [128, 128], BF16)
    ones32 = place("ones32", P_ONES32, [1, 128], F32)
    onesc32 = place("onesc32", P_ONESC32, [128, 1], F32)
    bq = place("bq_s", P_BQ, [128, 8], F32)
    bk = place("bk_s", P_BK, [128, 8], F32)
    bo = place("bo_s", P_BO, [128, 8], F32)
    b2 = place("b2_s", P_B2, [128, 8], F32)
    gamma = place("gamma_s", P_GAMMA, [128, 8], F32)
    beta = place("beta_s", P_BETA, [128, 8], F32)
    b1 = place("b1_s", P_B1, [128, 32], F32)
    gwt = place("gwt_s", P_GWT, [128, 64], F32)
    gbb = place("gbb_s", P_GBB, [128, 8], F32)
    upb = place("upb_s", P_UPB, [128, E * NFB], F32)
    gpb = place("gpb_s", P_GPB, [128, E * NFB], F32)
    dnb = place("dnb_s", P_DNB, [128, E * 8], F32)
    comb = place("comb_s", P_COMB, [128, 128], F32)
    small = place("small_s", P_SMALL, [128, 16], F32)
    sel = place("sel_s", P_SEL, [8, 1024], F32)
    xnt16 = place("xnt16", P_XNT16, [128, NDB * TH], BF16)
    bufr = place("bufr", P_BUFR, [128, NDB * T], F32)
    xt = [place("xt0", A_XT0, [128, D], F32), place("xt1", A_XT1, [128, D], F32)]
    xn32 = [place("xn0", A_XN0, [128, D], F32), place("xn1", A_XN1, [128, D], F32)]
    xn16 = [place("xn16_0", A_XN16_0, [128, D], BF16), place("xn16_1", A_XN16_1, [128, D], BF16)]
    g8 = place("g8_s", A_G8, [128, 8], F32)
    g8all = place("g8all", FI_OT0, [128, 128], F32)
    gsm = place("gsm", FI_OT0 + 512, [128, 64], F32)
    p8 = place("p8_s", A_P8, [128, 8], F32)
    prb = place("prb_s", A_PRB, [128, 8], F32)
    top8 = place("top8_s", A_TOP8, [128, 8], F32)
    mskg = place("mskg_s", A_MSK, [128, 8], F32)
    cw8 = place("cw8_s", A_CW, [128, 8], F32)
    st6 = place("st6_s", A_ST6, [128, 12], F32)
    fot = [place("fot0", FI_OT0, [128, D], F32), place("fot1", FI_OT1, [128, D], F32)]
    v16 = place("v16", AT_V16, [128, 12 * 1024], BF16)
    bvbc = place("bvbc_s", AT_BVBC, [128, D], F32)
    maskb = place("maskb_s", AT_MB, [128, 1536], BF16)
    maske0 = place("maske0_s", AT_ME0, [128, 1536], BF16)
    maske7 = place("maske7_s", AT_ME7, [128, 1536], BF16)
    otg = place("otg", AT_OTG, [128, H * 1024], BF16)
    kt16 = place("kt16", AT_KT, [128, 2 * 1536], BF16)
    q16 = place("q16", AT_Q16, [128, 2 * 1024], BF16)
    wsl = [place("wsl0", AT_WSL0, [128, 2048], BF16), place("wsl1", AT_WSL1, [128, 2048], BF16)]
    pn = [place("pn0", AT_PN0, [128, 768], F32), place("pn1", AT_PN1, [128, 768], F32)]
    pexp = [place("pexpA", AT_PEXPA, [128, 768], BF16), place("pexpB", AT_PEXPB, [128, 768], BF16)]
    pt16 = [place("ptA", AT_PTA, [128, 768], BF16), place("ptB", AT_PTB, [128, 768], BF16)]
    lsum = place("lsum", L_SUM, [1, 2048], F32)
    lsq = place("lsq", L_SQ, [1, 2048], F32)
    ltmp = place("ltmp", L_TMP, [1, 2048], F32)
    lta = [place("lta0", L_TA, [128, 512], F32), place("lta1", L_TA + 2048, [128, 512], F32)]
    w1sl = [place("w1sl0", FF_W1SL, [128, 4096], BF16), place("w1sl1", FF_W1SL + 8192, [128, 4096], BF16)]
    w2qs = [place(f"w2q{i}", FF_W2SL + i * 2048, [128, 1024], BF16) for i in range(4)]
    bt16 = place("bt16", FF_BT16, [128, NDB * T], BF16)
    h1 = place("h1", FF_H1, [128, 32 * 512], BF16)
    combt = place("combt", M_COMBT, [8, 1024], F32)
    cbuf = place("cbuf", M_CB, [128, 1024], BF16)
    tu = [place("tu0", M_TU, [128, 512], F32), place("tu1", M_TU + 2048, [128, 512], F32)]
    td = [place("td0", M_TD, [128, 512], F32), place("td1", M_TD + 2048, [128, 512], F32)]
    upw = [place("upw0", M_UPW, [128, 2048], BF16), place("upw1", M_UPW + 4096, [128, 2048], BF16)]
    cvb = [place("cvb0", S + 84992, [128, 2048], BF16), place("cvb1", S + 89088, [128, 2048], BF16)]
    gpw = [place("gpw0", M_GPW, [128, 2048], BF16), place("gpw1", M_GPW + 4096, [128, 2048], BF16)]
    dwts = [place("dwt0", M_DWT, [128, 3328], BF16), place("dwt1", M_DWT + 6656, [128, 3328], BF16)]
    h16 = place("h16", M_H16, [128, NFB * 1024], BF16)

    pf = [nc.alloc_psum_tensor(f"pf{i}", [128, 512], F32).ap() for i in range(7)]
    pb = nc.alloc_psum_tensor("pb", [128, 1024], BF16).ap()

    from contextlib import ExitStack
    with ExitStack() as ctx:
        semh = {s: ctx.enter_context(nc.semaphore(s)) for s in SEMS}
        block = ctx.enter_context(nc.Block())

        def flow(w: Wk):
            te = w.eng("tensor"); ve = w.eng("vector"); se = w.eng("scalar")
            gp = w.eng("gpsimd"); sy = w.eng("sync")
            PF = [Slot(p) for p in pf]
            PB = Slot(pb)
            xcol = lambda db, t0: db * TH + t0
            rcol = lambda db, t0: db * T + t0

            for dst, src in ((bq, bq_i), (bk, bk_i), (bo, bo_i), (b2, b2_i),
                             (gamma, gam_i), (beta, bet_i), (b1, b1_i),
                             (gwt, gwt_i), (gbb, gbb_i), (upb, upb_i),
                             (gpb, gpb_i), (dnb, dnb_i), (bvbc, bvbc_i),
                             (sel, sel_i), (maskb, mb_i), (maske0, me0_i),
                             (maske7, me7_i)):
                w.dma("cst", sy and sy.dma_start(out=dst, in_=src))
            w.inc("gps", gp and gp.iota(xn32[0][:, 0:128], pattern=[[1, 128]], base=0,
                                        channel_multiplier=-1,
                                        allow_small_or_imprecise_dtypes=True), 1)
            w.wait("vector", "gps")
            w.inc("dve", ve and ve.memset(small[:, 15:16], EPS))
            w.inc("dve", ve and ve.tensor_scalar(out=id32, in0=xn32[0][:, 0:128],
                                                 scalar1=0.0, scalar2=None,
                                                 op0=ALU.is_equal))
            w.inc("dve", ve and ve.tensor_copy(out=id16, in_=id32))
            w.inc("dve", ve and ve.memset(ones32, 1.0))
            w.inc("dve", ve and ve.memset(onesc32, 1.0))
            w.inc("dve", ve and ve.tensor_copy(out=small[:, 14:15], in_=small[:, 15:16]))
            ident_dve = w.c["dve"]

            xslot = [Slot(), Slot()]
            xnslot = [Slot(), Slot()]
            xn16slot = [Slot(), Slot()]
            w.wait("scalar", "dve", ident_dve)
            cv_jobs = []
            for j in range(16):
                cv_jobs.append((w1.rearrange("a b -> (a b)"), w1b, j))
            for j in range(16):
                cv_jobs.append((w2.rearrange("a b -> (a b)"), w2b, j))
            cv_state = {"i": 0, "tok": [None, None], "sttok": [None, None]}

            def emit_convert(n):
                CHUNK = 128 * 2048
                while cv_jobs and n > 0:
                    srcf, dstb, j = cv_jobs.pop(0)
                    i = cv_state["i"]; cv_state["i"] += 1
                    sl = i % 2
                    if cv_state["sttok"][sl] is not None:
                        w.wait_tok("gpsimd", cv_state["sttok"][sl])
                    cv_state["tok"][sl] = w.dma("cvl", gp and gp.dma_start(
                        out=cvb[sl],
                        in_=srcf[j * CHUNK:(j + 1) * CHUNK].rearrange("(p n) -> p n", p=128)))
                    w.wait_tok("sync", cv_state["tok"][sl])
                    cv_state["sttok"][sl] = w.dma("cvs", sy and sy.dma_start(
                        out=dstb[j * CHUNK:(j + 1) * CHUNK].rearrange("(p n) -> p n", p=128),
                        in_=cvb[sl]))
                    n -= 1
            w.wait_all("vector", "cst")
            w.wait_all("scalar", "cst")
            w.wait_all("tensor", "cst")
            for i in range(NTILES):
                s = i % 2
                xslot[s].pre_write(w, "sync")
                tok_x = w.dma("xin", sy and sy.dma_start(out=xt[s], in_=xc[i * 128:(i + 1) * 128, :]))
                emit_convert(1)
                w.wait_tok("vector", tok_x)
                xnslot[s].pre_write(w, "vector")
                w.inc("dve", ve and ve.reduce_sum(out=small[:, 0:1], in_=xt[s], axis=AX))
                w.inc("dve", ve and ve.tensor_mul(xn32[s], xt[s], xt[s]))
                w.inc("dve", ve and ve.reduce_sum(out=small[:, 1:2], in_=xn32[s], axis=AX))
                w.inc("dve", ve and ve.tensor_scalar_mul(small[:, 2:3], small[:, 0:1], 1.0 / D))
                w.inc("dve", ve and ve.tensor_scalar_mul(small[:, 3:4], small[:, 1:2], 1.0 / D))
                w.inc("dve", ve and ve.tensor_mul(small[:, 4:5], small[:, 2:3], small[:, 2:3]))
                d_var = w.inc("dve", ve and ve.tensor_sub(small[:, 3:4], small[:, 3:4], small[:, 4:5]))
                w.wait("scalar", "dve", d_var)
                w.inc("act", se and se.activation(out=small[:, 5:6], in_=small[:, 3:4], func=AF.Ln, bias=small[:, 15:16]))
                a_rstd = w.inc("act", se and se.activation(out=small[:, 6:7], in_=small[:, 5:6],
                                                           func=AF.Exp, scale=-0.5))
                w.wait("vector", "act", a_rstd)
                d_xn = w.inc("dve", ve and ve.tensor_scalar(out=xn32[s], in0=xt[s],
                                                            scalar1=small[:, 2:3],
                                                            scalar2=small[:, 6:7],
                                                            op0=ALU.subtract, op1=ALU.mult))
                xslot[s].freed = ("dve", d_xn)
                w.wait("scalar", "dve", d_xn)
                xn16slot[s].pre_write(w, "scalar")
                a_xn16 = w.inc("act", se and se.copy(out=xn16[s], in_=xn32[s]))
                is_center = 2 <= i < 18
                tb = i - 2
                w.wait("tensor", "act", a_xn16)
                w.wait("tensor", "dve", d_xn)
                if is_center:
                    for half in range(2):
                        PF[half].pre_write(w, "tensor")
                        for k in range(4):
                            db = half * 4 + k
                            ins = te and te.transpose(pf[half][:, k * 128:(k + 1) * 128],
                                                      xn32[s][:, db * 128:(db + 1) * 128], id32)
                            if k == 3:
                                PF[half].set_ready(w, "pe", ins)
                    xnslot[s].freed = ("pe", w.c["pe"])
                    for half in range(2):
                        PF[half].pre_read(w, "scalar")
                        dst = bufr.rearrange("p (b t) -> p b t", b=NDB)[:, half * 4:(half + 1) * 4,
                                                                       tb * 128:(tb + 1) * 128]
                        src = pf[half].rearrange("p (k t) -> p k t", k=4)
                        ins = se and se.copy(out=dst, in_=src)
                        PF[half].set_freed(w, "act", ins)
                PB.pre_write(w, "tensor")
                for db in range(NDB):
                    ins = te and te.transpose(pb[:, db * 128:(db + 1) * 128],
                                              xn16[s][:, db * 128:(db + 1) * 128], id16)
                    if db == 7:
                        PB.set_ready(w, "pe", ins)
                xn16slot[s].freed = ("pe", w.c["pe"])
                if not is_center:
                    xnslot[s].freed = ("pe", w.c["pe"])
                PB.pre_read(w, "scalar")
                dstx = xnt16.rearrange("p (b t) -> p b t", b=NDB)[:, :, i * 128:(i + 1) * 128]
                ins = se and se.copy(out=dstx, in_=pb.rearrange("p (k t) -> p k t", k=NDB))
                PB.set_freed(w, "act", ins)
                if is_center:
                    PF[2].pre_write(w, "tensor")
                    w.wait("tensor", "dve")
                    w.wait("tensor", "act")
                    for db in range(NDB):
                        c0 = rcol(db, tb * 128)
                        ins = te and te.matmul(pf[2][:, 0:8], bufr[:, c0:c0 + 128],
                                               gwt[:, db * 8:(db + 1) * 8],
                                               start=(db == 0), stop=(db == 7))
                        if db == 7:
                            PF[2].set_ready(w, "pe", ins)
                    PF[2].pre_read(w, "vector")
                    d1 = w.inc("dve", ve and ve.tensor_add(g8all[:, tb * 8:(tb + 1) * 8],
                                                           pf[2][:, 0:8], gbb))
                    PF[2].freed = ("dve", d1)

            g3 = g8all.rearrange("p (a b) -> p a b", b=8)
            w.inc("dve", ve and ve.reduce_max(out=gsm[:, 0:16].rearrange("p (a b) -> p a b", b=1),
                                              in_=g3, axis=AX, negate=True))
            w.inc("dve", ve and ve.tensor_tensor(out=g3, in0=g3,
                                                 in1=gsm[:, 0:16].rearrange("p (a b) -> p a b", b=1)
                                                 .to_broadcast([128, 16, 8]),
                                                 op=ALU.add))
            d_g = w.c["dve"]
            w.wait("scalar", "dve", d_g)
            a_g = w.inc("act", se and se.activation(out=g8all, in_=g8all, func=AF.Exp))
            w.wait("vector", "act", a_g)
            w.inc("dve", ve and ve.reduce_sum(out=gsm[:, 16:32].rearrange("p (a b) -> p a b", b=1),
                                              in_=g3, axis=AX))
            w.inc("dve", ve and ve.reciprocal(out=gsm[:, 16:32], in_=gsm[:, 16:32]))
            w.inc("dve", ve and ve.tensor_mul(g3, g3,
                                              gsm[:, 16:32].rearrange("p (a b) -> p a b", b=1)
                                              .to_broadcast([128, 16, 8])))
            w.inc("dve", ve and ve.reduce_max(out=gsm[:, 32:48].rearrange("p (a b) -> p a b", b=1),
                                              in_=g3, axis=AX))
            cm3 = comb.rearrange("p (a b) -> p a b", b=8)
            m1b = gsm[:, 32:48].rearrange("p (a b) -> p a b", b=1).to_broadcast([128, 16, 8])
            w.inc("dve", ve and ve.tensor_tensor(out=cm3, in0=g3, in1=m1b, op=ALU.is_equal))
            w.inc("dve", ve and ve.scalar_tensor_tensor(out=comb, in0=comb, scalar=-2.0,
                                                        in1=g8all, op0=ALU.mult, op1=ALU.add))
            w.inc("dve", ve and ve.reduce_max(out=gsm[:, 48:64].rearrange("p (a b) -> p a b", b=1),
                                              in_=comb.rearrange("p (a b) -> p a b", b=8), axis=AX))
            m2b = gsm[:, 48:64].rearrange("p (a b) -> p a b", b=1).to_broadcast([128, 16, 8])
            w.inc("dve", ve and ve.tensor_add(gsm[:, 32:48], gsm[:, 32:48], gsm[:, 48:64]))
            w.inc("dve", ve and ve.tensor_scalar_add(gsm[:, 32:48], gsm[:, 32:48], 1.0e-20))
            w.inc("dve", ve and ve.reciprocal(out=gsm[:, 32:48], in_=gsm[:, 32:48]))
            rdb = gsm[:, 32:48].rearrange("p (a b) -> p a b", b=1).to_broadcast([128, 16, 8])
            w.inc("dve", ve and ve.tensor_tensor(out=cm3, in0=g3, in1=m2b, op=ALU.is_ge))
            w.inc("dve", ve and ve.tensor_mul(comb, comb, g8all))
            w.inc("dve", ve and ve.tensor_tensor(out=cm3, in0=cm3, in1=rdb, op=ALU.mult))

            if "xnt" in dbg:
                w.wait("sync", "dve")
                w.wait("sync", "act")
                w.dma("out", sy and sy.dma_start(out=dbg["xnt"], in_=xnt16))
            if "xnTf" in dbg:
                w.wait("sync", "dve")
                w.wait("sync", "act")
                w.dma("out", sy and sy.dma_start(out=dbg["xnTf"], in_=bufr))
                w.wait_all("vector", "out")
            if "comb" in dbg:
                w.wait("sync", "dve")
                w.dma("out", sy and sy.dma_start(out=dbg["comb"], in_=comb))
            if "gate8" in dbg:
                w.wait("sync", "dve")
                w.wait("sync", "act")
                w.dma("out", sy and sy.dma_start(out=dbg["gate8"][:, 0:8], in_=g8))
                w.dma("out", sy and sy.dma_start(out=dbg["gate8"][:, 8:16], in_=prb))
                w.dma("out", sy and sy.dma_start(out=dbg["gate8"][:, 16:24], in_=top8))
                w.dma("out", sy and sy.dma_start(out=dbg["gate8"][:, 24:32], in_=mskg))
                w.dma("out", sy and sy.dma_start(out=dbg["gate8"][:, 32:40], in_=cw8))
                w.dma("out", sy and sy.dma_start(out=dbg["gate8"][:, 40:48], in_=small[:, 8:16]))

            w.wait("tensor", "dve")
            w.wait("tensor", "act")
            wslot = [Slot(), Slot()]
            att_it = [0]
            att_prev = [None]
            pexp_free = [0, 0]

            def emit_tr_o(pv):
                osl = pv["osl"]
                h_, cl_ = pv["h"], pv["cl"]
                for qh in range(2):
                    PB.pre_write(w, "tensor")
                    w.wait("tensor", "dve", pv["d_px"][qh])
                    for kc6 in range(6):
                        ins = te and te.transpose(pb[:, kc6 * 128:(kc6 + 1) * 128],
                                                  pexp[qh][:, kc6 * 128:(kc6 + 1) * 128], id16)
                        if kc6 == 5:
                            PB.set_ready(w, "pe", ins)
                    PB.pre_read(w, "scalar")
                    ins = se and se.copy(out=pt16[qh], in_=pb[:, 0:768])
                    PB.set_freed(w, "act", ins)
                    d_pt = w.c["act"]
                    w.wait("tensor", "act", d_pt)
                    for kc6 in range(6):
                        kpb = cl_ * 2 + kc6
                        v0 = kpb * 1024 + h_ * 128
                        ins = te and te.matmul(osl.ap[:, qh * 128:qh * 128 + 128],
                                               v16[:, v0:v0 + 128],
                                               pt16[qh][:, kc6 * 128:(kc6 + 1) * 128],
                                               start=(kc6 == 0), stop=(kc6 == 5))
                        if kc6 == 5:
                            if qh == 1:
                                osl.set_ready(w, "pe", ins)
                            else:
                                w.inc("pe", ins)
                    pexp_free[qh] = w.c["pe"]
                osl.pre_read(w, "scalar")
                o0 = pv["h"] * 1024 + pv["cl"] * 256
                ins = se and se.copy(out=otg[:, o0:o0 + 256], in_=osl.ap[:, 0:256])
                osl.set_freed(w, "act", ins)
            for g in range(2):
                gbase = g * 1024
                cbase = g * 1024
                for qtr in range(4):
                    sq = qtr % 2
                    wslot[sq].pre_write(w, "gpsimd")
                    tok_v = w.dma("wdm", gp and gp.dma_start(
                        out=wsl[sq].rearrange("p (b n) -> p b n", b=NDB),
                        in_=wv.rearrange("(b p) n -> p b n", p=128)[:, :, qtr * 256:(qtr + 1) * 256]))
                    emit_convert(1)
                    w.wait_tok("tensor", tok_v)
                    for pbi in range(12):
                        sl = PF[pbi % 2]
                        sl.pre_write(w, "tensor")
                        for db in range(NDB):
                            c0 = xcol(db, gbase + pbi * 128)
                            ins = te and te.matmul(sl.ap[:, 0:256], xnt16[:, c0:c0 + 128],
                                                   wsl[sq][:, db * 256:(db + 1) * 256],
                                                   start=(db == 0), stop=(db == 7))
                            if db == 7:
                                sl.set_ready(w, "pe", ins)
                        sl.pre_read(w, "vector")
                        o0 = pbi * 1024 + qtr * 256
                        ins = ve and ve.tensor_tensor(out=v16[:, o0:o0 + 256], in0=sl.ap[:, 0:256],
                                                      in1=bvbc[:, qtr * 256:(qtr + 1) * 256],
                                                      op=ALU.add)
                        sl.set_freed(w, "dve", ins)
                    wslot[sq].freed = ("pe", w.c["pe"])
                for hp in range(4):
                    wslot[0].pre_write(w, "gpsimd")
                    tok_q = w.dma("wdm", gp and gp.dma_start(
                        out=wsl[0].rearrange("p (b n) -> p b n", b=NDB),
                        in_=wq.rearrange("(b p) n -> p b n", p=128)[:, :, hp * 256:(hp + 1) * 256]))
                    wslot[1].pre_write(w, "gpsimd")
                    tok_k = w.dma("wdm", gp and gp.dma_start(
                        out=wsl[1].rearrange("p (b n) -> p b n", b=NDB),
                        in_=wk.rearrange("(b p) n -> p b n", p=128)[:, :, hp * 256:(hp + 1) * 256]))
                    emit_convert(1)
                    w.wait_tok("tensor", tok_q)
                    for h2 in range(2):
                        for nt in range(2):
                            sl = PF[(h2 * 2 + nt) % 2]
                            sl.pre_write(w, "tensor")
                            for db in range(NDB):
                                c0 = xcol(db, HALO + cbase + nt * 512)
                                ins = te and te.matmul(
                                    sl.ap,
                                    wsl[0][:, db * 256 + h2 * 128:db * 256 + h2 * 128 + 128],
                                    xnt16[:, c0:c0 + 512],
                                    start=(db == 0), stop=(db == 7))
                                if db == 7:
                                    sl.set_ready(w, "pe", ins)
                            sl.pre_read(w, "scalar")
                            o0 = h2 * 1024 + nt * 512
                            ins = se and se.activation(out=q16[:, o0:o0 + 512], in_=sl.ap,
                                                       func=AF.Identity,
                                                       bias=bq[:, hp * 2 + h2:hp * 2 + h2 + 1])
                            sl.set_freed(w, "act", ins)
                    w.wait_tok("tensor", tok_k)
                    for h2 in range(2):
                        for nt in range(3):
                            sl = PF[(h2 * 3 + nt) % 2]
                            sl.pre_write(w, "tensor")
                            for db in range(NDB):
                                c0 = xcol(db, gbase + nt * 512)
                                ins = te and te.matmul(
                                    sl.ap,
                                    wsl[1][:, db * 256 + h2 * 128:db * 256 + h2 * 128 + 128],
                                    xnt16[:, c0:c0 + 512],
                                    start=(db == 0), stop=(db == 7))
                                if db == 7:
                                    sl.set_ready(w, "pe", ins)
                            sl.pre_read(w, "scalar")
                            o0 = h2 * 1536 + nt * 512
                            ins = se and se.activation(out=kt16[:, o0:o0 + 512], in_=sl.ap,
                                                       func=AF.Identity,
                                                       bias=bk[:, hp * 2 + h2:hp * 2 + h2 + 1])
                            sl.set_freed(w, "act", ins)
                    wslot[0].freed = ("pe", w.c["pe"])
                    wslot[1].freed = ("pe", w.c["pe"])
                    qk_act = w.c["act"]
                    for cl in range(4):
                        ch = g * 4 + cl
                        msk = maske0 if ch == 0 else (maske7 if ch == 7 else maskb)
                        for h2 in range(2):
                            h = hp * 2 + h2
                            it = att_it[0]
                            att_it[0] += 1
                            osl = PF[6] if it % 2 == 0 else PF[0]
                            cur = dict(h=h, cl=cl, osl=osl, d_px=[0, 0])
                            osl.pre_write(w, "tensor")
                            scp = [(PF[2], PF[4]), (PF[3], PF[5])]
                            for qh in range(2):
                                scA, scB = scp[qh]
                                scA.pre_write(w, "tensor")
                                scB.pre_write(w, "tensor")
                                w.wait("tensor", "act", qk_act)
                                q0 = h2 * 1024 + cl * 256 + qh * 128
                                qap = q16[:, q0:q0 + 128]
                                for kc in range(3):
                                    dst = scA.ap[:, kc * 256:kc * 256 + 256] if kc < 2 \
                                        else scB.ap[:, 0:256]
                                    k0 = h2 * 1536 + cl * 256 + kc * 256
                                    ins = te and te.matmul(dst, qap, kt16[:, k0:k0 + 256],
                                                           start=True, stop=True)
                                    if kc == 1:
                                        scA.set_ready(w, "pe", ins)
                                    elif kc == 2:
                                        scB.set_ready(w, "pe", ins)
                            if att_prev[0] is not None:
                                emit_tr_o(att_prev[0])
                            for qh in range(2):
                                scA, scB = scp[qh]
                                c_nm = 8 + 3 * qh
                                scA.pre_read(w, "vector")
                                scB.pre_read(w, "vector")
                                ia = ve and ve.scalar_tensor_tensor(
                                    out=pn[qh][:, 0:512], in0=scA.ap, scalar=SCALE,
                                    in1=msk[:, qh * 768:qh * 768 + 512],
                                    op0=ALU.mult, op1=ALU.add)
                                scA.set_freed(w, "dve", ia)
                                ib = ve and ve.scalar_tensor_tensor(
                                    out=pn[qh][:, 512:768], in0=scB.ap[:, 0:256], scalar=SCALE,
                                    in1=msk[:, qh * 768 + 512:qh * 768 + 768],
                                    op0=ALU.mult, op1=ALU.add)
                                scB.set_freed(w, "dve", ib)
                                d_nm = w.inc("dve", ve and ve.reduce_max(
                                    out=small[:, c_nm:c_nm + 1], in_=pn[qh], axis=AX, negate=True))
                                w.wait("scalar", "dve", d_nm)
                                w.inc("act", se and se.activation(
                                    out=pn[qh], in_=pn[qh], func=AF.Exp,
                                    bias=small[:, c_nm:c_nm + 1],
                                    accum_out=small[:, c_nm + 1:c_nm + 2]))
                            for qh in range(2):
                                c_nm = 8 + 3 * qh
                                w.wait("vector", "act")
                                w.inc("dve", ve and ve.reciprocal(
                                    out=small[:, c_nm + 2:c_nm + 3], in_=small[:, c_nm + 1:c_nm + 2]))
                                w.wait("vector", "pe", pexp_free[qh])
                                cur["d_px"][qh] = w.inc("dve", ve and ve.tensor_scalar_mul(
                                    pexp[qh], pn[qh], small[:, c_nm + 2:c_nm + 3]))
                            att_prev[0] = cur
                if att_prev[0] is not None:
                    emit_tr_o(att_prev[0])
                    att_prev[0] = None
                w.wait("tensor", "act")
                for dob in range(NDB):
                    sw = dob % 2
                    wslot[sw].pre_write(w, "gpsimd")
                    tok_o = w.dma("wdm", gp and gp.dma_start(
                        out=wsl[sw][:, 0:1024].rearrange("p (b n) -> p b n", b=NDB),
                        in_=wo.rearrange("(b p) n -> p b n", p=128)[:, :, dob * 128:(dob + 1) * 128]))
                    w.wait_tok("tensor", tok_o)
                    for tb2 in range(2):
                        sl = PF[2 + tb2]
                        sl.pre_write(w, "tensor")
                        for dib in range(NDB):
                            ins = te and te.matmul(
                                sl.ap,
                                wsl[sw][:, dib * 128:(dib + 1) * 128],
                                otg[:, dib * 1024 + tb2 * 512:dib * 1024 + tb2 * 512 + 512],
                                start=(dib == 0), stop=(dib == 7))
                            if dib == 7:
                                sl.set_ready(w, "pe", ins)
                        sl.pre_read(w, "vector")
                        dst = bufr[:, rcol(dob, cbase + tb2 * 512):rcol(dob, cbase + tb2 * 512) + 512]
                        ins = ve and ve.scalar_tensor_tensor(out=dst, in0=sl.ap,
                                                             scalar=bo[:, dob:dob + 1],
                                                             in1=dst, op0=ALU.add, op1=ALU.add)
                        sl.set_freed(w, "dve", ins)
                    wslot[sw].freed = ("pe", w.c["pe"])

            if "aT" in dbg:
                w.wait("sync", "dve")
                w.dma("out", sy and sy.dma_start(out=dbg["aT"], in_=bufr))
                w.wait_all("vector", "out")

            aT_dve = w.c["dve"]
            w.wait("scalar", "dve", aT_dve)
            w.wait("tensor", "dve", aT_dve)
            ltaslot = [Slot(), Slot()]
            for tb4 in range(4):
                PF[2].pre_write(w, "tensor")
                for db in range(NDB):
                    src = bufr[:, rcol(db, tb4 * 512):rcol(db, tb4 * 512) + 512]
                    i1 = te and te.matmul(pf[2][0:1, :], onesc32, src,
                                          start=(db == 0), stop=(db == 7))
                    p1 = w.inc("pe", i1)
                    if db == 7:
                        PF[2].ready = ("pe", p1)
                PF[3].pre_write(w, "tensor")
                for db in range(NDB):
                    sidx = db % 2
                    src = bufr[:, rcol(db, tb4 * 512):rcol(db, tb4 * 512) + 512]
                    ltaslot[sidx].pre_write(w, "scalar")
                    a_s = w.inc("act", se and se.activation(out=lta[sidx], in_=src, func=AF.Square))
                    w.wait("tensor", "act", a_s)
                    i2 = te and te.matmul(pf[3][0:1, :], onesc32, lta[sidx],
                                          start=(db == 0), stop=(db == 7))
                    p2 = w.inc("pe", i2)
                    ltaslot[sidx].freed = ("pe", p2)
                    if db == 7:
                        PF[3].ready = ("pe", p2)
                PF[2].pre_read(w, "vector")
                ins = ve and ve.tensor_copy(out=lsum[0:1, tb4 * 512:(tb4 + 1) * 512], in_=pf[2][0:1, :])
                PF[2].set_freed(w, "dve", ins)
                PF[3].pre_read(w, "vector")
                ins = ve and ve.tensor_copy(out=lsq[0:1, tb4 * 512:(tb4 + 1) * 512], in_=pf[3][0:1, :])
                PF[3].set_freed(w, "dve", ins)
            w.inc("dve", ve and ve.tensor_scalar_mul(lsum, lsum, 1.0 / D))
            w.inc("dve", ve and ve.tensor_mul(ltmp, lsum, lsum))
            w.inc("dve", ve and ve.tensor_scalar_mul(lsq, lsq, 1.0 / D))
            d_v = w.inc("dve", ve and ve.tensor_sub(lsq, lsq, ltmp))
            w.wait("scalar", "dve", d_v)
            w.inc("act", se and se.activation(out=ltmp, in_=lsq, func=AF.Ln, bias=small[0:1, 15:16]))
            a_r = w.inc("act", se and se.activation(out=lsq, in_=ltmp, func=AF.Exp, scale=-0.5))
            for tb4 in range(4):
                PF[0].pre_write(w, "tensor")
                w.wait("tensor", "act", a_r)
                i0 = te and te.matmul(pf[0], ones32, lsum[0:1, tb4 * 512:(tb4 + 1) * 512],
                                      start=True, stop=True)
                PF[0].set_ready(w, "pe", i0)
                PF[1].pre_write(w, "tensor")
                i1 = te and te.matmul(pf[1], ones32, lsq[0:1, tb4 * 512:(tb4 + 1) * 512],
                                      start=True, stop=True)
                PF[1].set_ready(w, "pe", i1)
                PF[0].pre_read(w, "vector")
                PF[1].pre_read(w, "vector")
                for db in range(NDB):
                    tA = lta[db % 2]
                    ltaslot[db % 2].pre_write(w, "vector")
                    src = bufr[:, rcol(db, tb4 * 512):rcol(db, tb4 * 512) + 512]
                    w.inc("dve", ve and ve.tensor_sub(tA, src, pf[0]))
                    w.inc("dve", ve and ve.tensor_mul(tA, tA, pf[1]))
                    d_b = w.inc("dve", ve and ve.tensor_scalar(out=src, in0=tA,
                                                               scalar1=gamma[:, db:db + 1],
                                                               scalar2=beta[:, db:db + 1],
                                                               op0=ALU.mult, op1=ALU.add))
                    w.wait("scalar", "dve", d_b)
                    a_b = w.inc("act", se and se.copy(
                        out=bt16[:, rcol(db, tb4 * 512):rcol(db, tb4 * 512) + 512], in_=src))
                    ltaslot[db % 2].freed = ("dve", d_b)
                PF[0].freed = ("dve", w.c["dve"])
                PF[1].freed = ("dve", w.c["dve"])
            bT_act = w.c["act"]
            if "bT" in dbg:
                w.wait("sync", "dve")
                w.dma("out", sy and sy.dma_start(out=dbg["bT"], in_=bufr))
                w.wait_all("vector", "out")
            if "bt16" in dbg:
                w.wait("sync", "act", bT_act)
                w.dma("out", sy and sy.dma_start(out=dbg["bt16"], in_=bt16))

            w.wait("sync", "pe")
            w.wait("sync", "dve")
            emit_convert(100)
            for lane in range(2):
                if cv_state["sttok"][lane] is not None:
                    w.wait_tok("sync", cv_state["sttok"][lane])
                    w.wait_tok("gpsimd", cv_state["sttok"][lane])
                    w.wait_tok("scalar", cv_state["sttok"][lane])
                    w.wait_tok("vector", cv_state["sttok"][lane])
            w.wait("gpsimd", "pe")
            w.wait("gpsimd", "dve")
            w.wait("scalar", "pe")
            w.wait("scalar", "dve")
            w.wait("vector", "pe")
            w.wait("vector", "dve")
            w1slot = [Slot(), Slot()]
            w2slot = [Slot(), Slot(), Slot(), Slot()]
            h1slot = Slot()
            w1view = w1b.rearrange("(a b) -> a b", b=DFF).rearrange("(b p) n -> p b n", p=128)
            w2view = w2b.rearrange("(a b) -> a b", b=D).rearrange("(b p) n -> p b n", p=128)

            def w1_half(do4, hf):
                s = do4 % 2
                eng = sy if hf == 0 else se
                w1slot[s].pre_write(w, "sync" if hf == 0 else "scalar")
                c0 = do4 * 512 + hf * 256
                return w.dma("xin" if hf == 0 else "wsc", eng and eng.dma_start(
                    out=w1sl[s].rearrange("p (b n) -> p b n", b=NDB)[:, :, hf * 256:hf * 256 + 256],
                    in_=w1view[:, :, c0:c0 + 256]))

            def w2_quarter(do2, qq, w2tok):
                j = do2 * 4 + qq
                eng = gp if qq % 2 == 0 else sy
                w2slot[qq].pre_write(w, "gpsimd" if qq % 2 == 0 else "sync")
                w2tok[j] = w.dma("wdm" if qq % 2 == 0 else "wve", eng and eng.dma_start(
                    out=w2qs[qq].rearrange("p (b n) -> p b n", b=8),
                    in_=w2view[:, qq * 8:(qq + 1) * 8, do2 * 128:(do2 + 1) * 128]))

            for tt in range(4):
                tokA = [None] * 8
                tokB = [None] * 8
                w2tok = {}
                tokA[0] = w1_half(0, 0)
                tokB[0] = w1_half(0, 1)
                tokA[1] = w1_half(1, 0)
                tokB[1] = w1_half(1, 1)
                for qq in range(4):
                    w2_quarter(0, qq, w2tok)
                h1slot.pre_write(w, "scalar")
                for do4 in range(8):
                    s = do4 % 2
                    if 1 <= do4 < 7:
                        tokA[do4 + 1] = w1_half(do4 + 1, 0)
                        tokB[do4 + 1] = w1_half(do4 + 1, 1)
                    w.wait("tensor", "act", bT_act)
                    for k in range(4):
                        if k == 0:
                            w.wait_tok("tensor", tokA[do4])
                        if k == 2:
                            w.wait_tok("tensor", tokB[do4])
                        do = do4 * 4 + k
                        sl = PF[do % 4]
                        sl.pre_write(w, "tensor")
                        for db in range(NDB):
                            ins = te and te.matmul(
                                sl.ap, w1sl[s][:, db * 512 + k * 128:db * 512 + k * 128 + 128],
                                bt16[:, rcol(db, tt * 512):rcol(db, tt * 512) + 512],
                                start=(db == 0), stop=(db == 7))
                            if db == 7:
                                sl.set_ready(w, "pe", ins)
                        sl.pre_read(w, "scalar")
                        ins = se and se.activation(out=h1[:, do * 512:(do + 1) * 512], in_=sl.ap,
                                                   func=AF.Gelu, bias=b1[:, do:do + 1])
                        sl.set_freed(w, "act", ins)
                    w1slot[s].freed = ("pe", w.c["pe"])
                h1_act = w.c["act"]
                for do2 in range(NDB):
                    sl = PF[4 + do2 % 2]
                    sl.pre_write(w, "tensor")
                    w.wait("tensor", "act", h1_act)
                    for qq in range(4):
                        j = do2 * 4 + qq
                        w.wait_tok("tensor", w2tok[j])
                        for k in range(8):
                            dib = qq * 8 + k
                            ins = te and te.matmul(sl.ap, w2qs[qq][:, k * 128:(k + 1) * 128],
                                                   h1[:, dib * 512:(dib + 1) * 512],
                                                   start=(dib == 0), stop=(dib == 31))
                            if k == 7:
                                if qq == 3:
                                    sl.set_ready(w, "pe", ins)
                                else:
                                    w.inc("pe", ins)
                                w2slot[qq].freed = ("pe", w.c["pe"])
                    if do2 < 7:
                        for qq in range(4):
                            w2_quarter(do2 + 1, qq, w2tok)
                    sl.pre_read(w, "vector")
                    dst = bufr[:, rcol(do2, tt * 512):rcol(do2, tt * 512) + 512]
                    ins = ve and ve.scalar_tensor_tensor(out=dst, in0=sl.ap,
                                                         scalar=b2[:, do2:do2 + 1],
                                                         in1=dst, op0=ALU.add, op1=ALU.add)
                    sl.set_freed(w, "dve", ins)
                h1slot.freed = ("pe", w.c["pe"])
            if "ffn" in dbg:
                w.wait("sync", "dve")
                w.dma("out", sy and sy.dma_start(out=dbg["ffn"], in_=bufr))
                w.wait_all("vector", "out")
            if "bt16b" in dbg:
                w.wait("sync", "dve")
                w.dma("out", sy and sy.dma_start(out=dbg["bt16b"], in_=bt16))
                w.wait_all("vector", "out")
                w.wait_all("scalar", "out")
                w.wait_all("gpsimd", "out")
            if "h1f" in dbg:
                w.wait("sync", "act")
                w.dma("out", sy and sy.dma_start(out=dbg["h1f"], in_=h1))
                w.wait_all("gpsimd", "out")
                w.wait_all("scalar", "out")

            w.wait("gpsimd", "pe")
            w.wait("gpsimd", "dve")
            if cv_state["sttok"][0] is not None:
                w.wait_tok("gpsimd", cv_state["sttok"][0])
            if cv_state["sttok"][1] is not None:
                w.wait_tok("gpsimd", cv_state["sttok"][1])
            upslot = [Slot(), Slot()]
            gpslot = [Slot(), Slot()]
            dwslot = [Slot(), Slot()]
            tuslot = [Slot(), Slot()]
            tdslot = [Slot(), Slot()]
            h16slot = Slot()
            combt_slot = Slot()
            for th in range(2):
                tbase = th * 1024
                combt_slot.pre_write(w, "vector")
                for tb8 in range(8):
                    tbg = th * 8 + tb8
                    sl = PF[6]
                    sl.pre_write(w, "tensor")
                    w.wait("tensor", "dve")
                    ins = te and te.transpose(pf[6][0:8, 0:128], comb[:, tbg * 8:(tbg + 1) * 8], id32)
                    sl.set_ready(w, "pe", ins)
                    sl.pre_read(w, "vector")
                    ins = ve and ve.tensor_copy(out=combt[0:8, tb8 * 128:(tb8 + 1) * 128],
                                                in_=pf[6][0:8, 0:128])
                    sl.set_freed(w, "dve", ins)
                combt_dve = w.c["dve"]
                for e in range(E):
                    for tb2 in range(2):
                        sl = PF[6]
                        sl.pre_write(w, "tensor")
                        w.wait("tensor", "dve", combt_dve)
                        ins = te and te.matmul(pf[6], sel[:, e * 128:(e + 1) * 128],
                                               combt[0:8, tb2 * 512:(tb2 + 1) * 512],
                                               start=True, stop=True)
                        sl.set_ready(w, "pe", ins)
                        sl.pre_read(w, "vector")
                        ins = ve and ve.tensor_copy(out=cbuf[:, tb2 * 512:(tb2 + 1) * 512], in_=pf[6])
                        sl.set_freed(w, "dve", ins)
                    cb_dve = w.c["dve"]
                    h16slot.pre_write(w, "scalar")
                    for fb2 in range(NFB // 2):
                        s = fb2 % 2
                        upslot[s].pre_write(w, "gpsimd")
                        tok_u = w.dma("wdm", gp and gp.dma_start(
                            out=upw[s].rearrange("p (b n) -> p b n", b=NDB),
                            in_=upt[e].rearrange("(b p) f -> p b f", p=128)[:, :, fb2 * 256:(fb2 + 1) * 256]))
                        gpslot[s].pre_write(w, "gpsimd")
                        tok_f = w.dma("wdm", gp and gp.dma_start(
                            out=gpw[s].rearrange("p (b n) -> p b n", b=NDB),
                            in_=gpt[e].rearrange("(b p) f -> p b f", p=128)[:, :, fb2 * 256:(fb2 + 1) * 256]))
                        w.wait_tok("tensor", tok_u)
                        w.wait_tok("tensor", tok_f)
                        for k in range(2):
                            fb = fb2 * 2 + k
                            for tb2 in range(2):
                                r3 = (fb * 2 + tb2) % 3
                                slu = PF[(0, 1, 4)[r3]]
                                slg = PF[(2, 3, 5)[r3]]
                                slu.pre_write(w, "tensor")
                                slg.pre_write(w, "tensor")
                                for db in range(NDB):
                                    c0 = xcol(db, HALO + tbase + tb2 * 512)
                                    ins = te and te.matmul(
                                        slu.ap, upw[s][:, db * 256 + k * 128:db * 256 + k * 128 + 128],
                                        xnt16[:, c0:c0 + 512],
                                        start=(db == 0), stop=(db == 7))
                                    if db == 7:
                                        slu.set_ready(w, "pe", ins)
                                for db in range(NDB):
                                    c0 = xcol(db, HALO + tbase + tb2 * 512)
                                    ins = te and te.matmul(
                                        slg.ap, gpw[s][:, db * 256 + k * 128:db * 256 + k * 128 + 128],
                                        xnt16[:, c0:c0 + 512],
                                        start=(db == 0), stop=(db == 7))
                                    if db == 7:
                                        slg.set_ready(w, "pe", ins)
                                slu.pre_read(w, "scalar")
                                tuslot[tb2].pre_write(w, "scalar")
                                ins = se and se.activation(out=tu[tb2], in_=slu.ap, func=AF.Identity,
                                                           bias=upb[:, e * NFB + fb:e * NFB + fb + 1])
                                slu.set_freed(w, "act", ins)
                                a_u = w.c["act"]
                                slg.pre_read(w, "vector")
                                w.wait("vector", "act", a_u)
                                d_m = w.inc("dve", ve and ve.scalar_tensor_tensor(
                                    out=tu[tb2], in0=slg.ap,
                                    scalar=gpb[:, e * NFB + fb:e * NFB + fb + 1],
                                    in1=tu[tb2], op0=ALU.add, op1=ALU.mult))
                                slg.freed = ("dve", d_m)
                                w.wait("scalar", "dve", d_m)
                                o0 = fb * 1024 + tb2 * 512
                                a_h = w.inc("act", se and se.activation(out=h16[:, o0:o0 + 512],
                                                                        in_=tu[tb2], func=AF.Silu))
                                tuslot[tb2].freed = ("act", a_h)
                        upslot[s].freed = ("pe", w.c["pe"])
                        gpslot[s].freed = ("pe", w.c["pe"])
                    h_act = w.c["act"]
                    w.wait("vector", "dve", cb_dve)
                    for do in range(NDB):
                        s = do % 2
                        dwslot[s].pre_write(w, "gpsimd")
                        tok_d = w.dma("wdm", gp and gp.dma_start(
                            out=dwts[s].rearrange("p (b n) -> p b n", b=NFB),
                            in_=dwt[e].rearrange("(b p) n -> p b n", p=128)[:, :, do * 128:(do + 1) * 128]))
                        w.wait_tok("tensor", tok_d)
                        w.wait("tensor", "act", h_act)
                        for tb2 in range(2):
                            sl = PF[4 + tb2]
                            sl.pre_write(w, "tensor")
                            for fb in range(NFB):
                                o0 = fb * 1024 + tb2 * 512
                                ins = te and te.matmul(sl.ap, dwts[s][:, fb * 128:(fb + 1) * 128],
                                                       h16[:, o0:o0 + 512],
                                                       start=(fb == 0), stop=(fb == NFB - 1))
                                if fb == NFB - 1:
                                    sl.set_ready(w, "pe", ins)
                            sl.pre_read(w, "vector")
                            tdslot[tb2].pre_write(w, "vector")
                            ins = ve and ve.scalar_tensor_tensor(
                                out=td[tb2], in0=sl.ap,
                                scalar=dnb[:, e * 8 + do:e * 8 + do + 1],
                                in1=cbuf[:, tb2 * 512:(tb2 + 1) * 512],
                                op0=ALU.add, op1=ALU.mult)
                            sl.set_freed(w, "dve", ins)
                            dst = bufr[:, rcol(do, tbase + tb2 * 512):rcol(do, tbase + tb2 * 512) + 512]
                            d_add = w.inc("dve", ve and ve.tensor_add(dst, dst, td[tb2]))
                            tdslot[tb2].freed = ("dve", d_add)
                        dwslot[s].freed = ("pe", w.c["pe"])
                    h16slot.freed = ("pe", w.c["pe"])
                combt_slot.freed = ("pe", w.c["pe"])
            if "moe" in dbg:
                w.wait("sync", "dve")
                w.dma("out", sy and sy.dma_start(out=dbg["moe"], in_=bufr))
            if "h16" in dbg:
                w.wait("sync", "act")
                w.dma("out", sy and sy.dma_start(out=dbg["h16"], in_=h16))

            res_dve = w.c["dve"]
            w.wait("sync", "dve", res_dve)
            fslot = [Slot(), Slot()]
            oslot = [Slot(), Slot()]
            for i in range(16):
                s = i % 2
                fslot[s].pre_write(w, "sync")
                tok_xf = w.dma("xin", sy and sy.dma_start(out=xt[s], in_=xc[(2 + i) * 128:(3 + i) * 128, :]))
                for half in range(2):
                    sl = PF[half]
                    sl.pre_write(w, "tensor")
                    w.wait("tensor", "dve", res_dve)
                    for k in range(4):
                        db = half * 4 + k
                        c0 = rcol(db, i * 128)
                        ins = te and te.transpose(pf[half][:, k * 128:(k + 1) * 128],
                                                  bufr[:, c0:c0 + 128], id32)
                        if k == 3:
                            sl.set_ready(w, "pe", ins)
                oslot[s].pre_write(w, "vector")
                w.wait_tok("vector", tok_xf)
                for half in range(2):
                    sl = PF[half]
                    sl.pre_read(w, "vector")
                    ins = ve and ve.tensor_add(fot[s][:, half * 512:(half + 1) * 512],
                                               pf[half][:, 0:512],
                                               xt[s][:, half * 512:(half + 1) * 512])
                    sl.set_freed(w, "dve", ins)
                d_out = w.c["dve"]
                fslot[s].freed = ("dve", d_out)
                w.wait("sync", "dve", d_out)
                ins = sy and sy.dma_start(out=y[i * 128:(i + 1) * 128, :], in_=fot[s])
                oslot[s].freed = w.dma("out", ins)
            w.wait_all("sync", "out")

        engs = {}

        @block.tensor
        def _(eng):
            engs["tensor"] = eng
            flow(Wk("tensor", engs, semh))

        @block.vector
        def _(eng):
            engs["vector"] = eng
            flow(Wk("vector", engs, semh))

        @block.scalar
        def _(eng):
            engs["scalar"] = eng
            flow(Wk("scalar", engs, semh))

        @block.gpsimd
        def _(eng):
            engs["gpsimd"] = eng
            flow(Wk("gpsimd", engs, semh))

        @block.sync
        def _(eng):
            engs["sync"] = eng
            flow(Wk("sync", engs, semh))

    return nc


def _prep_inputs(L0_attn_x, L0_attn_Wq, L0_attn_bq, L0_attn_Wk, L0_attn_bk,
                 L0_attn_Wv, L0_attn_bv, L0_attn_Wo, L0_attn_bo,
                 L0_attn_gamma, L0_attn_beta, L0_attn_W1, L0_attn_b1,
                 L0_attn_W2, L0_attn_b2, L0_ffn_gate_w, L0_ffn_gate_b,
                 L0_ffn_up_w, L0_ffn_up_b, L0_ffn_gp_w, L0_ffn_gp_b,
                 L0_ffn_down_w, L0_ffn_down_b):
    f = lambda a: np.ascontiguousarray(np.asarray(a, dtype=np.float32))
    x = f(L0_attn_x).reshape(16384, D)
    xpad = np.zeros((16384 + 2 * HALO, D), np.float32)
    xpad[HALO:HALO + 16384] = x

    colb = lambda v, nb: np.ascontiguousarray(f(v).reshape(nb, 128).T)
    wq_, wk_, wv_, wo_ = f(L0_attn_Wq), f(L0_attn_Wk), f(L0_attn_Wv), f(L0_attn_Wo)
    w1_, w2_ = f(L0_attn_W1), f(L0_attn_W2)
    upt = np.zeros((E, D, FPAD), np.float32)
    upt[:, :, :F] = np.transpose(f(L0_ffn_up_w), (0, 2, 1))
    gpt = np.zeros((E, D, FPAD), np.float32)
    gpt[:, :, :F] = np.transpose(f(L0_ffn_gp_w), (0, 2, 1))
    dwt = np.zeros((E, FPAD, D), np.float32)
    dwt[:, :F, :] = np.transpose(f(L0_ffn_down_w), (0, 2, 1))
    upb = np.zeros((E, FPAD), np.float32); upb[:, :F] = f(L0_ffn_up_b)
    gpb = np.zeros((E, FPAD), np.float32); gpb[:, :F] = f(L0_ffn_gp_b)
    upb_re = np.ascontiguousarray(np.transpose(upb.reshape(E, NFB, 128), (2, 0, 1)).reshape(128, E * NFB))
    gpb_re = np.ascontiguousarray(np.transpose(gpb.reshape(E, NFB, 128), (2, 0, 1)).reshape(128, E * NFB))
    dnb_re = np.ascontiguousarray(np.transpose(f(L0_ffn_down_b).reshape(E, 8, 128), (2, 0, 1)).reshape(128, E * 8))
    gw = f(L0_ffn_gate_w)
    gwt_re = np.ascontiguousarray(np.transpose(gw.T.reshape(8, 128, E), (1, 0, 2)).reshape(128, 8 * E))
    gbb = np.ascontiguousarray(np.tile(f(L0_ffn_gate_b)[None, :], (128, 1)))
    bvbc = np.ascontiguousarray(np.tile(f(L0_attn_bv)[None, :], (128, 1)))
    selm = np.zeros((8, 1024), np.float32)
    for e in range(8):
        selm[e, e * 128:(e + 1) * 128] = 1.0

    qi = np.arange(256)[:, None]
    kj = np.arange(768)[None, :]
    band = (kj >= qi) & (kj <= qi + 512)
    mb = np.where(band, 0.0, NEG).astype(np.float32)
    m_int = np.concatenate([mb[0:128], mb[128:256]], axis=1)
    kill_first = np.where(np.arange(768) < 256, NEG, 0.0).astype(np.float32)
    kill_last = np.where(np.arange(768) >= 512, NEG, 0.0).astype(np.float32)

    per_core = []
    for c in range(8):
        m_e0 = m_int.copy()
        m_e7 = m_int.copy()
        if c == 0:
            for qh in range(2):
                m_e0[:, qh * 768:(qh + 1) * 768] = np.minimum(
                    m_e0[:, qh * 768:(qh + 1) * 768], kill_first[None, :])
        if c == 7:
            for qh in range(2):
                m_e7[:, qh * 768:(qh + 1) * 768] = np.minimum(
                    m_e7[:, qh * 768:(qh + 1) * 768], kill_last[None, :])
        per_core.append(dict(
            xc=np.ascontiguousarray(xpad[c * T:c * T + TH]),
            wq=wq_, wk=wk_, wv=wv_, wo=wo_, w1=w1_, w2=w2_,
            upt=upt, gpt=gpt, dwt=dwt,
            bq=colb(L0_attn_bq, 8), bk=colb(L0_attn_bk, 8),
            bo=colb(L0_attn_bo, 8), b2=colb(L0_attn_b2, 8),
            gamma=colb(L0_attn_gamma, 8), beta=colb(L0_attn_beta, 8),
            b1=colb(L0_attn_b1, 32),
            gwt=gwt_re, gbb=gbb, upb=upb_re, gpb=gpb_re, dnb=dnb_re,
            bvbc=bvbc, sel=selm,
            maskb=m_int.astype(BFNP), maske0=m_e0.astype(BFNP),
            maske7=m_e7.astype(BFNP),
        ))
    return per_core


def kernel(**inputs):
    in_maps = _prep_inputs(**inputs)
    nc = build_program()
    res = run_bass_kernel_spmd(nc, in_maps, core_ids=list(range(8)))
    out = np.concatenate([np.asarray(r["y"]) for r in res.results], axis=0)
    return out.reshape(1, 16384, D).astype(np.float32)


# version 15
# speedup vs baseline: 1.2440x; 1.0071x over previous
import numpy as np
import ml_dtypes
import concourse.bass as bass
import concourse.mybir as mybir
from concourse.bass_utils import run_bass_kernel_spmd

F32 = mybir.dt.float32
BF16 = mybir.dt.bfloat16
U8 = mybir.dt.uint8
AF = mybir.ActivationFunctionType
ALU = mybir.AluOpType
AX = mybir.AxisListType.X
BFNP = ml_dtypes.bfloat16

D = 1024
NDB = 8
T = 2048
HALO = 256
TH = T + 2 * HALO
NTILES = TH // 128
H = 8
E = 8
F = 3264
FPAD = 3328
NFB = FPAD // 128
DFF = 4096
SCALE = float(1.0 / np.sqrt(128.0))
NEG = -1.0e30
EPS = 1.0e-5

P_ID32 = 0
P_ID16 = 512
P_ONES32 = 768
P_ONESC32 = 1280
P_BQ = 1312
P_BK = 1344
P_BO = 1376
P_B2 = 1408
P_GAMMA = 1440
P_BETA = 1472
P_B1 = 1504
P_GWT = 1632
P_GBB = 1888
P_UPB = 1920
P_GPB = 2752
P_DNB = 3584
P_COMB = 3840
P_SMALL = 4352
P_SEL = 4416
P_XNT16 = 8512
P_BUFR = 49472
S = 115008
A_XT0 = S + 0
A_XT1 = S + 4096
A_XN0 = S + 8192
A_XN1 = S + 12288
A_XN16_0 = S + 16384
A_XN16_1 = S + 18432
A_G8 = S + 20480
A_P8 = A_G8 + 32
A_PRB = A_G8 + 64
A_TOP8 = A_G8 + 96
A_MSK = A_G8 + 128
A_CW = A_G8 + 160
A_ST6 = A_G8 + 192
FI_OT0 = S + 21504
FI_OT1 = S + 25600
AT_V16 = S + 0
AT_BVBC = S + 24576
AT_MB = S + 28672
AT_ME0 = S + 31744
AT_ME7 = S + 34816
AT_OTG = S + 37888
AT_KT = S + 54272
AT_Q16 = S + 60416
AT_WSL0 = S + 64512
AT_WSL1 = S + 68608
AT_PN0 = S + 72704
AT_PN1 = S + 75776
AT_PEXPA = S + 78848
AT_PEXPB = S + 80384
AT_PTA = S + 81920
AT_PTB = S + 83456
L_SUM = S + 0
L_SQ = S + 8192
L_TA = S + 20480
L_TMP = S + 70656
FF_W1SL = S + 0
FF_W2SL = S + 16384
FF_BT16 = S + 24576
FF_H1 = S + 57344
M_COMBT = S + 0
M_CB = S + 4096
M_TU = S + 6144
M_TD = S + 10240
M_UPW = S + 14336
M_GPW = S + 22528
M_DWT = S + 30720
M_H16 = S + 44032
ARENA = S + 97280

DMAQ = {"xin": 6, "wdm": 8, "out": 3, "cst": 1, "cvl": 2, "cvs": 2, "wsc": 3, "wve": 3}
SEMS = tuple(f"{q}{j}" for q, n in DMAQ.items() for j in range(n)) + ("pe", "dve", "act", "gps")


class Wk:
    def __init__(self, which, engs, sems):
        self.which = which
        self.engs = engs
        self.sems = sems
        self.c = {k: 0 for k in sems}
        self.hw = {}
        self.qidx = {q: 0 for q in DMAQ}

    def dma(self, q, instr):
        i = self.qidx[q]
        self.qidx[q] = i + 1
        ns = DMAQ[q]
        sname = f"{q}{i % ns}"
        v = 16 * (i // ns + 1)
        self.c[sname] = v
        if instr is not None:
            instr.then_inc(self.sems[sname], 16)
        return (sname, v)

    def wait_tok(self, engname, tok):
        self.wait(engname, tok[0], tok[1])

    def wait_all(self, engname, q):
        ns = DMAQ[q]
        tot = self.qidx[q]
        for j in range(ns):
            uses = (tot - j + ns - 1) // ns
            if uses > 0:
                self.wait(engname, f"{q}{j}", 16 * uses)

    def eng(self, name):
        return self.engs[name] if self.which == name else None

    SELFW = {"dve": "vector", "act": "scalar", "gps": "gpsimd"}

    def inc(self, sem, instr, k=1):
        self.c[sem] += k
        if instr is not None:
            instr.then_inc(self.sems[sem], k)
        eng = self.SELFW.get(sem)
        if eng is not None and self.which == eng:
            v = self.c[sem]
            if self.hw.get(sem, -1) < v:
                self.hw[sem] = v
                self.engs[eng].wait_ge(self.sems[sem], v)
        return self.c[sem]

    def wait(self, engname, sem, val=None):
        v = self.c[sem] if val is None else val
        if v <= 0:
            return
        if self.which == engname:
            key = sem
            if self.hw.get(key, -1) >= v:
                return
            self.hw[key] = v
            self.engs[engname].wait_ge(self.sems[sem], v)


class Slot:
    def __init__(self, ap=None):
        self.ap = ap
        self.ready = None
        self.freed = None

    def pre_write(self, w, eng):
        if self.freed is not None:
            w.wait(eng, *self.freed)

    def set_ready(self, w, sem, instr, k=1):
        self.ready = (sem, w.inc(sem, instr, k))

    def pre_read(self, w, eng):
        if self.ready is not None:
            w.wait(eng, *self.ready)

    def set_freed(self, w, sem, instr, k=1):
        self.freed = (sem, w.inc(sem, instr, k))


def build_program(debug=()):
    nc = bass.Bass("TRN2", target_bir_lowering=False)

    def din(name, shape, dt=F32):
        return nc.dram_tensor(name, list(shape), dt, kind="ExternalInput").ap()

    xc = din("xc", [TH, D])
    wq = din("wq", [D, D]); wk = din("wk", [D, D])
    wv = din("wv", [D, D]); wo = din("wo", [D, D])
    w1 = din("w1", [D, DFF]); w2 = din("w2", [DFF, D])
    upt = din("upt", [E, D, FPAD]); gpt = din("gpt", [E, D, FPAD])
    dwt = din("dwt", [E, FPAD, D])
    bq_i = din("bq", [128, 8]); bk_i = din("bk", [128, 8])
    bo_i = din("bo", [128, 8]); b2_i = din("b2", [128, 8])
    gam_i = din("gamma", [128, 8]); bet_i = din("beta", [128, 8])
    b1_i = din("b1", [128, 32])
    gwt_i = din("gwt", [128, 64]); gbb_i = din("gbb", [128, 8])
    upb_i = din("upb", [128, E * NFB]); gpb_i = din("gpb", [128, E * NFB])
    dnb_i = din("dnb", [128, E * 8])
    bvbc_i = din("bvbc", [128, D])
    sel_i = din("sel", [8, 1024])
    mb_i = din("maskb", [128, 1536], BF16)
    me0_i = din("maske0", [128, 1536], BF16)
    me7_i = din("maske7", [128, 1536], BF16)

    w1b = nc.dram_tensor("w1b", [D * DFF], BF16, kind="Internal").ap()
    w2b = nc.dram_tensor("w2b", [DFF * D], BF16, kind="Internal").ap()
    y = nc.dram_tensor("y", [T, D], F32, kind="ExternalOutput").ap()
    dbg = {}
    for name in debug:
        if name == "xnt":
            dbg[name] = nc.dram_tensor("d_xnt", [128, NDB * TH], BF16, kind="ExternalOutput").ap()
        elif name in ("xnTf", "aT", "bT", "ffn", "moe"):
            dbg[name] = nc.dram_tensor("d_" + name, [128, NDB * T], F32, kind="ExternalOutput").ap()
        elif name == "comb":
            dbg[name] = nc.dram_tensor("d_comb", [128, 128], F32, kind="ExternalOutput").ap()
        elif name == "gate8":
            dbg[name] = nc.dram_tensor("d_gate8", [128, 48], F32, kind="ExternalOutput").ap()
        elif name in ("bt16", "bt16b"):
            dbg[name] = nc.dram_tensor("d_" + name, [128, NDB * T], BF16, kind="ExternalOutput").ap()
        elif name == "h1f":
            dbg[name] = nc.dram_tensor("d_h1f", [128, 32 * 512], BF16, kind="ExternalOutput").ap()
        elif name == "h16":
            dbg[name] = nc.dram_tensor("d_h16", [128, NFB * 1024], BF16, kind="ExternalOutput").ap()

    arena = nc.alloc_sbuf_tensor("arena", [128, ARENA], U8)
    arena_addr = nc.lookup_mloc(arena).addr

    def place(name, off, shape, dt):
        h = nc.virtual_sbuf_tensor(name, list(shape), dt)
        nc._tensor(h.name, list(shape), dt, type="SB")
        mloc = nc.lookup_mloc(h)
        mloc.allocated = True
        mloc.addr = arena_addr + off
        return h.ap()

    id32 = place("id32", P_ID32, [128, 128], F32)
    id16 = place("id16", P_ID16, [128, 128], BF16)
    ones32 = place("ones32", P_ONES32, [1, 128], F32)
    onesc32 = place("onesc32", P_ONESC32, [128, 1], F32)
    bq = place("bq_s", P_BQ, [128, 8], F32)
    bk = place("bk_s", P_BK, [128, 8], F32)
    bo = place("bo_s", P_BO, [128, 8], F32)
    b2 = place("b2_s", P_B2, [128, 8], F32)
    gamma = place("gamma_s", P_GAMMA, [128, 8], F32)
    beta = place("beta_s", P_BETA, [128, 8], F32)
    b1 = place("b1_s", P_B1, [128, 32], F32)
    gwt = place("gwt_s", P_GWT, [128, 64], F32)
    gbb = place("gbb_s", P_GBB, [128, 8], F32)
    upb = place("upb_s", P_UPB, [128, E * NFB], F32)
    gpb = place("gpb_s", P_GPB, [128, E * NFB], F32)
    dnb = place("dnb_s", P_DNB, [128, E * 8], F32)
    comb = place("comb_s", P_COMB, [128, 128], F32)
    small = place("small_s", P_SMALL, [128, 16], F32)
    sel = place("sel_s", P_SEL, [8, 1024], F32)
    xnt16 = place("xnt16", P_XNT16, [128, NDB * TH], BF16)
    bufr = place("bufr", P_BUFR, [128, NDB * T], F32)
    xt = [place("xt0", A_XT0, [128, D], F32), place("xt1", A_XT1, [128, D], F32)]
    xn32 = [place("xn0", A_XN0, [128, D], F32), place("xn1", A_XN1, [128, D], F32)]
    xn16 = [place("xn16_0", A_XN16_0, [128, D], BF16), place("xn16_1", A_XN16_1, [128, D], BF16)]
    g8 = place("g8_s", A_G8, [128, 8], F32)
    g8all = place("g8all", FI_OT0, [128, 128], F32)
    gsm = place("gsm", FI_OT0 + 512, [128, 64], F32)
    p8 = place("p8_s", A_P8, [128, 8], F32)
    prb = place("prb_s", A_PRB, [128, 8], F32)
    top8 = place("top8_s", A_TOP8, [128, 8], F32)
    mskg = place("mskg_s", A_MSK, [128, 8], F32)
    cw8 = place("cw8_s", A_CW, [128, 8], F32)
    st6 = place("st6_s", A_ST6, [128, 12], F32)
    fot = [place("fot0", FI_OT0, [128, D], F32), place("fot1", FI_OT1, [128, D], F32)]
    v16 = place("v16", AT_V16, [128, 12 * 1024], BF16)
    bvbc = place("bvbc_s", AT_BVBC, [128, D], F32)
    maskb = place("maskb_s", AT_MB, [128, 1536], BF16)
    maske0 = place("maske0_s", AT_ME0, [128, 1536], BF16)
    maske7 = place("maske7_s", AT_ME7, [128, 1536], BF16)
    otg = place("otg", AT_OTG, [128, H * 1024], BF16)
    kt16 = place("kt16", AT_KT, [128, 2 * 1536], BF16)
    q16 = place("q16", AT_Q16, [128, 2 * 1024], BF16)
    wsl = [place("wsl0", AT_WSL0, [128, 2048], BF16), place("wsl1", AT_WSL1, [128, 2048], BF16)]
    pn = [place("pn0", AT_PN0, [128, 768], F32), place("pn1", AT_PN1, [128, 768], F32)]
    pexp = [place("pexpA", AT_PEXPA, [128, 768], BF16), place("pexpB", AT_PEXPB, [128, 768], BF16)]
    pt16 = [place("ptA", AT_PTA, [128, 768], BF16), place("ptB", AT_PTB, [128, 768], BF16)]
    lsum = place("lsum", L_SUM, [1, 2048], F32)
    lsq = place("lsq", L_SQ, [1, 2048], F32)
    ltmp = place("ltmp", L_TMP, [1, 2048], F32)
    lta = [place("lta0", L_TA, [128, 512], F32), place("lta1", L_TA + 2048, [128, 512], F32)]
    w1sl = [place("w1sl0", FF_W1SL, [128, 4096], BF16), place("w1sl1", FF_W1SL + 8192, [128, 4096], BF16)]
    w2qs = [place(f"w2q{i}", FF_W2SL + i * 2048, [128, 1024], BF16) for i in range(4)]
    bt16 = place("bt16", FF_BT16, [128, NDB * T], BF16)
    h1 = place("h1", FF_H1, [128, 32 * 512], BF16)
    combt = place("combt", M_COMBT, [8, 1024], F32)
    cbuf = place("cbuf", M_CB, [128, 1024], BF16)
    tu = [place("tu0", M_TU, [128, 512], F32), place("tu1", M_TU + 2048, [128, 512], F32)]
    td = [place("td0", M_TD, [128, 512], F32), place("td1", M_TD + 2048, [128, 512], F32)]
    upw = [place("upw0", M_UPW, [128, 2048], BF16), place("upw1", M_UPW + 4096, [128, 2048], BF16)]
    cvb = [place("cvb0", S + 84992, [128, 2048], BF16), place("cvb1", S + 89088, [128, 2048], BF16)]
    gpw = [place("gpw0", M_GPW, [128, 2048], BF16), place("gpw1", M_GPW + 4096, [128, 2048], BF16)]
    dwts = [place("dwt0", M_DWT, [128, 3328], BF16), place("dwt1", M_DWT + 6656, [128, 3328], BF16)]
    h16 = place("h16", M_H16, [128, NFB * 1024], BF16)

    pf = [nc.alloc_psum_tensor(f"pf{i}", [128, 512], F32).ap() for i in range(7)]
    pb = nc.alloc_psum_tensor("pb", [128, 1024], BF16).ap()

    from contextlib import ExitStack
    with ExitStack() as ctx:
        semh = {s: ctx.enter_context(nc.semaphore(s)) for s in SEMS}
        block = ctx.enter_context(nc.Block())

        def flow(w: Wk):
            te = w.eng("tensor"); ve = w.eng("vector"); se = w.eng("scalar")
            gp = w.eng("gpsimd"); sy = w.eng("sync")
            PF = [Slot(p) for p in pf]
            PB = Slot(pb)
            xcol = lambda db, t0: db * TH + t0
            rcol = lambda db, t0: db * T + t0

            for dst, src in ((bq, bq_i), (bk, bk_i), (bo, bo_i), (b2, b2_i),
                             (gamma, gam_i), (beta, bet_i), (b1, b1_i),
                             (gwt, gwt_i), (gbb, gbb_i), (upb, upb_i),
                             (gpb, gpb_i), (dnb, dnb_i), (bvbc, bvbc_i),
                             (sel, sel_i), (maskb, mb_i), (maske0, me0_i),
                             (maske7, me7_i)):
                w.dma("cst", sy and sy.dma_start(out=dst, in_=src))
            w.inc("gps", gp and gp.iota(xn32[0][:, 0:128], pattern=[[1, 128]], base=0,
                                        channel_multiplier=-1,
                                        allow_small_or_imprecise_dtypes=True), 1)
            w.wait("vector", "gps")
            w.inc("dve", ve and ve.memset(small[:, 15:16], EPS))
            w.inc("dve", ve and ve.tensor_scalar(out=id32, in0=xn32[0][:, 0:128],
                                                 scalar1=0.0, scalar2=None,
                                                 op0=ALU.is_equal))
            w.inc("dve", ve and ve.tensor_copy(out=id16, in_=id32))
            w.inc("dve", ve and ve.memset(ones32, 1.0))
            w.inc("dve", ve and ve.memset(onesc32, 1.0))
            w.inc("dve", ve and ve.tensor_copy(out=small[:, 14:15], in_=small[:, 15:16]))
            ident_dve = w.c["dve"]

            xslot = [Slot(), Slot()]
            xnslot = [Slot(), Slot()]
            xn16slot = [Slot(), Slot()]
            w.wait("scalar", "dve", ident_dve)
            cv_jobs = []
            for j in range(16):
                cv_jobs.append((w1.rearrange("a b -> (a b)"), w1b, j))
            for j in range(16):
                cv_jobs.append((w2.rearrange("a b -> (a b)"), w2b, j))
            cv_state = {"i": 0, "tok": [None, None], "sttok": [None, None]}

            def emit_convert(n):
                CHUNK = 128 * 2048
                while cv_jobs and n > 0:
                    srcf, dstb, j = cv_jobs.pop(0)
                    i = cv_state["i"]; cv_state["i"] += 1
                    sl = i % 2
                    if cv_state["sttok"][sl] is not None:
                        w.wait_tok("gpsimd", cv_state["sttok"][sl])
                    cv_state["tok"][sl] = w.dma("cvl", gp and gp.dma_start(
                        out=cvb[sl],
                        in_=srcf[j * CHUNK:(j + 1) * CHUNK].rearrange("(p n) -> p n", p=128)))
                    w.wait_tok("sync", cv_state["tok"][sl])
                    cv_state["sttok"][sl] = w.dma("cvs", sy and sy.dma_start(
                        out=dstb[j * CHUNK:(j + 1) * CHUNK].rearrange("(p n) -> p n", p=128),
                        in_=cvb[sl]))
                    n -= 1
            w.wait_all("vector", "cst")
            w.wait_all("scalar", "cst")
            w.wait_all("tensor", "cst")
            for i in range(NTILES):
                s = i % 2
                xslot[s].pre_write(w, "sync")
                tok_x = w.dma("xin", sy and sy.dma_start(out=xt[s], in_=xc[i * 128:(i + 1) * 128, :]))
                emit_convert(1)
                w.wait_tok("vector", tok_x)
                w.wait_tok("scalar", tok_x)
                xnslot[s].pre_write(w, "vector")
                xnslot[s].pre_write(w, "scalar")
                w.inc("dve", ve and ve.reduce_sum(out=small[:, 0:1], in_=xt[s], axis=AX))
                a_sq = w.inc("act", se and se.activation(out=xn32[s], in_=xt[s], func=AF.Square,
                                                         accum_out=small[:, 1:2]))
                w.inc("dve", ve and ve.tensor_scalar_mul(small[:, 2:3], small[:, 0:1], 1.0 / D))
                w.inc("dve", ve and ve.tensor_mul(small[:, 4:5], small[:, 2:3], small[:, 2:3]))
                w.wait("vector", "act", a_sq)
                w.inc("dve", ve and ve.tensor_scalar_mul(small[:, 3:4], small[:, 1:2], 1.0 / D))
                d_var = w.inc("dve", ve and ve.tensor_sub(small[:, 3:4], small[:, 3:4], small[:, 4:5]))
                w.wait("scalar", "dve", d_var)
                w.inc("act", se and se.activation(out=small[:, 5:6], in_=small[:, 3:4], func=AF.Ln, bias=small[:, 15:16]))
                a_rstd = w.inc("act", se and se.activation(out=small[:, 6:7], in_=small[:, 5:6],
                                                           func=AF.Exp, scale=-0.5))
                w.wait("vector", "act", a_rstd)
                d_xn = w.inc("dve", ve and ve.tensor_scalar(out=xn32[s], in0=xt[s],
                                                            scalar1=small[:, 2:3],
                                                            scalar2=small[:, 6:7],
                                                            op0=ALU.subtract, op1=ALU.mult))
                xslot[s].freed = ("dve", d_xn)
                w.wait("scalar", "dve", d_xn)
                xn16slot[s].pre_write(w, "scalar")
                a_xn16 = w.inc("act", se and se.copy(out=xn16[s], in_=xn32[s]))
                is_center = 2 <= i < 18
                tb = i - 2
                w.wait("tensor", "act", a_xn16)
                w.wait("tensor", "dve", d_xn)
                if is_center:
                    for half in range(2):
                        PF[half].pre_write(w, "tensor")
                        for k in range(4):
                            db = half * 4 + k
                            ins = te and te.transpose(pf[half][:, k * 128:(k + 1) * 128],
                                                      xn32[s][:, db * 128:(db + 1) * 128], id32)
                            if k == 3:
                                PF[half].set_ready(w, "pe", ins)
                    xnslot[s].freed = ("pe", w.c["pe"])
                    for half in range(2):
                        PF[half].pre_read(w, "scalar")
                        dst = bufr.rearrange("p (b t) -> p b t", b=NDB)[:, half * 4:(half + 1) * 4,
                                                                       tb * 128:(tb + 1) * 128]
                        src = pf[half].rearrange("p (k t) -> p k t", k=4)
                        ins = se and se.copy(out=dst, in_=src)
                        PF[half].set_freed(w, "act", ins)
                PB.pre_write(w, "tensor")
                for db in range(NDB):
                    ins = te and te.transpose(pb[:, db * 128:(db + 1) * 128],
                                              xn16[s][:, db * 128:(db + 1) * 128], id16)
                    if db == 7:
                        PB.set_ready(w, "pe", ins)
                xn16slot[s].freed = ("pe", w.c["pe"])
                if not is_center:
                    xnslot[s].freed = ("pe", w.c["pe"])
                PB.pre_read(w, "scalar")
                dstx = xnt16.rearrange("p (b t) -> p b t", b=NDB)[:, :, i * 128:(i + 1) * 128]
                ins = se and se.copy(out=dstx, in_=pb.rearrange("p (k t) -> p k t", k=NDB))
                PB.set_freed(w, "act", ins)
                if is_center:
                    PF[2].pre_write(w, "tensor")
                    w.wait("tensor", "dve")
                    w.wait("tensor", "act")
                    for db in range(NDB):
                        c0 = rcol(db, tb * 128)
                        ins = te and te.matmul(pf[2][:, 0:8], bufr[:, c0:c0 + 128],
                                               gwt[:, db * 8:(db + 1) * 8],
                                               start=(db == 0), stop=(db == 7))
                        if db == 7:
                            PF[2].set_ready(w, "pe", ins)
                    PF[2].pre_read(w, "vector")
                    d1 = w.inc("dve", ve and ve.tensor_add(g8all[:, tb * 8:(tb + 1) * 8],
                                                           pf[2][:, 0:8], gbb))
                    PF[2].freed = ("dve", d1)

            g3 = g8all.rearrange("p (a b) -> p a b", b=8)
            w.inc("dve", ve and ve.reduce_max(out=gsm[:, 0:16].rearrange("p (a b) -> p a b", b=1),
                                              in_=g3, axis=AX, negate=True))
            w.inc("dve", ve and ve.tensor_tensor(out=g3, in0=g3,
                                                 in1=gsm[:, 0:16].rearrange("p (a b) -> p a b", b=1)
                                                 .to_broadcast([128, 16, 8]),
                                                 op=ALU.add))
            d_g = w.c["dve"]
            w.wait("scalar", "dve", d_g)
            a_g = w.inc("act", se and se.activation(out=g8all, in_=g8all, func=AF.Exp))
            w.wait("vector", "act", a_g)
            w.inc("dve", ve and ve.reduce_sum(out=gsm[:, 16:32].rearrange("p (a b) -> p a b", b=1),
                                              in_=g3, axis=AX))
            w.inc("dve", ve and ve.reciprocal(out=gsm[:, 16:32], in_=gsm[:, 16:32]))
            w.inc("dve", ve and ve.tensor_mul(g3, g3,
                                              gsm[:, 16:32].rearrange("p (a b) -> p a b", b=1)
                                              .to_broadcast([128, 16, 8])))
            w.inc("dve", ve and ve.reduce_max(out=gsm[:, 32:48].rearrange("p (a b) -> p a b", b=1),
                                              in_=g3, axis=AX))
            cm3 = comb.rearrange("p (a b) -> p a b", b=8)
            m1b = gsm[:, 32:48].rearrange("p (a b) -> p a b", b=1).to_broadcast([128, 16, 8])
            w.inc("dve", ve and ve.tensor_tensor(out=cm3, in0=g3, in1=m1b, op=ALU.is_equal))
            w.inc("dve", ve and ve.scalar_tensor_tensor(out=comb, in0=comb, scalar=-2.0,
                                                        in1=g8all, op0=ALU.mult, op1=ALU.add))
            w.inc("dve", ve and ve.reduce_max(out=gsm[:, 48:64].rearrange("p (a b) -> p a b", b=1),
                                              in_=comb.rearrange("p (a b) -> p a b", b=8), axis=AX))
            m2b = gsm[:, 48:64].rearrange("p (a b) -> p a b", b=1).to_broadcast([128, 16, 8])
            w.inc("dve", ve and ve.tensor_add(gsm[:, 32:48], gsm[:, 32:48], gsm[:, 48:64]))
            w.inc("dve", ve and ve.tensor_scalar_add(gsm[:, 32:48], gsm[:, 32:48], 1.0e-20))
            w.inc("dve", ve and ve.reciprocal(out=gsm[:, 32:48], in_=gsm[:, 32:48]))
            rdb = gsm[:, 32:48].rearrange("p (a b) -> p a b", b=1).to_broadcast([128, 16, 8])
            w.inc("dve", ve and ve.tensor_tensor(out=cm3, in0=g3, in1=m2b, op=ALU.is_ge))
            w.inc("dve", ve and ve.tensor_mul(comb, comb, g8all))
            w.inc("dve", ve and ve.tensor_tensor(out=cm3, in0=cm3, in1=rdb, op=ALU.mult))

            if "xnt" in dbg:
                w.wait("sync", "dve")
                w.wait("sync", "act")
                w.dma("out", sy and sy.dma_start(out=dbg["xnt"], in_=xnt16))
            if "xnTf" in dbg:
                w.wait("sync", "dve")
                w.wait("sync", "act")
                w.dma("out", sy and sy.dma_start(out=dbg["xnTf"], in_=bufr))
                w.wait_all("vector", "out")
            if "comb" in dbg:
                w.wait("sync", "dve")
                w.dma("out", sy and sy.dma_start(out=dbg["comb"], in_=comb))
            if "gate8" in dbg:
                w.wait("sync", "dve")
                w.wait("sync", "act")
                w.dma("out", sy and sy.dma_start(out=dbg["gate8"][:, 0:8], in_=g8))
                w.dma("out", sy and sy.dma_start(out=dbg["gate8"][:, 8:16], in_=prb))
                w.dma("out", sy and sy.dma_start(out=dbg["gate8"][:, 16:24], in_=top8))
                w.dma("out", sy and sy.dma_start(out=dbg["gate8"][:, 24:32], in_=mskg))
                w.dma("out", sy and sy.dma_start(out=dbg["gate8"][:, 32:40], in_=cw8))
                w.dma("out", sy and sy.dma_start(out=dbg["gate8"][:, 40:48], in_=small[:, 8:16]))

            w.wait("tensor", "dve")
            w.wait("tensor", "act")
            wslot = [Slot(), Slot()]
            att_it = [0]
            att_prev = [None]
            pexp_free = [0, 0]

            def emit_tr_o(pv):
                osl = pv["osl"]
                h_, cl_ = pv["h"], pv["cl"]
                for qh in range(2):
                    PB.pre_write(w, "tensor")
                    w.wait("tensor", "dve", pv["d_px"][qh])
                    for kc6 in range(6):
                        ins = te and te.transpose(pb[:, kc6 * 128:(kc6 + 1) * 128],
                                                  pexp[qh][:, kc6 * 128:(kc6 + 1) * 128], id16)
                        if kc6 == 5:
                            PB.set_ready(w, "pe", ins)
                    PB.pre_read(w, "scalar")
                    ins = se and se.copy(out=pt16[qh], in_=pb[:, 0:768])
                    PB.set_freed(w, "act", ins)
                    d_pt = w.c["act"]
                    w.wait("tensor", "act", d_pt)
                    for kc6 in range(6):
                        kpb = cl_ * 2 + kc6
                        v0 = kpb * 1024 + h_ * 128
                        ins = te and te.matmul(osl.ap[:, qh * 128:qh * 128 + 128],
                                               v16[:, v0:v0 + 128],
                                               pt16[qh][:, kc6 * 128:(kc6 + 1) * 128],
                                               start=(kc6 == 0), stop=(kc6 == 5))
                        if kc6 == 5:
                            if qh == 1:
                                osl.set_ready(w, "pe", ins)
                            else:
                                w.inc("pe", ins)
                    pexp_free[qh] = w.c["pe"]
                osl.pre_read(w, "scalar")
                o0 = pv["h"] * 1024 + pv["cl"] * 256
                ins = se and se.copy(out=otg[:, o0:o0 + 256], in_=osl.ap[:, 0:256])
                osl.set_freed(w, "act", ins)
            for g in range(2):
                gbase = g * 1024
                cbase = g * 1024
                for qtr in range(4):
                    sq = qtr % 2
                    wslot[sq].pre_write(w, "gpsimd")
                    tok_v = w.dma("wdm", gp and gp.dma_start(
                        out=wsl[sq].rearrange("p (b n) -> p b n", b=NDB),
                        in_=wv.rearrange("(b p) n -> p b n", p=128)[:, :, qtr * 256:(qtr + 1) * 256]))
                    emit_convert(1)
                    w.wait_tok("tensor", tok_v)
                    for pbi in range(12):
                        sl = PF[pbi % 2]
                        sl.pre_write(w, "tensor")
                        for db in range(NDB):
                            c0 = xcol(db, gbase + pbi * 128)
                            ins = te and te.matmul(sl.ap[:, 0:256], xnt16[:, c0:c0 + 128],
                                                   wsl[sq][:, db * 256:(db + 1) * 256],
                                                   start=(db == 0), stop=(db == 7))
                            if db == 7:
                                sl.set_ready(w, "pe", ins)
                        sl.pre_read(w, "vector")
                        o0 = pbi * 1024 + qtr * 256
                        ins = ve and ve.tensor_tensor(out=v16[:, o0:o0 + 256], in0=sl.ap[:, 0:256],
                                                      in1=bvbc[:, qtr * 256:(qtr + 1) * 256],
                                                      op=ALU.add)
                        sl.set_freed(w, "dve", ins)
                    wslot[sq].freed = ("pe", w.c["pe"])
                for hp in range(4):
                    wslot[0].pre_write(w, "gpsimd")
                    tok_q = w.dma("wdm", gp and gp.dma_start(
                        out=wsl[0].rearrange("p (b n) -> p b n", b=NDB),
                        in_=wq.rearrange("(b p) n -> p b n", p=128)[:, :, hp * 256:(hp + 1) * 256]))
                    wslot[1].pre_write(w, "gpsimd")
                    tok_k = w.dma("wdm", gp and gp.dma_start(
                        out=wsl[1].rearrange("p (b n) -> p b n", b=NDB),
                        in_=wk.rearrange("(b p) n -> p b n", p=128)[:, :, hp * 256:(hp + 1) * 256]))
                    emit_convert(1)
                    w.wait_tok("tensor", tok_q)
                    for h2 in range(2):
                        for nt in range(2):
                            sl = PF[(h2 * 2 + nt) % 2]
                            sl.pre_write(w, "tensor")
                            for db in range(NDB):
                                c0 = xcol(db, HALO + cbase + nt * 512)
                                ins = te and te.matmul(
                                    sl.ap,
                                    wsl[0][:, db * 256 + h2 * 128:db * 256 + h2 * 128 + 128],
                                    xnt16[:, c0:c0 + 512],
                                    start=(db == 0), stop=(db == 7))
                                if db == 7:
                                    sl.set_ready(w, "pe", ins)
                            sl.pre_read(w, "scalar")
                            o0 = h2 * 1024 + nt * 512
                            ins = se and se.activation(out=q16[:, o0:o0 + 512], in_=sl.ap,
                                                       func=AF.Identity,
                                                       bias=bq[:, hp * 2 + h2:hp * 2 + h2 + 1])
                            sl.set_freed(w, "act", ins)
                    w.wait_tok("tensor", tok_k)
                    for h2 in range(2):
                        for nt in range(3):
                            sl = PF[(h2 * 3 + nt) % 2]
                            sl.pre_write(w, "tensor")
                            for db in range(NDB):
                                c0 = xcol(db, gbase + nt * 512)
                                ins = te and te.matmul(
                                    sl.ap,
                                    wsl[1][:, db * 256 + h2 * 128:db * 256 + h2 * 128 + 128],
                                    xnt16[:, c0:c0 + 512],
                                    start=(db == 0), stop=(db == 7))
                                if db == 7:
                                    sl.set_ready(w, "pe", ins)
                            sl.pre_read(w, "scalar")
                            o0 = h2 * 1536 + nt * 512
                            ins = se and se.activation(out=kt16[:, o0:o0 + 512], in_=sl.ap,
                                                       func=AF.Identity,
                                                       bias=bk[:, hp * 2 + h2:hp * 2 + h2 + 1])
                            sl.set_freed(w, "act", ins)
                    wslot[0].freed = ("pe", w.c["pe"])
                    wslot[1].freed = ("pe", w.c["pe"])
                    qk_act = w.c["act"]
                    for cl in range(4):
                        ch = g * 4 + cl
                        msk = maske0 if ch == 0 else (maske7 if ch == 7 else maskb)
                        for h2 in range(2):
                            h = hp * 2 + h2
                            it = att_it[0]
                            att_it[0] += 1
                            osl = PF[6] if it % 2 == 0 else PF[0]
                            cur = dict(h=h, cl=cl, osl=osl, d_px=[0, 0])
                            osl.pre_write(w, "tensor")
                            scp = [(PF[2], PF[4]), (PF[3], PF[5])]
                            for qh in range(2):
                                scA, scB = scp[qh]
                                scA.pre_write(w, "tensor")
                                scB.pre_write(w, "tensor")
                                w.wait("tensor", "act", qk_act)
                                q0 = h2 * 1024 + cl * 256 + qh * 128
                                qap = q16[:, q0:q0 + 128]
                                for kc in range(3):
                                    dst = scA.ap[:, kc * 256:kc * 256 + 256] if kc < 2 \
                                        else scB.ap[:, 0:256]
                                    k0 = h2 * 1536 + cl * 256 + kc * 256
                                    ins = te and te.matmul(dst, qap, kt16[:, k0:k0 + 256],
                                                           start=True, stop=True)
                                    if kc == 1:
                                        scA.set_ready(w, "pe", ins)
                                    elif kc == 2:
                                        scB.set_ready(w, "pe", ins)
                            if att_prev[0] is not None:
                                emit_tr_o(att_prev[0])
                            for qh in range(2):
                                scA, scB = scp[qh]
                                c_nm = 8 + 3 * qh
                                scA.pre_read(w, "vector")
                                scB.pre_read(w, "vector")
                                ia = ve and ve.scalar_tensor_tensor(
                                    out=pn[qh][:, 0:512], in0=scA.ap, scalar=SCALE,
                                    in1=msk[:, qh * 768:qh * 768 + 512],
                                    op0=ALU.mult, op1=ALU.add)
                                scA.set_freed(w, "dve", ia)
                                ib = ve and ve.scalar_tensor_tensor(
                                    out=pn[qh][:, 512:768], in0=scB.ap[:, 0:256], scalar=SCALE,
                                    in1=msk[:, qh * 768 + 512:qh * 768 + 768],
                                    op0=ALU.mult, op1=ALU.add)
                                scB.set_freed(w, "dve", ib)
                                d_nm = w.inc("dve", ve and ve.reduce_max(
                                    out=small[:, c_nm:c_nm + 1], in_=pn[qh], axis=AX, negate=True))
                                w.wait("scalar", "dve", d_nm)
                                w.inc("act", se and se.activation(
                                    out=pn[qh], in_=pn[qh], func=AF.Exp,
                                    bias=small[:, c_nm:c_nm + 1],
                                    accum_out=small[:, c_nm + 1:c_nm + 2]))
                            for qh in range(2):
                                c_nm = 8 + 3 * qh
                                w.wait("vector", "act")
                                w.inc("dve", ve and ve.reciprocal(
                                    out=small[:, c_nm + 2:c_nm + 3], in_=small[:, c_nm + 1:c_nm + 2]))
                                w.wait("vector", "pe", pexp_free[qh])
                                cur["d_px"][qh] = w.inc("dve", ve and ve.tensor_scalar_mul(
                                    pexp[qh], pn[qh], small[:, c_nm + 2:c_nm + 3]))
                            att_prev[0] = cur
                if att_prev[0] is not None:
                    emit_tr_o(att_prev[0])
                    att_prev[0] = None
                w.wait("tensor", "act")
                for dob in range(NDB):
                    sw = dob % 2
                    wslot[sw].pre_write(w, "gpsimd")
                    tok_o = w.dma("wdm", gp and gp.dma_start(
                        out=wsl[sw][:, 0:1024].rearrange("p (b n) -> p b n", b=NDB),
                        in_=wo.rearrange("(b p) n -> p b n", p=128)[:, :, dob * 128:(dob + 1) * 128]))
                    w.wait_tok("tensor", tok_o)
                    for tb2 in range(2):
                        sl = PF[2 + tb2]
                        sl.pre_write(w, "tensor")
                        for dib in range(NDB):
                            ins = te and te.matmul(
                                sl.ap,
                                wsl[sw][:, dib * 128:(dib + 1) * 128],
                                otg[:, dib * 1024 + tb2 * 512:dib * 1024 + tb2 * 512 + 512],
                                start=(dib == 0), stop=(dib == 7))
                            if dib == 7:
                                sl.set_ready(w, "pe", ins)
                        sl.pre_read(w, "vector")
                        dst = bufr[:, rcol(dob, cbase + tb2 * 512):rcol(dob, cbase + tb2 * 512) + 512]
                        ins = ve and ve.scalar_tensor_tensor(out=dst, in0=sl.ap,
                                                             scalar=bo[:, dob:dob + 1],
                                                             in1=dst, op0=ALU.add, op1=ALU.add)
                        sl.set_freed(w, "dve", ins)
                    wslot[sw].freed = ("pe", w.c["pe"])

            if "aT" in dbg:
                w.wait("sync", "dve")
                w.dma("out", sy and sy.dma_start(out=dbg["aT"], in_=bufr))
                w.wait_all("vector", "out")

            aT_dve = w.c["dve"]
            w.wait("scalar", "dve", aT_dve)
            w.wait("tensor", "dve", aT_dve)
            ltaslot = [Slot(), Slot()]
            for tb4 in range(4):
                PF[2].pre_write(w, "tensor")
                for db in range(NDB):
                    src = bufr[:, rcol(db, tb4 * 512):rcol(db, tb4 * 512) + 512]
                    i1 = te and te.matmul(pf[2][0:1, :], onesc32, src,
                                          start=(db == 0), stop=(db == 7))
                    p1 = w.inc("pe", i1)
                    if db == 7:
                        PF[2].ready = ("pe", p1)
                PF[3].pre_write(w, "tensor")
                for db in range(NDB):
                    sidx = db % 2
                    src = bufr[:, rcol(db, tb4 * 512):rcol(db, tb4 * 512) + 512]
                    ltaslot[sidx].pre_write(w, "scalar")
                    a_s = w.inc("act", se and se.activation(out=lta[sidx], in_=src, func=AF.Square))
                    w.wait("tensor", "act", a_s)
                    i2 = te and te.matmul(pf[3][0:1, :], onesc32, lta[sidx],
                                          start=(db == 0), stop=(db == 7))
                    p2 = w.inc("pe", i2)
                    ltaslot[sidx].freed = ("pe", p2)
                    if db == 7:
                        PF[3].ready = ("pe", p2)
                PF[2].pre_read(w, "vector")
                ins = ve and ve.tensor_copy(out=lsum[0:1, tb4 * 512:(tb4 + 1) * 512], in_=pf[2][0:1, :])
                PF[2].set_freed(w, "dve", ins)
                PF[3].pre_read(w, "vector")
                ins = ve and ve.tensor_copy(out=lsq[0:1, tb4 * 512:(tb4 + 1) * 512], in_=pf[3][0:1, :])
                PF[3].set_freed(w, "dve", ins)
            w.inc("dve", ve and ve.tensor_scalar_mul(lsum, lsum, 1.0 / D))
            w.inc("dve", ve and ve.tensor_mul(ltmp, lsum, lsum))
            w.inc("dve", ve and ve.tensor_scalar_mul(lsq, lsq, 1.0 / D))
            d_v = w.inc("dve", ve and ve.tensor_sub(lsq, lsq, ltmp))
            w.wait("scalar", "dve", d_v)
            w.inc("act", se and se.activation(out=ltmp, in_=lsq, func=AF.Ln, bias=small[0:1, 15:16]))
            a_r = w.inc("act", se and se.activation(out=lsq, in_=ltmp, func=AF.Exp, scale=-0.5))
            for tb4 in range(4):
                PF[0].pre_write(w, "tensor")
                w.wait("tensor", "act", a_r)
                i0 = te and te.matmul(pf[0], ones32, lsum[0:1, tb4 * 512:(tb4 + 1) * 512],
                                      start=True, stop=True)
                PF[0].set_ready(w, "pe", i0)
                PF[1].pre_write(w, "tensor")
                i1 = te and te.matmul(pf[1], ones32, lsq[0:1, tb4 * 512:(tb4 + 1) * 512],
                                      start=True, stop=True)
                PF[1].set_ready(w, "pe", i1)
                PF[0].pre_read(w, "vector")
                PF[1].pre_read(w, "vector")
                for db in range(NDB):
                    src = bufr[:, rcol(db, tb4 * 512):rcol(db, tb4 * 512) + 512]
                    ins = ve and ve.tensor_sub(src, src, pf[0])
                    if db == 7:
                        w.inc("dve", ins)
                for db in range(NDB):
                    src = bufr[:, rcol(db, tb4 * 512):rcol(db, tb4 * 512) + 512]
                    ins = ve and ve.tensor_mul(src, src, pf[1])
                    if db == 7:
                        d_g2 = w.inc("dve", ins)
                for db in range(NDB):
                    src = bufr[:, rcol(db, tb4 * 512):rcol(db, tb4 * 512) + 512]
                    ins = ve and ve.tensor_scalar(out=src, in0=src,
                                                  scalar1=gamma[:, db:db + 1],
                                                  scalar2=beta[:, db:db + 1],
                                                  op0=ALU.mult, op1=ALU.add)
                    if db == 7:
                        d_g3 = w.inc("dve", ins)
                w.wait("scalar", "dve", d_g3)
                for db in range(NDB):
                    ins = se and se.copy(
                        out=bt16[:, rcol(db, tb4 * 512):rcol(db, tb4 * 512) + 512],
                        in_=bufr[:, rcol(db, tb4 * 512):rcol(db, tb4 * 512) + 512])
                    if db == 7:
                        w.inc("act", ins)
                PF[0].freed = ("dve", d_g2)
                PF[1].freed = ("dve", d_g2)
            bT_act = w.c["act"]
            if "bT" in dbg:
                w.wait("sync", "dve")
                w.dma("out", sy and sy.dma_start(out=dbg["bT"], in_=bufr))
                w.wait_all("vector", "out")
            if "bt16" in dbg:
                w.wait("sync", "act", bT_act)
                w.dma("out", sy and sy.dma_start(out=dbg["bt16"], in_=bt16))

            w.wait("sync", "pe")
            w.wait("sync", "dve")
            emit_convert(100)
            for lane in range(2):
                if cv_state["sttok"][lane] is not None:
                    w.wait_tok("sync", cv_state["sttok"][lane])
                    w.wait_tok("gpsimd", cv_state["sttok"][lane])
                    w.wait_tok("scalar", cv_state["sttok"][lane])
                    w.wait_tok("vector", cv_state["sttok"][lane])
            w.wait("gpsimd", "pe")
            w.wait("gpsimd", "dve")
            w.wait("scalar", "pe")
            w.wait("scalar", "dve")
            w.wait("vector", "pe")
            w.wait("vector", "dve")
            w1slot = [Slot(), Slot()]
            w2slot = [Slot(), Slot(), Slot(), Slot()]
            h1slot = Slot()
            w1view = w1b.rearrange("(a b) -> a b", b=DFF).rearrange("(b p) n -> p b n", p=128)
            w2view = w2b.rearrange("(a b) -> a b", b=D).rearrange("(b p) n -> p b n", p=128)

            def w1_half(do4, hf):
                s = do4 % 2
                eng = sy if hf == 0 else se
                w1slot[s].pre_write(w, "sync" if hf == 0 else "scalar")
                c0 = do4 * 512 + hf * 256
                return w.dma("xin" if hf == 0 else "wsc", eng and eng.dma_start(
                    out=w1sl[s].rearrange("p (b n) -> p b n", b=NDB)[:, :, hf * 256:hf * 256 + 256],
                    in_=w1view[:, :, c0:c0 + 256]))

            def w2_quarter(do2, qq, w2tok):
                j = do2 * 4 + qq
                eng = gp if qq % 2 == 0 else sy
                w2slot[qq].pre_write(w, "gpsimd" if qq % 2 == 0 else "sync")
                w2tok[j] = w.dma("wdm" if qq % 2 == 0 else "wve", eng and eng.dma_start(
                    out=w2qs[qq].rearrange("p (b n) -> p b n", b=8),
                    in_=w2view[:, qq * 8:(qq + 1) * 8, do2 * 128:(do2 + 1) * 128]))

            for tt in range(4):
                tokA = [None] * 8
                tokB = [None] * 8
                w2tok = {}
                tokA[0] = w1_half(0, 0)
                tokB[0] = w1_half(0, 1)
                tokA[1] = w1_half(1, 0)
                tokB[1] = w1_half(1, 1)
                for qq in range(4):
                    w2_quarter(0, qq, w2tok)
                h1slot.pre_write(w, "scalar")
                for do4 in range(8):
                    s = do4 % 2
                    if 1 <= do4 < 7:
                        tokA[do4 + 1] = w1_half(do4 + 1, 0)
                        tokB[do4 + 1] = w1_half(do4 + 1, 1)
                    w.wait("tensor", "act", bT_act)
                    for k in range(4):
                        if k == 0:
                            w.wait_tok("tensor", tokA[do4])
                        if k == 2:
                            w.wait_tok("tensor", tokB[do4])
                        do = do4 * 4 + k
                        sl = PF[do % 4]
                        sl.pre_write(w, "tensor")
                        for db in range(NDB):
                            ins = te and te.matmul(
                                sl.ap, w1sl[s][:, db * 512 + k * 128:db * 512 + k * 128 + 128],
                                bt16[:, rcol(db, tt * 512):rcol(db, tt * 512) + 512],
                                start=(db == 0), stop=(db == 7))
                            if db == 7:
                                sl.set_ready(w, "pe", ins)
                        sl.pre_read(w, "scalar")
                        ins = se and se.activation(out=h1[:, do * 512:(do + 1) * 512], in_=sl.ap,
                                                   func=AF.Gelu, bias=b1[:, do:do + 1])
                        sl.set_freed(w, "act", ins)
                    w1slot[s].freed = ("pe", w.c["pe"])
                h1_act = w.c["act"]
                for do2 in range(NDB):
                    sl = PF[4 + do2 % 2]
                    sl.pre_write(w, "tensor")
                    w.wait("tensor", "act", h1_act)
                    for qq in range(4):
                        j = do2 * 4 + qq
                        w.wait_tok("tensor", w2tok[j])
                        for k in range(8):
                            dib = qq * 8 + k
                            ins = te and te.matmul(sl.ap, w2qs[qq][:, k * 128:(k + 1) * 128],
                                                   h1[:, dib * 512:(dib + 1) * 512],
                                                   start=(dib == 0), stop=(dib == 31))
                            if k == 7:
                                if qq == 3:
                                    sl.set_ready(w, "pe", ins)
                                else:
                                    w.inc("pe", ins)
                                w2slot[qq].freed = ("pe", w.c["pe"])
                    if do2 < 7:
                        for qq in range(4):
                            w2_quarter(do2 + 1, qq, w2tok)
                    sl.pre_read(w, "vector")
                    dst = bufr[:, rcol(do2, tt * 512):rcol(do2, tt * 512) + 512]
                    ins = ve and ve.scalar_tensor_tensor(out=dst, in0=sl.ap,
                                                         scalar=b2[:, do2:do2 + 1],
                                                         in1=dst, op0=ALU.add, op1=ALU.add)
                    sl.set_freed(w, "dve", ins)
                h1slot.freed = ("pe", w.c["pe"])
            if "ffn" in dbg:
                w.wait("sync", "dve")
                w.dma("out", sy and sy.dma_start(out=dbg["ffn"], in_=bufr))
                w.wait_all("vector", "out")
            if "bt16b" in dbg:
                w.wait("sync", "dve")
                w.dma("out", sy and sy.dma_start(out=dbg["bt16b"], in_=bt16))
                w.wait_all("vector", "out")
                w.wait_all("scalar", "out")
                w.wait_all("gpsimd", "out")
            if "h1f" in dbg:
                w.wait("sync", "act")
                w.dma("out", sy and sy.dma_start(out=dbg["h1f"], in_=h1))
                w.wait_all("gpsimd", "out")
                w.wait_all("scalar", "out")

            w.wait("gpsimd", "pe")
            w.wait("gpsimd", "dve")
            if cv_state["sttok"][0] is not None:
                w.wait_tok("gpsimd", cv_state["sttok"][0])
            if cv_state["sttok"][1] is not None:
                w.wait_tok("gpsimd", cv_state["sttok"][1])
            upslot = [Slot(), Slot()]
            gpslot = [Slot(), Slot()]
            dwslot = [Slot(), Slot()]
            tuslot = [Slot(), Slot()]
            tdslot = [Slot(), Slot()]
            h16slot = Slot()
            combt_slot = Slot()
            for th in range(2):
                tbase = th * 1024
                combt_slot.pre_write(w, "vector")
                for tb8 in range(8):
                    tbg = th * 8 + tb8
                    sl = PF[6]
                    sl.pre_write(w, "tensor")
                    w.wait("tensor", "dve")
                    ins = te and te.transpose(pf[6][0:8, 0:128], comb[:, tbg * 8:(tbg + 1) * 8], id32)
                    sl.set_ready(w, "pe", ins)
                    sl.pre_read(w, "vector")
                    ins = ve and ve.tensor_copy(out=combt[0:8, tb8 * 128:(tb8 + 1) * 128],
                                                in_=pf[6][0:8, 0:128])
                    sl.set_freed(w, "dve", ins)
                combt_dve = w.c["dve"]
                for e in range(E):
                    for tb2 in range(2):
                        sl = PF[6]
                        sl.pre_write(w, "tensor")
                        w.wait("tensor", "dve", combt_dve)
                        ins = te and te.matmul(pf[6], sel[:, e * 128:(e + 1) * 128],
                                               combt[0:8, tb2 * 512:(tb2 + 1) * 512],
                                               start=True, stop=True)
                        sl.set_ready(w, "pe", ins)
                        sl.pre_read(w, "vector")
                        ins = ve and ve.tensor_copy(out=cbuf[:, tb2 * 512:(tb2 + 1) * 512], in_=pf[6])
                        sl.set_freed(w, "dve", ins)
                    cb_dve = w.c["dve"]
                    h16slot.pre_write(w, "scalar")
                    for fb2 in range(NFB // 2):
                        s = fb2 % 2
                        upslot[s].pre_write(w, "gpsimd")
                        tok_u = w.dma("wdm", gp and gp.dma_start(
                            out=upw[s].rearrange("p (b n) -> p b n", b=NDB),
                            in_=upt[e].rearrange("(b p) f -> p b f", p=128)[:, :, fb2 * 256:(fb2 + 1) * 256]))
                        gpslot[s].pre_write(w, "gpsimd")
                        tok_f = w.dma("wdm", gp and gp.dma_start(
                            out=gpw[s].rearrange("p (b n) -> p b n", b=NDB),
                            in_=gpt[e].rearrange("(b p) f -> p b f", p=128)[:, :, fb2 * 256:(fb2 + 1) * 256]))
                        w.wait_tok("tensor", tok_u)
                        w.wait_tok("tensor", tok_f)
                        for k in range(2):
                            fb = fb2 * 2 + k
                            for tb2 in range(2):
                                r3 = (fb * 2 + tb2) % 3
                                slu = PF[(0, 1, 4)[r3]]
                                slg = PF[(2, 3, 5)[r3]]
                                slu.pre_write(w, "tensor")
                                slg.pre_write(w, "tensor")
                                for db in range(NDB):
                                    c0 = xcol(db, HALO + tbase + tb2 * 512)
                                    ins = te and te.matmul(
                                        slu.ap, upw[s][:, db * 256 + k * 128:db * 256 + k * 128 + 128],
                                        xnt16[:, c0:c0 + 512],
                                        start=(db == 0), stop=(db == 7))
                                    if db == 7:
                                        slu.set_ready(w, "pe", ins)
                                for db in range(NDB):
                                    c0 = xcol(db, HALO + tbase + tb2 * 512)
                                    ins = te and te.matmul(
                                        slg.ap, gpw[s][:, db * 256 + k * 128:db * 256 + k * 128 + 128],
                                        xnt16[:, c0:c0 + 512],
                                        start=(db == 0), stop=(db == 7))
                                    if db == 7:
                                        slg.set_ready(w, "pe", ins)
                                slu.pre_read(w, "scalar")
                                tuslot[tb2].pre_write(w, "scalar")
                                ins = se and se.activation(out=tu[tb2], in_=slu.ap, func=AF.Identity,
                                                           bias=upb[:, e * NFB + fb:e * NFB + fb + 1])
                                slu.set_freed(w, "act", ins)
                                a_u = w.c["act"]
                                slg.pre_read(w, "vector")
                                w.wait("vector", "act", a_u)
                                d_m = w.inc("dve", ve and ve.scalar_tensor_tensor(
                                    out=tu[tb2], in0=slg.ap,
                                    scalar=gpb[:, e * NFB + fb:e * NFB + fb + 1],
                                    in1=tu[tb2], op0=ALU.add, op1=ALU.mult))
                                slg.freed = ("dve", d_m)
                                w.wait("scalar", "dve", d_m)
                                o0 = fb * 1024 + tb2 * 512
                                a_h = w.inc("act", se and se.activation(out=h16[:, o0:o0 + 512],
                                                                        in_=tu[tb2], func=AF.Silu))
                                tuslot[tb2].freed = ("act", a_h)
                        upslot[s].freed = ("pe", w.c["pe"])
                        gpslot[s].freed = ("pe", w.c["pe"])
                    h_act = w.c["act"]
                    w.wait("vector", "dve", cb_dve)
                    for do in range(NDB):
                        s = do % 2
                        dwslot[s].pre_write(w, "gpsimd")
                        tok_d = w.dma("wdm", gp and gp.dma_start(
                            out=dwts[s].rearrange("p (b n) -> p b n", b=NFB),
                            in_=dwt[e].rearrange("(b p) n -> p b n", p=128)[:, :, do * 128:(do + 1) * 128]))
                        w.wait_tok("tensor", tok_d)
                        w.wait("tensor", "act", h_act)
                        for tb2 in range(2):
                            sl = PF[4 + tb2]
                            sl.pre_write(w, "tensor")
                            for fb in range(NFB):
                                o0 = fb * 1024 + tb2 * 512
                                ins = te and te.matmul(sl.ap, dwts[s][:, fb * 128:(fb + 1) * 128],
                                                       h16[:, o0:o0 + 512],
                                                       start=(fb == 0), stop=(fb == NFB - 1))
                                if fb == NFB - 1:
                                    sl.set_ready(w, "pe", ins)
                            sl.pre_read(w, "vector")
                            tdslot[tb2].pre_write(w, "vector")
                            ins = ve and ve.scalar_tensor_tensor(
                                out=td[tb2], in0=sl.ap,
                                scalar=dnb[:, e * 8 + do:e * 8 + do + 1],
                                in1=cbuf[:, tb2 * 512:(tb2 + 1) * 512],
                                op0=ALU.add, op1=ALU.mult)
                            sl.set_freed(w, "dve", ins)
                            dst = bufr[:, rcol(do, tbase + tb2 * 512):rcol(do, tbase + tb2 * 512) + 512]
                            d_add = w.inc("dve", ve and ve.tensor_add(dst, dst, td[tb2]))
                            tdslot[tb2].freed = ("dve", d_add)
                        dwslot[s].freed = ("pe", w.c["pe"])
                    h16slot.freed = ("pe", w.c["pe"])
                combt_slot.freed = ("pe", w.c["pe"])
            if "moe" in dbg:
                w.wait("sync", "dve")
                w.dma("out", sy and sy.dma_start(out=dbg["moe"], in_=bufr))
            if "h16" in dbg:
                w.wait("sync", "act")
                w.dma("out", sy and sy.dma_start(out=dbg["h16"], in_=h16))

            res_dve = w.c["dve"]
            w.wait("sync", "dve", res_dve)
            fslot = [Slot(), Slot()]
            oslot = [Slot(), Slot()]
            for i in range(16):
                s = i % 2
                fslot[s].pre_write(w, "sync")
                tok_xf = w.dma("xin", sy and sy.dma_start(out=xt[s], in_=xc[(2 + i) * 128:(3 + i) * 128, :]))
                for half in range(2):
                    sl = PF[half]
                    sl.pre_write(w, "tensor")
                    w.wait("tensor", "dve", res_dve)
                    for k in range(4):
                        db = half * 4 + k
                        c0 = rcol(db, i * 128)
                        ins = te and te.transpose(pf[half][:, k * 128:(k + 1) * 128],
                                                  bufr[:, c0:c0 + 128], id32)
                        if k == 3:
                            sl.set_ready(w, "pe", ins)
                oslot[s].pre_write(w, "vector")
                w.wait_tok("vector", tok_xf)
                for half in range(2):
                    sl = PF[half]
                    sl.pre_read(w, "vector")
                    ins = ve and ve.tensor_add(fot[s][:, half * 512:(half + 1) * 512],
                                               pf[half][:, 0:512],
                                               xt[s][:, half * 512:(half + 1) * 512])
                    sl.set_freed(w, "dve", ins)
                d_out = w.c["dve"]
                fslot[s].freed = ("dve", d_out)
                w.wait("sync", "dve", d_out)
                ins = sy and sy.dma_start(out=y[i * 128:(i + 1) * 128, :], in_=fot[s])
                oslot[s].freed = w.dma("out", ins)
            w.wait_all("sync", "out")

        engs = {}

        @block.tensor
        def _(eng):
            engs["tensor"] = eng
            flow(Wk("tensor", engs, semh))

        @block.vector
        def _(eng):
            engs["vector"] = eng
            flow(Wk("vector", engs, semh))

        @block.scalar
        def _(eng):
            engs["scalar"] = eng
            flow(Wk("scalar", engs, semh))

        @block.gpsimd
        def _(eng):
            engs["gpsimd"] = eng
            flow(Wk("gpsimd", engs, semh))

        @block.sync
        def _(eng):
            engs["sync"] = eng
            flow(Wk("sync", engs, semh))

    return nc


def _prep_inputs(L0_attn_x, L0_attn_Wq, L0_attn_bq, L0_attn_Wk, L0_attn_bk,
                 L0_attn_Wv, L0_attn_bv, L0_attn_Wo, L0_attn_bo,
                 L0_attn_gamma, L0_attn_beta, L0_attn_W1, L0_attn_b1,
                 L0_attn_W2, L0_attn_b2, L0_ffn_gate_w, L0_ffn_gate_b,
                 L0_ffn_up_w, L0_ffn_up_b, L0_ffn_gp_w, L0_ffn_gp_b,
                 L0_ffn_down_w, L0_ffn_down_b):
    f = lambda a: np.ascontiguousarray(np.asarray(a, dtype=np.float32))
    x = f(L0_attn_x).reshape(16384, D)
    xpad = np.zeros((16384 + 2 * HALO, D), np.float32)
    xpad[HALO:HALO + 16384] = x

    colb = lambda v, nb: np.ascontiguousarray(f(v).reshape(nb, 128).T)
    wq_, wk_, wv_, wo_ = f(L0_attn_Wq), f(L0_attn_Wk), f(L0_attn_Wv), f(L0_attn_Wo)
    w1_, w2_ = f(L0_attn_W1), f(L0_attn_W2)
    upt = np.zeros((E, D, FPAD), np.float32)
    upt[:, :, :F] = np.transpose(f(L0_ffn_up_w), (0, 2, 1))
    gpt = np.zeros((E, D, FPAD), np.float32)
    gpt[:, :, :F] = np.transpose(f(L0_ffn_gp_w), (0, 2, 1))
    dwt = np.zeros((E, FPAD, D), np.float32)
    dwt[:, :F, :] = np.transpose(f(L0_ffn_down_w), (0, 2, 1))
    upb = np.zeros((E, FPAD), np.float32); upb[:, :F] = f(L0_ffn_up_b)
    gpb = np.zeros((E, FPAD), np.float32); gpb[:, :F] = f(L0_ffn_gp_b)
    upb_re = np.ascontiguousarray(np.transpose(upb.reshape(E, NFB, 128), (2, 0, 1)).reshape(128, E * NFB))
    gpb_re = np.ascontiguousarray(np.transpose(gpb.reshape(E, NFB, 128), (2, 0, 1)).reshape(128, E * NFB))
    dnb_re = np.ascontiguousarray(np.transpose(f(L0_ffn_down_b).reshape(E, 8, 128), (2, 0, 1)).reshape(128, E * 8))
    gw = f(L0_ffn_gate_w)
    gwt_re = np.ascontiguousarray(np.transpose(gw.T.reshape(8, 128, E), (1, 0, 2)).reshape(128, 8 * E))
    gbb = np.ascontiguousarray(np.tile(f(L0_ffn_gate_b)[None, :], (128, 1)))
    bvbc = np.ascontiguousarray(np.tile(f(L0_attn_bv)[None, :], (128, 1)))
    selm = np.zeros((8, 1024), np.float32)
    for e in range(8):
        selm[e, e * 128:(e + 1) * 128] = 1.0

    qi = np.arange(256)[:, None]
    kj = np.arange(768)[None, :]
    band = (kj >= qi) & (kj <= qi + 512)
    mb = np.where(band, 0.0, NEG).astype(np.float32)
    m_int = np.concatenate([mb[0:128], mb[128:256]], axis=1)
    kill_first = np.where(np.arange(768) < 256, NEG, 0.0).astype(np.float32)
    kill_last = np.where(np.arange(768) >= 512, NEG, 0.0).astype(np.float32)

    per_core = []
    for c in range(8):
        m_e0 = m_int.copy()
        m_e7 = m_int.copy()
        if c == 0:
            for qh in range(2):
                m_e0[:, qh * 768:(qh + 1) * 768] = np.minimum(
                    m_e0[:, qh * 768:(qh + 1) * 768], kill_first[None, :])
        if c == 7:
            for qh in range(2):
                m_e7[:, qh * 768:(qh + 1) * 768] = np.minimum(
                    m_e7[:, qh * 768:(qh + 1) * 768], kill_last[None, :])
        per_core.append(dict(
            xc=np.ascontiguousarray(xpad[c * T:c * T + TH]),
            wq=wq_, wk=wk_, wv=wv_, wo=wo_, w1=w1_, w2=w2_,
            upt=upt, gpt=gpt, dwt=dwt,
            bq=colb(L0_attn_bq, 8), bk=colb(L0_attn_bk, 8),
            bo=colb(L0_attn_bo, 8), b2=colb(L0_attn_b2, 8),
            gamma=colb(L0_attn_gamma, 8), beta=colb(L0_attn_beta, 8),
            b1=colb(L0_attn_b1, 32),
            gwt=gwt_re, gbb=gbb, upb=upb_re, gpb=gpb_re, dnb=dnb_re,
            bvbc=bvbc, sel=selm,
            maskb=m_int.astype(BFNP), maske0=m_e0.astype(BFNP),
            maske7=m_e7.astype(BFNP),
        ))
    return per_core


def kernel(**inputs):
    in_maps = _prep_inputs(**inputs)
    nc = build_program()
    res = run_bass_kernel_spmd(nc, in_maps, core_ids=list(range(8)))
    out = np.concatenate([np.asarray(r["y"]) for r in res.results], axis=0)
    return out.reshape(1, 16384, D).astype(np.float32)
